# version 15
# baseline (speedup 1.0000x reference)
import math
import numpy as np
import ml_dtypes
import concourse.bass as bass
import concourse.mybir as mybir
from concourse.bass_utils import run_bass_kernel_spmd

F32 = mybir.dt.float32
BF16 = mybir.dt.bfloat16
I32 = mybir.dt.int32
AF = mybir.ActivationFunctionType
ALU = mybir.AluOpType
AX = mybir.AxisListType

B_, S, E, H, D = 4, 4096, 2048, 16, 128
INNER = 8192
OWN = 2048
NT = 16
KTN = 32
NQG = 8
QG = 256
CK = 1024
NCK = OWN // CK
ET = E // 128
OTN = INNER // 128
SCALE = 1.0 / math.sqrt(D)
EXPC = 18.0
BF = ml_dtypes.bfloat16
ROLES = ("sync", "tensor", "vector", "scalar", "gpsimd")
DEBUG_DUMP = False


def _split(v):
    hi = np.float32(v)
    return float(hi), float(np.float64(v) - np.float64(hi))


LNC_HI, LNC_LO = _split(math.log(10000.0))
SC_HI, SC_LO = _split(1.0 / float(np.float32(np.sqrt(np.float32(128.0)))))
SQA_HI, SQA_LO = _split(math.sqrt(2048.0))
SQB_HI, SQB_LO = _split(math.sqrt(8192.0))
I127_HI, I127_LO = _split(1.0 / 127.0)
TPI_HI, TPI_LO = _split(2.0 * math.pi)
PIH_HI, PIH_LO = _split(0.5 * math.pi)
INV2PI = float(np.float32(1.0 / (2.0 * math.pi)))
PI_CLAMP = 3.1415925


_TABLES = None


def _rope_tables_f32():
    global _TABLES
    if _TABLES is not None:
        return _TABLES
    try:
        import jax
        import jax.numpy as jnp
        with jax.default_device(jax.devices("cpu")[0]):
            theta = jnp.repeat(-jnp.arange(0, D, 2, dtype=jnp.float32) / D, 2)
            theta = jnp.power(10000.0, theta)
            pos = jnp.arange(S, dtype=jnp.float32)[:, None]
            ang = pos * theta[None, :]
            cos = np.asarray(jax.device_get(jnp.cos(ang)), dtype=np.float32)
            sin = np.asarray(jax.device_get(jnp.sin(ang)), dtype=np.float32)
    except Exception:
        d = np.float32(128.0)
        theta = np.repeat(-(np.arange(0, 128, dtype=np.float32))[::2] / d, 2).astype(np.float32)
        theta = np.power(np.float32(10000.0), theta).astype(np.float32)
        ang = (np.arange(S, dtype=np.float32)[:, None] * theta[None, :]).astype(np.float32)
        cos = np.cos(ang).astype(np.float32)
        sin = np.sin(ang).astype(np.float32)
    sgn = np.where(np.arange(128) % 2 == 0, np.float32(-1.0), np.float32(1.0))
    _TABLES = (cos, (sin * sgn[None, :]).astype(np.float32))
    return _TABLES


class Op:
    __slots__ = ("eng", "fn", "deps", "inc", "cum", "needed", "grp", "gcum", "small")

    def __init__(self, eng, fn, deps, inc):
        self.eng, self.fn, self.deps, self.inc = eng, fn, deps, inc
        self.cum, self.needed = 0, False
        self.grp, self.gcum = None, 0
        self.small = False


class Grp:
    __slots__ = ("name", "count", "sem")

    def __init__(self, name):
        self.name, self.count, self.sem = name, 0, None


class Res:
    __slots__ = ("w", "readers")

    def __init__(self):
        self.w, self.readers = None, []


class Prog:
    def __init__(self):
        self.ops = {e: [] for e in ROLES}
        self.all = []

    def add(self, eng, fn, reads=(), writes=(), deps=(), dma=False, g=None, small=False):
        dd = []
        seen = set()
        def push(d):
            if d is not None and id(d) not in seen:
                seen.add(id(d))
                dd.append(d)
        for d in deps:
            push(d)
        for r in reads:
            push(r.w)
        for r in writes:
            for x in r.readers:
                push(x)
            push(r.w)
        op = Op(eng, fn, dd, 16 if dma else 1)
        op.small = small
        if dma:
            assert g is not None, "dma op needs a group"
            g.count += 16
            op.grp, op.gcum = g, g.count
        for r in reads:
            r.readers.append(op)
        for r in writes:
            r.w, r.readers = op, []
        self.ops[eng].append(op)
        self.all.append(op)
        return op

    def finalize(self):
        for op in self.all:
            if op.grp is not None:
                op.needed = True
            for d in op.deps:
                if d.eng != op.eng or d.grp is not None:
                    d.needed = True
        for e in ROLES:
            c = 0
            for op in self.ops[e]:
                if op.needed and op.grp is None:
                    c += op.inc
                    op.cum = c

    def emit(self, role, eng, sems):
        wm = {}
        for op in self.ops[role]:
            for d in op.deps:
                if not d.needed:
                    continue
                if d.grp is not None:
                    key, val, sem = d.grp.name, d.gcum, d.grp.sem
                elif d.eng != role:
                    key, val, sem = d.eng, d.cum, sems[d.eng]
                else:
                    continue
                if val > wm.get(key, 0):
                    eng.wait_ge(sem, val)
                    wm[key] = val
            r = op.fn(eng)
            if op.small and op.grp is None:
                r = eng.drain()
            if op.needed:
                tgt = op.grp.sem if op.grp is not None else sems[op.eng]
                (r if r is not None else eng.nop()).then_inc(tgt, op.inc)


def build():
    nc = bass.Bass("TRN2", target_bir_lowering=False, detect_race_conditions=False)

    xk = nc.dram_tensor("xk", [S, E], F32, kind="ExternalInput").ap()
    xo = nc.dram_tensor("xo", [OWN, E], F32, kind="ExternalInput").ap()
    Wq = nc.dram_tensor("Wq", [E, E], F32, kind="ExternalInput").ap()
    Wk = nc.dram_tensor("Wk", [E, E], F32, kind="ExternalInput").ap()
    Wv = nc.dram_tensor("Wv", [E, E], F32, kind="ExternalInput").ap()
    Wo = nc.dram_tensor("Wo", [E, E], F32, kind="ExternalInput").ap()
    bq = nc.dram_tensor("bq", [E], F32, kind="ExternalInput").ap()
    bk = nc.dram_tensor("bk", [E], F32, kind="ExternalInput").ap()
    bv = nc.dram_tensor("bv", [E], F32, kind="ExternalInput").ap()
    bo = nc.dram_tensor("bo", [E], F32, kind="ExternalInput").ap()
    W1 = nc.dram_tensor("W1", [INNER, E], F32, kind="ExternalInput").ap()
    W2 = nc.dram_tensor("W2", [E, INNER], F32, kind="ExternalInput").ap()
    maskin = nc.dram_tensor("maskin", [KTN * 128, QG], BF16, kind="ExternalInput").ap()
    poso = nc.dram_tensor("poso", [128, OWN], F32, kind="ExternalInput").ap()
    out = nc.dram_tensor("out", [OWN, E], F32, kind="ExternalOutput").ap()

    def scr(name, shape, dt=BF16):
        return nc.dram_tensor(name, shape, dt, kind="Internal").ap()

    WbS = {"Wq": scr("WqbS", [E, E]), "Wk": scr("WkbS", [E, E]),
           "Wv": scr("WvbS", [E, E]), "Wo": scr("WobS", [E, E])}
    bqkS = scr("bqkS", [32, 128])
    W1qS = scr("W1qS", [INNER, E])
    W2qS = scr("W2qS", [E, INNER])
    xnS = scr("xnS", [S, E])
    xnoS = scr("xnoS", [OWN, E])
    KS = scr("KS", [E, S])
    VS = scr("VS", [S, E])
    QS = scr("QS", [E, OWN])
    OS = scr("OS", [E, OWN])
    QA1S = scr("QA1S", [CK, E])
    ghS = scr("ghS", [CK, INNER], F32)
    QA2S = scr("QA2S", [CK, INNER])

    cos32, sinF32 = _rope_tables_f32()
    C_cosk = nc.inline_tensor(np.ascontiguousarray(cos32.T), "coskC").ap()
    C_sink = nc.inline_tensor(np.ascontiguousarray(sinF32.T), "sinkC").ap()
    cosoI = nc.dram_tensor("cosoI", [128, OWN], F32, kind="ExternalInput").ap()
    sinoI = nc.dram_tensor("sinoI", [128, OWN], F32, kind="ExternalInput").ap()
    perm_np = np.zeros((128, 128), dtype=np.float32)
    for i in range(64):
        perm_np[2 * i + 1, 2 * i] = 1.0
        perm_np[2 * i, 2 * i + 1] = 1.0
    C_perm = nc.inline_tensor(perm_np.astype(BF), "permC").ap()
    C_o128b = nc.inline_tensor(np.ones((128, 1), dtype=BF), "o128bC").ap()
    C_o128f = nc.inline_tensor(np.ones((128, 1), dtype=np.float32), "o128fC").ap()
    C_or128b = nc.inline_tensor(np.ones((1, 128), dtype=BF), "or128bC").ap()
    C_or128f = nc.inline_tensor(np.ones((1, 128), dtype=np.float32), "or128fC").ap()

    A = nc.alloc_sbuf_tensor("Af", [128, 14336], F32)
    Bb = nc.alloc_sbuf_tensor("Bbf", [128, 58368], BF16)
    Ii = nc.alloc_sbuf_tensor("Iint", [128, 1024], I32)
    cosb = nc.alloc_sbuf_tensor("cosb", [128, S], BF16)
    sinb = nc.alloc_sbuf_tensor("sinb", [128, S], BF16)
    cosbo = nc.alloc_sbuf_tensor("cosbo", [128, OWN], BF16)
    sinbo = nc.alloc_sbuf_tensor("sinbo", [128, OWN], BF16)
    o128b = nc.alloc_sbuf_tensor("o128b", [128, 1], BF16)
    o128f = nc.alloc_sbuf_tensor("o128f", [128, 1], F32)
    or128b = nc.alloc_sbuf_tensor("or128b", [1, 128], BF16)
    or128f = nc.alloc_sbuf_tensor("or128f", [1, 128], F32)
    permM = nc.alloc_sbuf_tensor("permM", [128, 128], BF16)
    bqkf = nc.alloc_sbuf_tensor("bqkf", [128, 32], F32)
    bvrow = nc.alloc_sbuf_tensor("bvrow", [1, E], BF16)
    borow = nc.alloc_sbuf_tensor("borow", [1, E], BF16)
    swc = nc.alloc_sbuf_tensor("swc", [128, 4], F32)
    smallf = nc.alloc_sbuf_tensor("smallf", [1, 64], F32)
    colsf = nc.alloc_sbuf_tensor("colsf", [128, 80], F32)
    PS = [nc.alloc_psum_tensor("ps%d" % i, [128, 512], F32) for i in range(8)]

    P = Prog()
    GR = {}

    def G(name):
        if name not in GR:
            GR[name] = Grp(name)
        return GR[name]

    NGA = 14336 // 256
    NGB = 58368 // 512
    rga = [Res() for _ in range(NGA)]
    rgb = [Res() for _ in range(NGB)]
    rI = Res()
    rPS = [Res() for _ in range(8)]
    rtab = Res()
    rconst = Res()
    rbqk, rbv, rbo, rswc, rsmall = Res(), Res(), Res(), Res(), Res()
    rcols = [Res() for _ in range(80)]
    rscr = {k: Res() for k in ("WqbS", "WkbS", "WvbS", "WobS", "bqkS", "W1qS", "W2qS",
                               "xnoS", "QS", "OS", "QA1S", "ghS", "QA2S", "KS", "VS")}
    rxn = [Res() for _ in range(8)]

    def af(off, w):
        assert off + w <= 14336
        return A[:, off:off + w]

    def ab(off, w):
        assert off + w <= 58368
        return Bb[:, off:off + w]

    def rAs(off, w):
        return [rga[i] for i in range(off // 256, (off + w + 255) // 256)]

    def rBs(off, w):
        return [rgb[i] for i in range(off // 512, (off + w + 511) // 512)]

    OUT_OPS = []

    def emit_round2048(src, dst, tmp):
        for c0 in range(0, 2048, 1024):
            P.add("vector", lambda e, c0=c0: e.tensor_copy(Ii[:, 0:1024], af(src + c0, 1024)),
                  reads=rAs(src + c0, 1024), writes=[rI])
            P.add("vector", lambda e, c0=c0: e.tensor_copy(af(dst + c0, 1024), Ii[:, 0:1024]),
                  reads=[rI], writes=rAs(dst + c0, 1024))
        P.add("vector", lambda e: e.tensor_sub(af(tmp, 2048), af(dst, 2048), af(src, 2048)),
              reads=rAs(dst, 2048) + rAs(src, 2048), writes=rAs(tmp, 2048))
        P.add("vector", lambda e: e.scalar_tensor_tensor(af(dst, 2048), af(tmp, 2048), -0.5, af(dst, 2048),
                                                         ALU.is_lt, ALU.add),
              reads=rAs(tmp, 2048) + rAs(dst, 2048), writes=rAs(dst, 2048))
        P.add("vector", lambda e: e.tensor_scalar(af(tmp, 2048), af(tmp, 2048), 0.5, None, ALU.is_gt),
              reads=rAs(tmp, 2048), writes=rAs(tmp, 2048))
        P.add("vector", lambda e: e.tensor_sub(af(dst, 2048), af(dst, 2048), af(tmp, 2048)),
              reads=rAs(dst, 2048) + rAs(tmp, 2048), writes=rAs(dst, 2048))


    def emit_ln(xoff, boff, cbase, quant):
        cm = colsf[:, cbase + 0:cbase + 1]
        crs = colsf[:, cbase + 1:cbase + 2]
        cac = colsf[:, cbase + 2:cbase + 3]
        ctm = colsf[:, cbase + 3:cbase + 4]
        cc1 = colsf[:, cbase + 4:cbase + 5]
        cd1 = colsf[:, cbase + 5:cbase + 6]
        cmx = colsf[:, cbase + 6:cbase + 7]
        cmn = colsf[:, cbase + 7:cbase + 8]
        rc = rcols[cbase]
        x = af(xoff, 2048)
        rx = rAs(xoff, 2048)
        sq = af(xoff + 2048, 2048)
        rsq = rAs(xoff + 2048, 2048)
        P.add("scalar", lambda e: e.activation(sq, x, AF.Square, accum_out=cac), reads=rx, writes=rsq + [rc], small=True)
        P.add("vector", lambda e: e.tensor_reduce(cm, x, AX.X, ALU.add), reads=rx, writes=[rc], small=True)
        P.add("vector", lambda e: e.tensor_scalar(cm, cm, 1.0 / 2048.0, None, ALU.mult), reads=[rc], writes=[rc], small=True)
        P.add("vector", lambda e: e.tensor_scalar(cac, cac, 1.0 / 2048.0, None, ALU.mult), reads=[rc], writes=[rc], small=True)
        P.add("vector", lambda e: e.tensor_mul(ctm, cm, cm), reads=[rc], writes=[rc], small=True)
        P.add("vector", lambda e: e.tensor_sub(cac, cac, ctm), reads=[rc], writes=[rc], small=True)
        P.add("vector", lambda e: e.tensor_scalar(ctm, cac, 1.0, 1e-5, ALU.mult, ALU.add), reads=[rc], writes=[rc], small=True)
        P.add("scalar", lambda e: e.activation(ctm, ctm, AF.Sqrt), reads=[rc], writes=[rc], small=True)
        P.add("vector", lambda e: e.reciprocal(crs, ctm), reads=[rc], writes=[rc], small=True)
        if not quant:
            P.add("vector", lambda e: e.tensor_scalar(ab(boff, 2048), x, cm, crs, ALU.subtract, ALU.mult),
                  reads=rx + [rc], writes=rBs(boff, 2048), small=True)
            return
        P.add("vector", lambda e: e.tensor_mul(cc1, crs, crs), reads=[rc], writes=[rc], small=True)
        P.add("vector", lambda e: e.tensor_mul(cc1, cc1, cac), reads=[rc], writes=[rc], small=True)
        P.add("vector", lambda e: e.tensor_scalar(cc1, cc1, 2048.0, None, ALU.mult), reads=[rc], writes=[rc], small=True)
        P.add("scalar", lambda e: e.activation(cc1, cc1, AF.Sqrt), reads=[rc], writes=[rc], small=True)
        P.add("vector", lambda e: e.tensor_scalar(cc1, cc1, 1.0, 1e-12, ALU.mult, ALU.max), reads=[rc], writes=[rc], small=True)
        P.add("vector", lambda e: e.reciprocal(ctm, cc1), reads=[rc], writes=[rc], small=True)
        P.add("vector", lambda e: e.tensor_scalar(ctm, ctm, math.sqrt(2048.0), None, ALU.mult), reads=[rc], writes=[rc], small=True)
        P.add("vector", lambda e: e.tensor_reduce(cmx, x, AX.X, ALU.max), reads=rx, writes=[rc], small=True)
        P.add("vector", lambda e: e.tensor_reduce(cmn, x, AX.X, ALU.min), reads=rx, writes=[rc], small=True)
        P.add("vector", lambda e: e.tensor_sub(cmx, cmx, cm), reads=[rc], writes=[rc], small=True)
        P.add("vector", lambda e: e.tensor_sub(cmn, cm, cmn), reads=[rc], writes=[rc], small=True)
        P.add("vector", lambda e: e.tensor_max(cmx, cmx, cmn), reads=[rc], writes=[rc], small=True)
        P.add("vector", lambda e: e.tensor_mul(cmx, cmx, crs), reads=[rc], writes=[rc], small=True)
        P.add("vector", lambda e: e.tensor_mul(cmx, cmx, ctm), reads=[rc], writes=[rc], small=True)
        P.add("vector", lambda e: e.tensor_scalar(cmx, cmx, 1.0, 1e-5, ALU.mult, ALU.max), reads=[rc], writes=[rc], small=True)
        P.add("vector", lambda e: e.reciprocal(cmn, cmx), reads=[rc], writes=[rc], small=True)
        P.add("vector", lambda e: e.tensor_scalar(cmn, cmn, 127.0, None, ALU.mult), reads=[rc], writes=[rc], small=True)
        P.add("vector", lambda e: e.tensor_mul(cc1, ctm, cmn), reads=[rc], writes=[rc], small=True)
        P.add("vector", lambda e: e.tensor_mul(cc1, cc1, crs), reads=[rc], writes=[rc], small=True)
        P.add("vector", lambda e: e.tensor_scalar(cd1, cmx, 1.0 / 127.0, None, ALU.mult), reads=[rc], writes=[rc], small=True)
        P.add("vector", lambda e: e.tensor_mul(cd1, cd1, swc[:, 0:1]), reads=[rc, rswc], writes=[rc], small=True)
        t0 = sq
        P.add("vector", lambda e: e.tensor_scalar(t0, x, cm, cc1, ALU.subtract, ALU.mult),
              reads=rx + [rc], writes=rsq, small=True)
        QR, QT = 8192, 10240
        emit_round2048(xoff + 2048, QR, QT)
        P.add("vector", lambda e: e.tensor_copy(ab(boff, 2048), af(QR, 2048)),
              reads=rAs(QR, 2048), writes=rBs(boff, 2048))


    for nm, cst, dstt, rr in (("o128b", C_o128b, o128b, rconst), ("o128f", C_o128f, o128f, rconst),
                              ("or128b", C_or128b, or128b, rconst), ("or128f", C_or128f, or128f, rconst),
                              ("perm", C_perm, permM, rconst)):
        P.add("sync", lambda e, c=cst, d=dstt: e.dma_start(out=d[:, :], in_=c), writes=[rr], dma=True, g=G("once"))

    mats = [(Wq, "Wq"), (Wk, "Wk"), (Wv, "Wv"), (Wo, "Wo")]
    XT = [0, 4096]
    XTa = [4096, 8192]
    XNB = [54272, 56320]

    def a1a_tile(n):
        kc, j = n // 4, n % 4
        sl = n % 2
        P.add("sync", lambda e, kc=kc, j=j, sl=sl: e.dma_start(
            out=af(XTa[sl], 2048), in_=xk[512 * kc + 128 * j:512 * kc + 128 * j + 128, :]),
            writes=rAs(XTa[sl], 2048), dma=True, g=G("a2" if sl == 0 else "a3"))
        emit_ln(XTa[sl], XNB[sl], 8, False)
        P.add("gpsimd", lambda e, kc=kc, j=j, sl=sl: e.dma_start(
            out=xnS[512 * kc + 128 * j:512 * kc + 128 * j + 128, :], in_=ab(XNB[sl], 2048)),
            reads=rBs(XNB[sl], 2048), writes=[rxn[kc]], dma=True, g=G("xns%d" % kc))

    for i in range(64):
        m, nmw = mats[i // 16]
        et = i % 16
        fo = 2048 * (i % 2)
        bo_ = 2048 * (i % 2)
        P.add("sync", lambda e, m=m, et=et, fo=fo: e.dma_start(out=af(fo, 2048), in_=m[128 * et:128 * (et + 1), :]),
              writes=rAs(fo, 2048), dma=True, g=G("a%d" % (i % 2)))
        P.add("scalar", lambda e, fo=fo, bo_=bo_: e.activation(ab(bo_, 2048), af(fo, 2048), AF.Copy),
              reads=rAs(fo, 2048), writes=rBs(bo_, 2048))
        P.add("gpsimd", lambda e, nmw=nmw, et=et, bo_=bo_: e.dma_start(
            out=WbS[nmw][128 * et:128 * (et + 1), :], in_=ab(bo_, 2048)),
            reads=rBs(bo_, 2048), writes=[rscr[nmw + "bS"]], dma=True, g=G("wbs" + nmw))
        if i % 2 == 1:
            a1a_tile(i // 2)
    P.add("sync", lambda e: e.dma_start(out=A[0:16, 4096:4224], in_=bq.rearrange("(a b) -> a b", b=128)),
          writes=rAs(4096, 128), dma=True, g=G("a2"))
    P.add("sync", lambda e: e.dma_start(out=A[16:32, 4096:4224], in_=bk.rearrange("(a b) -> a b", b=128)),
          writes=rAs(4096, 128), dma=True, g=G("a2"))
    P.add("vector", lambda e: e.tensor_copy(Bb[0:32, 4096:4224], A[0:32, 4096:4224]),
          reads=rAs(4096, 128), writes=rBs(4096, 128))
    P.add("gpsimd", lambda e: e.dma_start(out=bqkS, in_=Bb[0:32, 4096:4224]),
          reads=rBs(4096, 128), writes=[rscr["bqkS"]], dma=True, g=G("bqks"))
    P.add("sync", lambda e: e.dma_start(out=af(8192, 2048)[0:1, :], in_=bv.rearrange("(o a) -> o a", o=1)),
          writes=rAs(8192, 2048), dma=True, g=G("a3"))
    P.add("vector", lambda e: e.tensor_copy(bvrow[0:1, :], af(8192, 2048)[0:1, :]),
          reads=rAs(8192, 2048), writes=[rbv], small=True)
    P.add("sync", lambda e: e.dma_start(out=af(8192, 2048)[0:1, :], in_=bo.rearrange("(o a) -> o a", o=1)),
          writes=rAs(8192, 2048), dma=True, g=G("a3"))
    P.add("vector", lambda e: e.tensor_copy(borow[0:1, :], af(8192, 2048)[0:1, :]),
          reads=rAs(8192, 2048), writes=[rbo], small=True)
    P.add("sync", lambda e: e.dma_start(out=Bb[:, 4096:4128], in_=bqkS, transpose=True),
          reads=[rscr["bqkS"]], writes=rBs(4096, 64), dma=True, g=G("bqkt"))
    P.add("vector", lambda e: e.tensor_copy(bqkf[:, :], Bb[:, 4096:4128]),
          reads=rBs(4096, 64), writes=[rbqk], small=True)

    for dst, srcap, w, rr in ((cosb, C_cosk, S, rtab), (sinb, C_sink, S, rtab),
                              (cosbo, cosoI, OWN, rtab), (sinbo, sinoI, OWN, rtab)):
        for c0 in range(0, w, 2048):
            P.add("sync", lambda e, dst=dst, srcap=srcap, c0=c0: e.dma_start(
                out=af(0, 2048), in_=srcap[:, c0:c0 + 2048]),
                writes=rAs(0, 2048), dma=True, g=G("a0"))
            P.add("scalar", lambda e, dst=dst, c0=c0: e.activation(dst[:, c0:c0 + 2048], af(0, 2048), AF.Copy),
                  reads=rAs(0, 2048), writes=[rr])

    def w_src(i):
        if i < 64:
            return W1[128 * i:128 * (i + 1), :]
        j = i - 64
        return W2[128 * (j // 4):128 * (j // 4 + 1), 2048 * (j % 4):2048 * (j % 4 + 1)]

    def w_dst(i):
        if i < 64:
            return W1qS[128 * i:128 * (i + 1), :]
        j = i - 64
        return W2qS[128 * (j // 4):128 * (j // 4 + 1), 2048 * (j % 4):2048 * (j % 4 + 1)]

    ACC = [colsf[:, 4:5], colsf[:, 5:6]]
    RCOL = colsf[:, 6:7]
    rACC = [rcols[4], rcols[5]]
    rRC = rcols[6]
    MSL = [0, 2048, 4096, 6144]
    MGRP = ["a0", "am1", "a2", "am3"]
    MENG = ["sync", "gpsimd", "sync", "gpsimd"]

    def mean_tile(i):
        s = i % 4
        P.add(MENG[s], lambda e, i=i, s=s: e.dma_start(out=af(MSL[s], 2048), in_=w_src(i)),
              writes=rAs(MSL[s], 2048), dma=True, g=G(MGRP[s]))
        P.add("vector", lambda e, s=s: e.tensor_reduce(RCOL, af(MSL[s], 2048), AX.X, ALU.add,
                                                       apply_absolute_value=True),
              reads=rAs(MSL[s], 2048), writes=[rRC], small=True)
        jj = 0 if i < 64 else 1
        if i in (0, 64):
            P.add("vector", lambda e, jj=jj: e.tensor_copy(ACC[jj], RCOL), reads=[rRC], writes=[rACC[jj]], small=True)
        else:
            P.add("vector", lambda e, jj=jj: e.tensor_add(ACC[jj], ACC[jj], RCOL), reads=[rRC], writes=[rACC[jj]], small=True)

    def swc_chain():
        for j in range(2):
            P.add("tensor", lambda e, j=j: e.matmul(PS[0][0:1, 0:1], o128f[:, :], ACC[j], start=True, stop=True),
                  reads=[rACC[j], rconst], writes=[rPS[0]])
            P.add("vector", lambda e, j=j: e.tensor_scalar(smallf[0:1, 2 * j:2 * j + 1], PS[0][0:1, 0:1],
                                                           1.0 / (INNER * E), 1e-5, ALU.mult, ALU.max),
                  reads=[rPS[0]], writes=[rsmall], small=True)
            P.add("vector", lambda e, j=j: e.reciprocal(smallf[0:1, 2 * j + 1:2 * j + 2], smallf[0:1, 2 * j:2 * j + 1]),
                  reads=[rsmall], writes=[rsmall], small=True)
            P.add("tensor", lambda e, j=j: e.matmul(PS[1][:, 0:2], or128f[:, :], smallf[0:1, 2 * j:2 * j + 2],
                                                    start=True, stop=True),
                  reads=[rsmall, rconst], writes=[rPS[1]])
            P.add("scalar", lambda e, j=j: e.activation(swc[:, 2 * j:2 * j + 2], PS[1][:, 0:2], AF.Copy),
                  reads=[rPS[1]], writes=[rswc])

    T0q, GQ, LQ = 8192, 10240, 12288
    QOB = [54272, 56320]

    def quant_tile(i):
        s = i % 4
        bo_ = QOB[i % 2]
        scol = swc[:, 1:2] if i < 64 else swc[:, 3:4]
        P.add(MENG[s], lambda e, i=i, s=s: e.dma_start(out=af(MSL[s], 2048), in_=w_src(i)),
              writes=rAs(MSL[s], 2048), dma=True, g=G(MGRP[s]))
        P.add("scalar", lambda e, s=s, scol=scol: e.activation(af(T0q, 2048), af(MSL[s], 2048), AF.Copy, scale=scol),
              reads=rAs(MSL[s], 2048) + [rswc], writes=rAs(T0q, 2048))
        P.add("vector", lambda e: e.tensor_scalar(af(GQ, 2048), af(T0q, 2048), 0.5, None, ALU.is_gt),
              reads=rAs(T0q, 2048), writes=rAs(GQ, 2048))
        P.add("vector", lambda e: e.tensor_scalar(af(LQ, 2048), af(T0q, 2048), -0.5, None, ALU.is_lt),
              reads=rAs(T0q, 2048), writes=rAs(LQ, 2048))
        P.add("vector", lambda e, bo_=bo_: e.tensor_sub(ab(bo_, 2048), af(GQ, 2048), af(LQ, 2048)),
              reads=rAs(GQ, 2048) + rAs(LQ, 2048), writes=rBs(bo_, 2048))
        dn = "W1qS" if i < 64 else "W2qS"
        P.add("gpsimd", lambda e, i=i, bo_=bo_: e.dma_start(out=w_dst(i), in_=ab(bo_, 2048)),
              reads=rBs(bo_, 2048), writes=[rscr[dn]], dma=True, g=G(dn))

    def mk_feeder(fn, total):
        state = {"i": 0}

        def feed():
            if state["i"] < total:
                fn(state["i"])
                state["i"] += 1
        return feed

    WT0 = 0
    XF = [32768, 40960]
    KB0, PB0, RT0, RO0, VB0 = 49152, 50176, 51200, 52224, 53248

    def wTt(et):
        return Bb[:, WT0 + 2048 * et:WT0 + 2048 * et + 2048]

    rWTg = rBs(WT0, 32768)

    def proj(which, extra=None):
        srcS = xnS if which in ("K", "V") else xnoS
        nkc = 8 if which in ("K", "V") else 4
        wname = which if which != "K" else "Wk"
        wname = {"K": "Wk", "V": "Wv", "Q": "Wq"}[which]
        for et in range(ET):
            P.add("sync", lambda e, et=et, wname=wname: e.dma_start(
                out=wTt(et), in_=WbS[wname][:, 128 * et:128 * (et + 1)], transpose=True),
                reads=[rscr[wname + "bS"]], writes=rBs(WT0 + 2048 * et, 2048), dma=True, g=G("bwt"))
        for kc in range(nkc):
            sl = kc % 2
            rsrc = rxn[kc] if which in ("K", "V") else rscr["xnoS"]
            for et in range(ET):
                P.add("sync", lambda e, kc=kc, et=et, sl=sl, srcS=srcS: e.dma_start(
                    out=Bb[:, XF[sl] + 512 * et:XF[sl] + 512 * et + 512],
                    in_=srcS[512 * kc:512 * (kc + 1), 128 * et:128 * (et + 1)], transpose=True),
                    reads=[rsrc], writes=rBs(XF[sl] + 512 * et, 512), dma=True, g=G("xf%d" % sl))
            rxf = rBs(XF[sl], 8192)
            if which in ("K", "Q"):
                outS = KS if which == "K" else QS
                rout = rscr["KS" if which == "K" else "QS"]
                bcol = 16 if which == "K" else 0
                cosT = cosb if which == "K" else cosbo
                sinT = sinb if which == "K" else sinbo
                for h in range(H):
                    pk = rPS[h % 2]
                    def mmk(e, h=h, sl=sl):
                        r = None
                        for et in range(ET):
                            r = e.matmul(PS[h % 2][:, 0:512], wTt(et)[:, 128 * h:128 * (h + 1)],
                                         Bb[:, XF[sl] + 512 * et:XF[sl] + 512 * et + 512],
                                         start=(et == 0), stop=(et == ET - 1))
                        return r
                    P.add("tensor", mmk, reads=rxf + rWTg, writes=[pk])
                    kb = 512 * (h % 2)
                    P.add("scalar", lambda e, h=h, kb=kb, bcol=bcol: e.activation(
                        Bb[:, KB0 + kb:KB0 + kb + 512], PS[h % 2][:, 0:512], AF.Identity,
                        bias=bqkf[:, bcol + h:bcol + h + 1]),
                        reads=[pk, rbqk], writes=rBs(KB0 + kb, 512))
                    P.add("tensor", lambda e, h=h, kb=kb: e.matmul(
                        PS[2 + h % 2][:, 0:512], permM[:, :], Bb[:, KB0 + kb:KB0 + kb + 512], start=True, stop=True),
                        reads=rBs(KB0 + kb, 512) + [rconst], writes=[rPS[2 + h % 2]])
                    P.add("scalar", lambda e, h=h, kb=kb: e.activation(
                        Bb[:, PB0 + kb:PB0 + kb + 512], PS[2 + h % 2][:, 0:512], AF.Copy),
                        reads=[rPS[2 + h % 2]], writes=rBs(PB0 + kb, 512))
                    def rope(e, kc=kc, h=h, kb=kb, cosT=cosT, sinT=sinT, wq=(which == "Q")):
                        t1 = Bb[:, RT0:RT0 + 512]
                        t2 = Bb[:, RT0 + 512:RT0 + 1024]
                        e.tensor_mul(t1, Bb[:, KB0 + kb:KB0 + kb + 512], cosT[:, 512 * kc:512 * (kc + 1)])
                        e.tensor_mul(t2, Bb[:, PB0 + kb:PB0 + kb + 512], sinT[:, 512 * kc:512 * (kc + 1)])
                        if not wq:
                            return e.tensor_add(Bb[:, RO0 + kb:RO0 + kb + 512], t1, t2)
                        e.tensor_add(t1, t1, t2)
                        e.tensor_scalar(t2, t1, SC_LO, None, ALU.mult)
                        e.tensor_scalar(t1, t1, SC_HI, None, ALU.mult)
                        return e.tensor_add(Bb[:, RO0 + kb:RO0 + kb + 512], t1, t2)
                    P.add("vector", rope, reads=rBs(KB0 + kb, 512) + rBs(PB0 + kb, 512) + [rtab],
                          writes=rBs(RT0, 1024) + rBs(RO0 + kb, 512))
                    P.add("gpsimd", lambda e, kc=kc, h=h, kb=kb, outS=outS: e.dma_start(
                        out=outS[128 * h:128 * (h + 1), 512 * kc:512 * (kc + 1)], in_=Bb[:, RO0 + kb:RO0 + kb + 512]),
                        reads=rBs(RO0 + kb, 512), writes=[rout], dma=True, g=G("ks" if which == "K" else "qs"))
                    if extra:
                        extra()
            else:
                for g2 in range(16):
                    kt, fc = g2 // 4, g2 % 4
                    pv = rPS[4 + g2 % 2]
                    def mmv(e, kt=kt, fc=fc, g2=g2, sl=sl):
                        for et in range(ET):
                            e.matmul(PS[4 + g2 % 2][:, 0:512],
                                     Bb[:, XF[sl] + 512 * et + 128 * kt:XF[sl] + 512 * et + 128 * kt + 128],
                                     wTt(et)[:, 512 * fc:512 * (fc + 1)], start=(et == 0), stop=False)
                        return e.matmul(PS[4 + g2 % 2][:, 0:512], or128b[:, :], bvrow[0:1, 512 * fc:512 * (fc + 1)],
                                        start=False, stop=True)
                    P.add("tensor", mmv, reads=rxf + rWTg + [rconst, rbv], writes=[pv])
                    vb = 512 * (g2 % 2)
                    P.add("scalar", lambda e, g2=g2, vb=vb: e.activation(
                        Bb[:, VB0 + vb:VB0 + vb + 512], PS[4 + g2 % 2][:, 0:512], AF.Copy),
                        reads=[pv], writes=rBs(VB0 + vb, 512))
                    P.add("gpsimd", lambda e, kc=kc, kt=kt, fc=fc, vb=vb: e.dma_start(
                        out=VS[512 * kc + 128 * kt:512 * kc + 128 * kt + 128, 512 * fc:512 * (fc + 1)],
                        in_=Bb[:, VB0 + vb:VB0 + vb + 512]),
                        reads=rBs(VB0 + vb, 512), writes=[rscr["VS"]], dma=True, g=G("vs"))
                    if extra:
                        extra()

    proj("K", extra=mk_feeder(mean_tile, 128))
    swc_chain()
    proj("V", extra=mk_feeder(quant_tile, 128))
    for n in range(16):
        oc, j = n // 4, n % 4
        sl = n % 2
        P.add("sync", lambda e, oc=oc, j=j, sl=sl: e.dma_start(
            out=af(XT[sl], 2048), in_=xo[512 * oc + 128 * j:512 * oc + 128 * j + 128, :]),
            writes=rAs(XT[sl], 2048), dma=True, g=G("a0" if sl == 0 else "a2"))
        emit_ln(XT[sl], XNB[sl], 8, False)
        P.add("gpsimd", lambda e, oc=oc, j=j, sl=sl: e.dma_start(
            out=xnoS[512 * oc + 128 * j:512 * oc + 128 * j + 128, :], in_=ab(XNB[sl], 2048)),
            reads=rBs(XNB[sl], 2048), writes=[rscr["xnoS"]], dma=True, g=G("xnos"))
    proj("Q")

    KH0, VH0, QH0, MSK0, KSQ0, PB_, ONB, QSQ0 = 0, 8192, 16384, 20480, 28672, 32768, 49152, 49408
    SST = 4096
    P.add("sync", lambda e: e.dma_start(out=Bb[:, MSK0:MSK0 + 8192],
                                        in_=maskin.rearrange("(t p) q -> p t q", p=128)),
          writes=rBs(MSK0, 8192), dma=True, g=G("msk"))
    for h in range(H):
        hs = h % 2
        rKH = rBs(KH0 + 4096 * hs, 4096)
        rVH = rBs(VH0 + 4096 * hs, 4096)
        rQH = rBs(QH0 + 2048 * hs, 2048)
        P.add("sync", lambda e, h=h, hs=hs: e.dma_start(
            out=Bb[:, KH0 + 4096 * hs:KH0 + 4096 * hs + 4096], in_=KS[128 * h:128 * (h + 1), :]),
            reads=[rscr["KS"]], writes=rKH, dma=True, g=G("kh%d" % hs))
        for part in range(4):
            P.add("sync", lambda e, h=h, hs=hs, part=part: e.dma_start(
                out=Bb[:, VH0 + 4096 * hs + 1024 * part:VH0 + 4096 * hs + 1024 * part + 1024],
                in_=VS[1024 * part:1024 * (part + 1), 128 * h:128 * (h + 1)].rearrange("(kt p) d -> p kt d", p=128)),
                reads=[rscr["VS"]], writes=rBs(VH0 + 4096 * hs + 1024 * part, 1024), dma=True, g=G("vh%d" % hs))
        P.add("sync", lambda e, h=h, hs=hs: e.dma_start(
            out=Bb[:, QH0 + 2048 * hs:QH0 + 2048 * hs + 2048], in_=QS[128 * h:128 * (h + 1), :]),
            reads=[rscr["QS"]], writes=rQH, dma=True, g=G("qh%d" % hs))
        P.add("scalar", lambda e, hs=hs: e.activation(Bb[:, KSQ0:KSQ0 + 4096],
                                                      Bb[:, KH0 + 4096 * hs:KH0 + 4096 * hs + 4096], AF.Square),
              reads=rKH, writes=rBs(KSQ0, 4096))
        for sl8 in range(8):
            P.add("tensor", lambda e, sl8=sl8: e.matmul(PS[6][0:1, 0:512], o128b[:, :],
                                                        Bb[:, KSQ0 + 512 * sl8:KSQ0 + 512 * sl8 + 512],
                                                        start=True, stop=True),
                  reads=rBs(KSQ0 + 512 * sl8, 512) + [rconst], writes=[rPS[6]])
            P.add("scalar", lambda e, sl8=sl8: e.activation(af(1024 + 0, 512)[0:1, 512 * 0:512] if False else
                                                            A[0:1, 2048 + 512 * (sl8 % 4):2048 + 512 * (sl8 % 4) + 512],
                                                            PS[6][0:1, 0:512], AF.Copy),
                  reads=[rPS[6]], writes=rAs(2048 + 512 * (sl8 % 4), 512), small=True)
            if sl8 % 4 == 3:
                P.add("vector", lambda e, sl8=sl8: e.tensor_reduce(smallf[0:1, 8 + sl8 // 4:9 + sl8 // 4],
                                                                   A[0:1, 2048:2048 + 2048], AX.X, ALU.max),
                      reads=rAs(2048, 2048), writes=[rsmall], small=True)
        P.add("vector", lambda e: e.tensor_max(smallf[0:1, 10:11], smallf[0:1, 8:9], smallf[0:1, 9:10]),
              reads=[rsmall], writes=[rsmall], small=True)
        P.add("vector", lambda e: e.tensor_scalar(smallf[0:1, 10:11], smallf[0:1, 10:11], 1.0, 1e-35, ALU.mult, ALU.add),
              reads=[rsmall], writes=[rsmall], small=True)
        P.add("scalar", lambda e: e.activation(smallf[0:1, 11:12], smallf[0:1, 10:11], AF.Ln),
              reads=[rsmall], writes=[rsmall], small=True)
        P.add("scalar", lambda e, hs=hs: e.activation(Bb[:, KSQ0:KSQ0 + 2048],
                                                      Bb[:, QH0 + 2048 * hs:QH0 + 2048 * hs + 2048], AF.Square),
              reads=rQH, writes=rBs(KSQ0, 2048))
        for j4 in range(4):
            P.add("tensor", lambda e, j4=j4: e.matmul(PS[6][0:1, 0:512], o128b[:, :],
                                                      Bb[:, KSQ0 + 512 * j4:KSQ0 + 512 * j4 + 512],
                                                      start=True, stop=True),
                  reads=rBs(KSQ0 + 512 * j4, 512) + [rconst], writes=[rPS[6]])
            P.add("vector", lambda e, j4=j4: e.tensor_scalar(A[0:1, 12288 + 512 * j4:12800 + 512 * j4],
                                                             PS[6][0:1, 0:512], 1.0, 1e-35, ALU.mult, ALU.add),
                  reads=[rPS[6]], writes=rAs(12288 + 512 * j4, 512), small=True)
        P.add("scalar", lambda e: e.activation(A[0:1, 12288:14336], A[0:1, 12288:14336], AF.Ln),
              reads=rAs(12288, 2048), writes=rAs(12288, 2048), small=True)
        P.add("vector", lambda e: e.tensor_scalar(A[0:1, 12288:14336], A[0:1, 12288:14336],
                                                  smallf[0:1, 11:12], None, ALU.add),
              reads=rAs(12288, 2048) + [rsmall], writes=rAs(12288, 2048), small=True)
        P.add("vector", lambda e: e.tensor_scalar(A[0:1, 12288:14336], A[0:1, 12288:14336], 0.5, 0.02,
                                                  ALU.mult, ALU.add),
              reads=rAs(12288, 2048), writes=rAs(12288, 2048), small=True)
        P.add("scalar", lambda e: e.activation(A[0:1, 12288:14336], A[0:1, 12288:14336], AF.Exp),
              reads=rAs(12288, 2048), writes=rAs(12288, 2048), small=True)
        P.add("vector", lambda e: e.tensor_scalar(A[0:1, 12288:14336], A[0:1, 12288:14336], -1.0, EXPC,
                                                  ALU.mult, ALU.add),
              reads=rAs(12288, 2048), writes=rAs(12288, 2048), small=True)
        def a2_phase1(qg):
            nkt = 4 * (qg + 1)
            gsl = qg % 2
            P.add("tensor", lambda e, qg=qg: e.matmul(PS[7][:, 0:256], or128f[:, :],
                                                      A[0:1, 12288 + 256 * qg:12544 + 256 * qg], start=True, stop=True),
                  reads=rAs(12288 + 256 * qg, 256) + [rconst], writes=[rPS[7]])
            P.add("scalar", lambda e, gsl=gsl: e.activation(af(256 * gsl, 256), PS[7][:, 0:256], AF.Copy),
                  reads=[rPS[7]], writes=rAs(256 * gsl, 256))
            for kt in range(nkt):
                ks = kt % 2
                P.add("tensor", lambda e, hs=hs, kt=kt, ks=ks, qg=qg: e.matmul(
                    PS[ks][:, 0:256], Bb[:, KH0 + 4096 * hs + 128 * kt:KH0 + 4096 * hs + 128 * kt + 128],
                    Bb[:, QH0 + 2048 * hs + 256 * qg:QH0 + 2048 * hs + 256 * qg + 256], start=True, stop=True),
                    reads=rKH + rQH, writes=[rPS[ks]])
                P.add("vector", lambda e, ks=ks, gsl=gsl, kt=kt: e.tensor_add(
                    af(SST + 256 * kt, 256), PS[ks][:, 0:256], af(256 * gsl, 256)),
                    reads=[rPS[ks]] + rAs(256 * gsl, 256), writes=rAs(SST + 256 * kt, 256))
            pso = PB_ + 8192 * gsl
            P.add("scalar", lambda e, nkt=nkt, pso=pso: e.activation(
                Bb[:, pso:pso + 256 * nkt], af(SST, 256 * nkt), AF.Exp),
                reads=rAs(SST, 256 * nkt), writes=rBs(pso, 256 * nkt))
            kt0 = nkt - 4
            P.add("vector", lambda e, kt0=kt0, qg=qg, pso=pso: e.tensor_mul(
                Bb[:, pso + 256 * kt0:pso + 256 * kt0 + 1024], Bb[:, pso + 256 * kt0:pso + 256 * kt0 + 1024],
                Bb[:, MSK0 + 1024 * qg:MSK0 + 1024 * qg + 1024]),
                reads=rBs(pso + 256 * kt0, 1024) + rBs(MSK0 + 1024 * qg, 1024), writes=rBs(pso + 256 * kt0, 1024))

        def a2_phase2(qg):
            nkt = 4 * (qg + 1)
            gsl = qg % 2
            pso = PB_ + 8192 * gsl
            for kt in range(nkt):
                P.add("tensor", lambda e, hs=hs, kt=kt, nkt=nkt, pso=pso: e.matmul(
                    PS[2][:, 0:256], Bb[:, VH0 + 4096 * hs + 128 * kt:VH0 + 4096 * hs + 128 * kt + 128],
                    Bb[:, pso + 256 * kt:pso + 256 * kt + 256], start=(kt == 0), stop=(kt == nkt - 1)),
                    reads=rBs(pso + 256 * kt, 256) + rVH, writes=[rPS[2]])
                P.add("tensor", lambda e, kt=kt, nkt=nkt, pso=pso: e.matmul(
                    PS[3][0:1, 0:256], o128b[:, :], Bb[:, pso + 256 * kt:pso + 256 * kt + 256],
                    start=(kt == 0), stop=(kt == nkt - 1)),
                    reads=rBs(pso + 256 * kt, 256) + [rconst], writes=[rPS[3]])
            P.add("vector", lambda e: e.tensor_scalar(A[0:1, 1536:1792], PS[3][0:1, 0:256], 1.0, 1e-30,
                                                      ALU.mult, ALU.add),
                  reads=[rPS[3]], writes=rAs(1536, 256), small=True)
            P.add("vector", lambda e: e.reciprocal(A[0:1, 1536:1792], A[0:1, 1536:1792]),
                  reads=rAs(1536, 256), writes=rAs(1536, 256), small=True)
            P.add("tensor", lambda e: e.matmul(PS[7][:, 0:256], or128f[:, :], A[0:1, 1536:1792], start=True, stop=True),
                  reads=rAs(1536, 256) + [rconst], writes=[rPS[7]])
            P.add("scalar", lambda e: e.activation(af(1792, 256), PS[7][:, 0:256], AF.Copy),
                  reads=[rPS[7]], writes=rAs(1792, 256))
            P.add("vector", lambda e: e.tensor_mul(Bb[:, ONB:ONB + 256], PS[2][:, 0:256], af(1792, 256)),
                  reads=[rPS[2]] + rAs(1792, 256), writes=rBs(ONB, 256))
            P.add("gpsimd", lambda e, h=h, qg=qg: e.dma_start(
                out=OS[128 * h:128 * (h + 1), 256 * qg:256 * (qg + 1)], in_=Bb[:, ONB:ONB + 256]),
                reads=rBs(ONB, 256), writes=[rscr["OS"]], dma=True, g=G("os"))

        for qg in range(NQG):
            a2_phase1(qg)
            if qg > 0:
                a2_phase2(qg - 1)
        a2_phase2(NQG - 1)

    QA1F0, W1T0, GH0, OFM0, WOT0 = 0, 16384, 32768, 33792, 50176
    for ck in range(NCK):
        for j in range(8):
            sl = j % 2
            row0 = CK * ck + 128 * j
            P.add("sync", lambda e, row0=row0, sl=sl: e.dma_start(out=af(XT[sl], 2048), in_=xo[row0:row0 + 128, :]),
                  writes=rAs(XT[sl], 2048), dma=True, g=G("a0" if sl == 0 else "a2"))
            emit_ln(XT[sl], XNB[sl], 16 + 8 * sl, True)
            P.add("vector", lambda e, j=j, sl=sl: e.tensor_copy(colsf[:, 32 + j:33 + j],
                                                                colsf[:, 16 + 8 * sl + 5:16 + 8 * sl + 6]),
                  reads=[rcols[16 + 8 * sl]], writes=[rcols[32 + j]], small=True)
            P.add("gpsimd", lambda e, j=j, sl=sl: e.dma_start(out=QA1S[128 * j:128 * (j + 1), :], in_=ab(XNB[sl], 2048)),
                  reads=rBs(XNB[sl], 2048), writes=[rscr["QA1S"]], dma=True, g=G("qa1s"))
        for et in range(ET):
            P.add("sync", lambda e, et=et: e.dma_start(out=Bb[:, QA1F0 + 1024 * et:QA1F0 + 1024 * et + 1024],
                                                       in_=QA1S[:, 128 * et:128 * (et + 1)], transpose=True),
                  reads=[rscr["QA1S"]], writes=rBs(QA1F0 + 1024 * et, 1024), dma=True, g=G("qa1f"))
        rQA1F = rBs(QA1F0, 16384)
        for og in range(16):
            osl = og % 2
            rW1T = rBs(W1T0 + 8192 * osl, 8192)
            for et in range(ET):
                P.add("sync", lambda e, og=og, et=et, osl=osl: e.dma_start(
                    out=Bb[:, W1T0 + 8192 * osl + 512 * et:W1T0 + 8192 * osl + 512 * et + 512],
                    in_=W1qS[512 * og:512 * (og + 1), 128 * et:128 * (et + 1)], transpose=True),
                    reads=[rscr["W1qS"]], writes=rBs(W1T0 + 8192 * osl + 512 * et, 512), dma=True, g=G("w1t%d" % osl))
            for tt in range(8):
                pb = rPS[tt % 2]
                def mm1(e, tt=tt, osl=osl):
                    r = None
                    for et in range(ET):
                        r = e.matmul(PS[tt % 2][:, 0:512],
                                     Bb[:, QA1F0 + 1024 * et + 128 * tt:QA1F0 + 1024 * et + 128 * tt + 128],
                                     Bb[:, W1T0 + 8192 * osl + 512 * et:W1T0 + 8192 * osl + 512 * et + 512],
                                     start=(et == 0), stop=(et == ET - 1))
                    return r
                P.add("tensor", mm1, reads=rQA1F + rW1T, writes=[pb])
                gb = 13312 + 512 * ((og * 8 + tt) % 2)
                P.add("scalar", lambda e, tt=tt, gb=gb: e.activation(
                    af(gb, 512), PS[tt % 2][:, 0:512], AF.Gelu, scale=colsf[:, 32 + tt:33 + tt]),
                    reads=[pb, rcols[32 + tt]], writes=rAs(gb, 512), small=True)
                P.add("scalar", lambda e, gb=gb: e.activation(af(12288, 512), af(gb, 512),
                                                              AF.Square, accum_out=colsf[:, 56:57]),
                      reads=rAs(gb, 512), writes=rAs(12288, 512) + [rcols[56]], small=True)
                if og == 0:
                    P.add("vector", lambda e, tt=tt: e.tensor_copy(colsf[:, 40 + tt:41 + tt], colsf[:, 56:57]),
                          reads=[rcols[56]], writes=[rcols[40 + tt]], small=True)
                else:
                    P.add("vector", lambda e, tt=tt: e.tensor_add(colsf[:, 40 + tt:41 + tt],
                                                                  colsf[:, 40 + tt:41 + tt], colsf[:, 56:57]),
                          reads=[rcols[56]], writes=[rcols[40 + tt]], small=True)
                P.add("vector", lambda e, gb=gb: e.tensor_reduce(colsf[:, 57:58], af(gb, 512),
                                                                 AX.X, ALU.max, apply_absolute_value=True),
                      reads=rAs(gb, 512), writes=[rcols[57]], small=True)
                if og == 0:
                    P.add("vector", lambda e, tt=tt: e.tensor_copy(colsf[:, 48 + tt:49 + tt], colsf[:, 57:58]),
                          reads=[rcols[57]], writes=[rcols[48 + tt]], small=True)
                else:
                    P.add("vector", lambda e, tt=tt: e.tensor_max(colsf[:, 48 + tt:49 + tt],
                                                                  colsf[:, 48 + tt:49 + tt], colsf[:, 57:58]),
                          reads=[rcols[57]], writes=[rcols[48 + tt]], small=True)
                P.add("gpsimd", lambda e, og=og, tt=tt, gb=gb: e.dma_start(
                    out=ghS[128 * tt:128 * (tt + 1), 512 * og:512 * (og + 1)], in_=af(gb, 512)),
                    reads=rAs(gb, 512), writes=[rscr["ghS"]], dma=True, g=G("ghs"))
        for tt in range(8):
            gs = colsf[:, 40 + tt:41 + tt]
            gm = colsf[:, 48 + tt:49 + tt]
            rgs, rgm = rcols[40 + tt], rcols[48 + tt]
            P.add("scalar", lambda e, gs=gs: e.activation(gs, gs, AF.Sqrt), reads=[rgs], writes=[rgs], small=True)
            P.add("vector", lambda e, gs=gs: e.tensor_scalar(gs, gs, 1.0, 1e-12, ALU.mult, ALU.max),
                  reads=[rgs], writes=[rgs], small=True)
            P.add("vector", lambda e, gs=gs: e.reciprocal(gs, gs), reads=[rgs], writes=[rgs], small=True)
            P.add("vector", lambda e, gs=gs: e.tensor_scalar(colsf[:, 59:60], gs, SQB_LO, None, ALU.mult),
                  reads=[rgs], writes=[rcols[59]], small=True)
            P.add("vector", lambda e, gs=gs: e.tensor_scalar(gs, gs, SQB_HI, None, ALU.mult),
                  reads=[rgs], writes=[rgs], small=True)
            P.add("vector", lambda e, gs=gs: e.tensor_add(gs, gs, colsf[:, 59:60]),
                  reads=[rgs, rcols[59]], writes=[rgs], small=True)
            P.add("vector", lambda e, gs=gs, gm=gm: e.tensor_mul(gm, gm, gs), reads=[rgs], writes=[rgm], small=True)
            P.add("vector", lambda e, gm=gm: e.tensor_scalar(gm, gm, 1.0, 1e-5, ALU.mult, ALU.max),
                  reads=[rgm], writes=[rgm], small=True)
            P.add("vector", lambda e, gm=gm, tt=tt: e.tensor_scalar(colsf[:, 64 + tt:65 + tt], gm,
                                                                    I127_HI, None, ALU.mult),
                  reads=[rgm], writes=[rcols[64 + tt]], small=True)
            P.add("vector", lambda e, gm=gm: e.tensor_scalar(colsf[:, 59:60], gm, I127_LO, None, ALU.mult),
                  reads=[rgm], writes=[rcols[59]], small=True)
            P.add("vector", lambda e, tt=tt: e.tensor_add(colsf[:, 64 + tt:65 + tt], colsf[:, 64 + tt:65 + tt],
                                                          colsf[:, 59:60]),
                  reads=[rcols[59]], writes=[rcols[64 + tt]], small=True)
            P.add("vector", lambda e, tt=tt: e.tensor_mul(colsf[:, 64 + tt:65 + tt], colsf[:, 64 + tt:65 + tt],
                                                          swc[:, 2:3]),
                  reads=[rswc], writes=[rcols[64 + tt]], small=True)
            P.add("vector", lambda e, gm=gm: e.reciprocal(gm, gm), reads=[rgm], writes=[rgm], small=True)
            P.add("vector", lambda e, gm=gm: e.tensor_scalar(gm, gm, 127.0, None, ALU.mult), reads=[rgm], writes=[rgm], small=True)
            P.add("vector", lambda e, gs=gs, gm=gm: e.tensor_mul(gs, gs, gm), reads=[rgm], writes=[rgs], small=True)
        for og in range(16):
            for tt in range(8):
                gb = 512 * ((og * 8 + tt) % 8)
                gbo = 512 * ((og * 8 + tt) % 2)
                P.add("sync", lambda e, og=og, tt=tt, gb=gb: e.dma_start(
                    out=af(gb, 512), in_=ghS[128 * tt:128 * (tt + 1), 512 * og:512 * (og + 1)]),
                    reads=[rscr["ghS"]], writes=rAs(gb, 512), dma=True, g=G("ghb%d" % (gb // 512)))
                P.add("scalar", lambda e, tt=tt, gb=gb: e.activation(af(8192, 512), af(gb, 512), AF.Copy,
                                                                     scale=colsf[:, 40 + tt:41 + tt]),
                      reads=rAs(gb, 512) + [rcols[40 + tt]], writes=rAs(8192, 512), small=True)
                P.add("vector", lambda e: e.tensor_copy(Ii[:, 0:512], af(8192, 512)), reads=rAs(8192, 512), writes=[rI])
                P.add("vector", lambda e: e.tensor_copy(af(8704, 512), Ii[:, 0:512]), reads=[rI], writes=rAs(8704, 512))
                P.add("vector", lambda e: e.tensor_sub(af(9216, 512), af(8704, 512), af(8192, 512)),
                      reads=rAs(8704, 512) + rAs(8192, 512), writes=rAs(9216, 512))
                P.add("vector", lambda e: e.scalar_tensor_tensor(af(8704, 512), af(9216, 512), -0.5, af(8704, 512),
                                                                 ALU.is_lt, ALU.add),
                      reads=rAs(9216, 512) + rAs(8704, 512), writes=rAs(8704, 512))
                P.add("vector", lambda e: e.tensor_scalar(af(9216, 512), af(9216, 512), 0.5, None, ALU.is_gt),
                      reads=rAs(9216, 512), writes=rAs(9216, 512))
                P.add("vector", lambda e, gbo=gbo: e.tensor_sub(Bb[:, GH0 + gbo:GH0 + gbo + 512], af(8704, 512), af(9216, 512)),
                      reads=rAs(8704, 512) + rAs(9216, 512), writes=rBs(GH0 + gbo, 512))
                P.add("gpsimd", lambda e, og=og, tt=tt, gbo=gbo: e.dma_start(
                    out=QA2S[128 * tt:128 * (tt + 1), 512 * og:512 * (og + 1)], in_=Bb[:, GH0 + gbo:GH0 + gbo + 512]),
                    reads=rBs(GH0 + gbo, 512), writes=[rscr["QA2S"]], dma=True, g=G("qa2s"))
        for et in range(ET):
            P.add("sync", lambda e, ck=ck, et=et: e.dma_start(
                out=Bb[:, OFM0 + 1024 * et:OFM0 + 1024 * et + 1024],
                in_=OS[128 * et:128 * (et + 1), CK * ck:CK * (ck + 1)]),
                reads=[rscr["OS"]], writes=rBs(OFM0 + 1024 * et, 1024), dma=True, g=G("ofm"))
        rOFM = rBs(OFM0, 16384)
        for fc in range(4):
            for ot in range(OTN):
                osl = ot % 4
                q2o = W1T0 + 1024 * osl
                w2o = W1T0 + 8192 + 512 * osl
                rq2 = rBs(q2o, 1024)
                rw2 = rBs(w2o, 512)
                P.add("sync", lambda e, ot=ot, q2o=q2o: e.dma_start(
                    out=Bb[:, q2o:q2o + 1024],
                    in_=QA2S[:, 128 * ot:128 * (ot + 1)], transpose=True),
                    reads=[rscr["QA2S"]], writes=rq2, dma=True, g=G("q2f%d" % osl))
                P.add("sync", lambda e, fc=fc, ot=ot, w2o=w2o: e.dma_start(
                    out=Bb[:, w2o:w2o + 512],
                    in_=W2qS[512 * fc:512 * (fc + 1), 128 * ot:128 * (ot + 1)], transpose=True),
                    reads=[rscr["W2qS"]], writes=rw2, dma=True, g=G("w2b%d" % osl))
                def mm2(e, ot=ot, q2o=q2o, w2o=w2o):
                    r = None
                    for tt in range(8):
                        r = e.matmul(PS[tt][:, 0:512],
                                     Bb[:, q2o + 128 * tt:q2o + 128 * tt + 128],
                                     Bb[:, w2o:w2o + 512],
                                     start=(ot == 0), stop=(ot == OTN - 1))
                    return r
                P.add("tensor", mm2, reads=rq2 + rw2, writes=rPS)
            for tt in range(8):
                P.add("vector", lambda e, tt=tt: e.tensor_scalar(af(8192 + 512 * tt // 1, 512) if False else
                                                                 A[:, 8192 + 512 * (tt % 8):8192 + 512 * (tt % 8) + 512],
                                                                 PS[tt][:, 0:512], colsf[:, 64 + tt:65 + tt], None, ALU.mult),
                      reads=[rPS[tt], rcols[64 + tt]], writes=rAs(8192 + 512 * tt, 512), small=True)
            for et in range(ET):
                P.add("sync", lambda e, fc=fc, et=et: e.dma_start(
                    out=Bb[:, WOT0 + 512 * et:WOT0 + 512 * et + 512],
                    in_=WbS["Wo"][512 * fc:512 * (fc + 1), 128 * et:128 * (et + 1)], transpose=True),
                    reads=[rscr["WobS"]], writes=rBs(WOT0 + 512 * et, 512), dma=True, g=G("wot"))
            rWOT = rBs(WOT0, 8192)
            for tt in range(8):
                pa = rPS[tt]
                def mmA(e, tt=tt, fc=fc):
                    for et in range(ET):
                        e.matmul(PS[tt][:, 0:512],
                                 Bb[:, OFM0 + 1024 * et + 128 * tt:OFM0 + 1024 * et + 128 * tt + 128],
                                 Bb[:, WOT0 + 512 * et:WOT0 + 512 * et + 512], start=(et == 0), stop=False)
                    return e.matmul(PS[tt][:, 0:512], or128b[:, :], borow[0:1, 512 * fc:512 * (fc + 1)],
                                    start=False, stop=True)
                P.add("tensor", mmA, reads=rOFM + rWOT + [rconst, rbo] + rAs(8192 + 512 * tt, 512), writes=[pa])
                xb = 12288 + 512 * (tt % 2)
                P.add("sync", lambda e, ck=ck, tt=tt, fc=fc, xb=xb: e.dma_start(
                    out=af(xb, 512), in_=xo[CK * ck + 128 * tt:CK * ck + 128 * tt + 128, 512 * fc:512 * (fc + 1)]),
                    writes=rAs(xb, 512), dma=True, g=G("xs%d" % (tt % 2)))
                P.add("vector", lambda e, tt=tt, xb=xb: e.tensor_add(af(xb, 512), af(xb, 512),
                                                                     A[:, 8192 + 512 * tt:8192 + 512 * tt + 512]),
                      reads=rAs(8192 + 512 * tt, 512) + rAs(xb, 512), writes=rAs(xb, 512))
                P.add("vector", lambda e, tt=tt, xb=xb: e.tensor_add(af(xb, 512), af(xb, 512), PS[tt][:, 0:512]),
                      reads=[pa] + rAs(xb, 512), writes=rAs(xb, 512))
                o = P.add("gpsimd", lambda e, ck=ck, tt=tt, fc=fc, xb=xb: e.dma_start(
                    out=out[CK * ck + 128 * tt:CK * ck + 128 * tt + 128, 512 * fc:512 * (fc + 1)], in_=af(xb, 512)),
                    reads=rAs(xb, 512), dma=True, g=G("gout"))
                OUT_OPS.append(o)

    P.add("sync", lambda e: e.nop(), deps=OUT_OPS)

    if DEBUG_DUMP:
        dbg_list = [("xnS", xnS, [S, E]), ("KS", KS, [E, S]), ("VS", VS, [S, E]),
                    ("QS", QS, [E, OWN]), ("OS", OS, [E, OWN]),
                    ("W1qS", W1qS, [INNER, E]), ("ghS", ghS, [CK, INNER]), ("QA2S", QA2S, [CK, INNER]),
                    ("QA1S", QA1S, [CK, E])]
        for nm, ap, shp in dbg_list:
            dout = nc.dram_tensor("dbg_" + nm, shp, BF16, kind="ExternalOutput").ap()
            P.add("gpsimd", lambda e, ap=ap, dout=dout: e.dma_start(out=dout, in_=ap),
                  reads=[rscr[nm] if nm in rscr else rxn[7]], deps=OUT_OPS, dma=True, g=G("gdbg"))
        dsw = nc.dram_tensor("dbg_swc", [128, 4], F32, kind="ExternalOutput").ap()
        P.add("gpsimd", lambda e: e.dma_start(out=dsw, in_=swc[:, :]), reads=[rswc], deps=OUT_OPS, dma=True, g=G("gdbg"))
        fin = [op for op in P.ops["gpsimd"] if op.grp is not None and op.grp.name == "gdbg"]
        P.add("sync", lambda e: e.nop(), deps=fin)

    P.finalize()
    sems = {e: nc.alloc_semaphore("sem_" + e) for e in ROLES}
    for gg in GR.values():
        gg.sem = nc.alloc_semaphore("g_" + gg.name)
    with nc.Block() as block:
        @block.sync
        def _(eng):
            P.emit("sync", eng, sems)

        @block.tensor
        def _(eng):
            P.emit("tensor", eng, sems)

        @block.vector
        def _(eng):
            P.emit("vector", eng, sems)

        @block.scalar
        def _(eng):
            P.emit("scalar", eng, sems)

        @block.gpsimd
        def _(eng):
            P.emit("gpsimd", eng, sems)
    return nc


_NC = None


def _get_nc():
    global _NC
    if _NC is None:
        _NC = build()
    return _NC


def _pack_inputs(inputs):
    x = np.asarray(inputs["L0_attn_x"], dtype=np.float32)
    masks = {}
    posos = {}
    cosos = {}
    sinos = {}
    for g in range(2):
        m = np.zeros((KTN, 128, QG), dtype=np.float32)
        for qg in range(NQG):
            nkt = 4 * (qg + 1)
            for slot in range(4):
                kt = nkt - 4 + slot
                for lt in range(2):
                    gt = 4 * qg + 2 * lt + g
                    qglob = 128 * gt + np.arange(128)
                    kglob = 128 * kt + np.arange(128)
                    m[4 * qg + slot, :, 128 * lt:128 * (lt + 1)] = (kglob[:, None] <= qglob[None, :])
        masks[g] = np.ascontiguousarray(m.reshape(KTN * 128, QG).astype(BF))
        own_tiles = [2 * i + g for i in range(NT)]
        pos = np.concatenate([128 * t + np.arange(128) for t in own_tiles]).astype(np.float32)
        posos[g] = np.ascontiguousarray(np.tile(pos, (128, 1)))
        cos32, sinF32 = _rope_tables_f32()
        idx = pos.astype(np.int64)
        cosos[g] = np.ascontiguousarray(cos32[idx].T)
        sinos[g] = np.ascontiguousarray(sinF32[idx].T)
    base = {
        "Wq": np.ascontiguousarray(np.asarray(inputs["L0_attn_Wq"], dtype=np.float32)),
        "Wk": np.ascontiguousarray(np.asarray(inputs["L0_attn_Wk"], dtype=np.float32)),
        "Wv": np.ascontiguousarray(np.asarray(inputs["L0_attn_Wv"], dtype=np.float32)),
        "Wo": np.ascontiguousarray(np.asarray(inputs["L0_attn_Wo"], dtype=np.float32)),
        "bq": np.ascontiguousarray(np.asarray(inputs["L0_attn_bq"], dtype=np.float32)),
        "bk": np.ascontiguousarray(np.asarray(inputs["L0_attn_bk"], dtype=np.float32)),
        "bv": np.ascontiguousarray(np.asarray(inputs["L0_attn_bv"], dtype=np.float32)),
        "bo": np.ascontiguousarray(np.asarray(inputs["L0_attn_bo"], dtype=np.float32)),
        "W1": np.ascontiguousarray(np.asarray(inputs["L0_ffn_w1"], dtype=np.float32)),
        "W2": np.ascontiguousarray(np.asarray(inputs["L0_ffn_w2"], dtype=np.float32)),
    }
    in_maps = []
    for c in range(8):
        b, g = c // 2, c % 2
        own_tiles = [2 * i + g for i in range(NT)]
        xb = np.ascontiguousarray(x[b])
        xown = np.ascontiguousarray(xb.reshape(32, 128, E)[own_tiles].reshape(OWN, E))
        d = dict(base)
        d.update({"xk": xb, "xo": xown, "maskin": masks[g], "poso": posos[g],
                  "cosoI": cosos[g], "sinoI": sinos[g]})
        in_maps.append(d)
    return in_maps


def kernel(**inputs):
    nc = _get_nc()
    in_maps = _pack_inputs(inputs)
    res = run_bass_kernel_spmd(nc, in_maps, core_ids=list(range(8)))
    outp = np.zeros((B_, S, E), dtype=np.float32)
    for c in range(8):
        b, g = c // 2, c % 2
        own_tiles = [2 * i + g for i in range(NT)]
        o = np.asarray(res.results[c]["out"], dtype=np.float32).reshape(NT, 128, E)
        outp.reshape(B_, 32, 128, E)[b, own_tiles] = o
    return outp


# version 16
# speedup vs baseline: 1.0062x; 1.0062x over previous
import math
import numpy as np
import ml_dtypes
import concourse.bass as bass
import concourse.mybir as mybir
from concourse.bass_utils import run_bass_kernel_spmd

F32 = mybir.dt.float32
BF16 = mybir.dt.bfloat16
I32 = mybir.dt.int32
AF = mybir.ActivationFunctionType
ALU = mybir.AluOpType
AX = mybir.AxisListType

B_, S, E, H, D = 4, 4096, 2048, 16, 128
INNER = 8192
OWN = 2048
NT = 16
KTN = 32
NQG = 8
QG = 256
CK = 1024
NCK = OWN // CK
ET = E // 128
OTN = INNER // 128
SCALE = 1.0 / math.sqrt(D)
EXPC = 18.0
BF = ml_dtypes.bfloat16
ROLES = ("sync", "tensor", "vector", "scalar", "gpsimd")
DEBUG_DUMP = False


def _split(v):
    hi = np.float32(v)
    return float(hi), float(np.float64(v) - np.float64(hi))


LNC_HI, LNC_LO = _split(math.log(10000.0))
SC_HI, SC_LO = _split(1.0 / float(np.float32(np.sqrt(np.float32(128.0)))))
SQA_HI, SQA_LO = _split(math.sqrt(2048.0))
SQB_HI, SQB_LO = _split(math.sqrt(8192.0))
I127_HI, I127_LO = _split(1.0 / 127.0)
TPI_HI, TPI_LO = _split(2.0 * math.pi)
PIH_HI, PIH_LO = _split(0.5 * math.pi)
INV2PI = float(np.float32(1.0 / (2.0 * math.pi)))
PI_CLAMP = 3.1415925


_TABLES = None


def _rope_tables_f32():
    global _TABLES
    if _TABLES is not None:
        return _TABLES
    try:
        import jax
        import jax.numpy as jnp
        with jax.default_device(jax.devices("cpu")[0]):
            theta = jnp.repeat(-jnp.arange(0, D, 2, dtype=jnp.float32) / D, 2)
            theta = jnp.power(10000.0, theta)
            pos = jnp.arange(S, dtype=jnp.float32)[:, None]
            ang = pos * theta[None, :]
            cos = np.asarray(jax.device_get(jnp.cos(ang)), dtype=np.float32)
            sin = np.asarray(jax.device_get(jnp.sin(ang)), dtype=np.float32)
    except Exception:
        d = np.float32(128.0)
        theta = np.repeat(-(np.arange(0, 128, dtype=np.float32))[::2] / d, 2).astype(np.float32)
        theta = np.power(np.float32(10000.0), theta).astype(np.float32)
        ang = (np.arange(S, dtype=np.float32)[:, None] * theta[None, :]).astype(np.float32)
        cos = np.cos(ang).astype(np.float32)
        sin = np.sin(ang).astype(np.float32)
    sgn = np.where(np.arange(128) % 2 == 0, np.float32(-1.0), np.float32(1.0))
    _TABLES = (cos, (sin * sgn[None, :]).astype(np.float32))
    return _TABLES


class Op:
    __slots__ = ("eng", "fn", "deps", "inc", "cum", "needed", "grp", "gcum", "small")

    def __init__(self, eng, fn, deps, inc):
        self.eng, self.fn, self.deps, self.inc = eng, fn, deps, inc
        self.cum, self.needed = 0, False
        self.grp, self.gcum = None, 0
        self.small = False


class Grp:
    __slots__ = ("name", "count", "sem")

    def __init__(self, name):
        self.name, self.count, self.sem = name, 0, None


class Res:
    __slots__ = ("w", "readers")

    def __init__(self):
        self.w, self.readers = None, []


class Prog:
    def __init__(self):
        self.ops = {e: [] for e in ROLES}
        self.all = []

    def add(self, eng, fn, reads=(), writes=(), deps=(), dma=False, g=None, small=False):
        dd = []
        seen = set()
        def push(d):
            if d is not None and id(d) not in seen:
                seen.add(id(d))
                dd.append(d)
        for d in deps:
            push(d)
        for r in reads:
            push(r.w)
        for r in writes:
            for x in r.readers:
                push(x)
            push(r.w)
        op = Op(eng, fn, dd, 16 if dma else 1)
        op.small = small
        if dma:
            assert g is not None, "dma op needs a group"
            g.count += 16
            op.grp, op.gcum = g, g.count
        for r in reads:
            r.readers.append(op)
        for r in writes:
            r.w, r.readers = op, []
        self.ops[eng].append(op)
        self.all.append(op)
        return op

    def finalize(self):
        for op in self.all:
            if op.grp is not None:
                op.needed = True
            for d in op.deps:
                if d.eng != op.eng or d.grp is not None:
                    d.needed = True
        for e in ROLES:
            c = 0
            for op in self.ops[e]:
                if op.needed and op.grp is None:
                    c += op.inc
                    op.cum = c

    def emit(self, role, eng, sems):
        wm = {}
        for op in self.ops[role]:
            for d in op.deps:
                if not d.needed:
                    continue
                if d.grp is not None:
                    key, val, sem = d.grp.name, d.gcum, d.grp.sem
                elif d.eng != role:
                    key, val, sem = d.eng, d.cum, sems[d.eng]
                else:
                    continue
                if val > wm.get(key, 0):
                    eng.wait_ge(sem, val)
                    wm[key] = val
            r = op.fn(eng)
            if op.small and op.grp is None:
                r = eng.drain()
            if op.needed:
                tgt = op.grp.sem if op.grp is not None else sems[op.eng]
                (r if r is not None else eng.nop()).then_inc(tgt, op.inc)


def build():
    nc = bass.Bass("TRN2", target_bir_lowering=False, detect_race_conditions=False)

    xk = nc.dram_tensor("xk", [S, E], F32, kind="ExternalInput").ap()
    xo = nc.dram_tensor("xo", [OWN, E], F32, kind="ExternalInput").ap()
    Wq = nc.dram_tensor("Wq", [E, E], F32, kind="ExternalInput").ap()
    Wk = nc.dram_tensor("Wk", [E, E], F32, kind="ExternalInput").ap()
    Wv = nc.dram_tensor("Wv", [E, E], F32, kind="ExternalInput").ap()
    Wo = nc.dram_tensor("Wo", [E, E], F32, kind="ExternalInput").ap()
    bq = nc.dram_tensor("bq", [E], F32, kind="ExternalInput").ap()
    bk = nc.dram_tensor("bk", [E], F32, kind="ExternalInput").ap()
    bv = nc.dram_tensor("bv", [E], F32, kind="ExternalInput").ap()
    bo = nc.dram_tensor("bo", [E], F32, kind="ExternalInput").ap()
    W1 = nc.dram_tensor("W1", [INNER, E], F32, kind="ExternalInput").ap()
    W2 = nc.dram_tensor("W2", [E, INNER], F32, kind="ExternalInput").ap()
    maskin = nc.dram_tensor("maskin", [KTN * 128, QG], BF16, kind="ExternalInput").ap()
    poso = nc.dram_tensor("poso", [128, OWN], F32, kind="ExternalInput").ap()
    out = nc.dram_tensor("out", [OWN, E], F32, kind="ExternalOutput").ap()

    def scr(name, shape, dt=BF16):
        return nc.dram_tensor(name, shape, dt, kind="Internal").ap()

    WbS = {"Wq": scr("WqbS", [E, E]), "Wk": scr("WkbS", [E, E]),
           "Wv": scr("WvbS", [E, E]), "Wo": scr("WobS", [E, E])}
    bqkS = scr("bqkS", [32, 128])
    W1qS = scr("W1qS", [INNER, E])
    W2qS = scr("W2qS", [E, INNER])
    xnS = scr("xnS", [S, E])
    xnoS = scr("xnoS", [OWN, E])
    KS = scr("KS", [E, S])
    VS = scr("VS", [S, E])
    QS = scr("QS", [E, OWN])
    OS = scr("OS", [E, OWN])
    QA1S = scr("QA1S", [CK, E])
    ghS = scr("ghS", [CK, INNER], F32)
    QA2S = scr("QA2S", [CK, INNER])

    cos32, sinF32 = _rope_tables_f32()
    C_cosk = nc.inline_tensor(np.ascontiguousarray(cos32.T), "coskC").ap()
    C_sink = nc.inline_tensor(np.ascontiguousarray(sinF32.T), "sinkC").ap()
    cosoI = nc.dram_tensor("cosoI", [128, OWN], F32, kind="ExternalInput").ap()
    sinoI = nc.dram_tensor("sinoI", [128, OWN], F32, kind="ExternalInput").ap()
    perm_np = np.zeros((128, 128), dtype=np.float32)
    for i in range(64):
        perm_np[2 * i + 1, 2 * i] = 1.0
        perm_np[2 * i, 2 * i + 1] = 1.0
    C_perm = nc.inline_tensor(perm_np.astype(BF), "permC").ap()
    C_o128b = nc.inline_tensor(np.ones((128, 1), dtype=BF), "o128bC").ap()
    C_o128f = nc.inline_tensor(np.ones((128, 1), dtype=np.float32), "o128fC").ap()
    C_or128b = nc.inline_tensor(np.ones((1, 128), dtype=BF), "or128bC").ap()
    C_or128f = nc.inline_tensor(np.ones((1, 128), dtype=np.float32), "or128fC").ap()

    A = nc.alloc_sbuf_tensor("Af", [128, 14336], F32)
    Bb = nc.alloc_sbuf_tensor("Bbf", [128, 58368], BF16)
    Ii = nc.alloc_sbuf_tensor("Iint", [128, 1024], I32)
    cosb = nc.alloc_sbuf_tensor("cosb", [128, S], BF16)
    sinb = nc.alloc_sbuf_tensor("sinb", [128, S], BF16)
    cosbo = nc.alloc_sbuf_tensor("cosbo", [128, OWN], BF16)
    sinbo = nc.alloc_sbuf_tensor("sinbo", [128, OWN], BF16)
    o128b = nc.alloc_sbuf_tensor("o128b", [128, 1], BF16)
    o128f = nc.alloc_sbuf_tensor("o128f", [128, 1], F32)
    or128b = nc.alloc_sbuf_tensor("or128b", [1, 128], BF16)
    or128f = nc.alloc_sbuf_tensor("or128f", [1, 128], F32)
    permM = nc.alloc_sbuf_tensor("permM", [128, 128], BF16)
    bqkf = nc.alloc_sbuf_tensor("bqkf", [128, 32], F32)
    bvrow = nc.alloc_sbuf_tensor("bvrow", [1, E], BF16)
    borow = nc.alloc_sbuf_tensor("borow", [1, E], BF16)
    swc = nc.alloc_sbuf_tensor("swc", [128, 4], F32)
    smallf = nc.alloc_sbuf_tensor("smallf", [1, 64], F32)
    colsf = nc.alloc_sbuf_tensor("colsf", [128, 80], F32)
    PS = [nc.alloc_psum_tensor("ps%d" % i, [128, 512], F32) for i in range(8)]

    P = Prog()
    GR = {}

    def G(name):
        if name not in GR:
            GR[name] = Grp(name)
        return GR[name]

    NGA = 14336 // 256
    NGB = 58368 // 512
    rga = [Res() for _ in range(NGA)]
    rgb = [Res() for _ in range(NGB)]
    rI = Res()
    rPS = [Res() for _ in range(8)]
    rtab = Res()
    rconst = Res()
    rbqk, rbv, rbo, rswc, rsmall = Res(), Res(), Res(), Res(), Res()
    rcols = [Res() for _ in range(80)]
    rscr = {k: Res() for k in ("WqbS", "WkbS", "WvbS", "WobS", "bqkS", "W1qS", "W2qS",
                               "xnoS", "QS", "OS", "QA1S", "ghS", "QA2S", "KS", "VS")}
    rxn = [Res() for _ in range(8)]

    def af(off, w):
        assert off + w <= 14336
        return A[:, off:off + w]

    def ab(off, w):
        assert off + w <= 58368
        return Bb[:, off:off + w]

    def rAs(off, w):
        return [rga[i] for i in range(off // 256, (off + w + 255) // 256)]

    def rBs(off, w):
        return [rgb[i] for i in range(off // 512, (off + w + 511) // 512)]

    OUT_OPS = []

    def emit_round2048(src, dst, tmp):
        for c0 in range(0, 2048, 1024):
            P.add("vector", lambda e, c0=c0: e.tensor_copy(Ii[:, 0:1024], af(src + c0, 1024)),
                  reads=rAs(src + c0, 1024), writes=[rI])
            P.add("vector", lambda e, c0=c0: e.tensor_copy(af(dst + c0, 1024), Ii[:, 0:1024]),
                  reads=[rI], writes=rAs(dst + c0, 1024))
        P.add("vector", lambda e: e.tensor_sub(af(tmp, 2048), af(dst, 2048), af(src, 2048)),
              reads=rAs(dst, 2048) + rAs(src, 2048), writes=rAs(tmp, 2048))
        P.add("vector", lambda e: e.scalar_tensor_tensor(af(dst, 2048), af(tmp, 2048), -0.5, af(dst, 2048),
                                                         ALU.is_lt, ALU.add),
              reads=rAs(tmp, 2048) + rAs(dst, 2048), writes=rAs(dst, 2048))
        P.add("vector", lambda e: e.tensor_scalar(af(tmp, 2048), af(tmp, 2048), 0.5, None, ALU.is_gt),
              reads=rAs(tmp, 2048), writes=rAs(tmp, 2048))
        P.add("vector", lambda e: e.tensor_sub(af(dst, 2048), af(dst, 2048), af(tmp, 2048)),
              reads=rAs(dst, 2048) + rAs(tmp, 2048), writes=rAs(dst, 2048))


    def emit_ln(xoff, boff, cbase, quant):
        cm = colsf[:, cbase + 0:cbase + 1]
        crs = colsf[:, cbase + 1:cbase + 2]
        cac = colsf[:, cbase + 2:cbase + 3]
        ctm = colsf[:, cbase + 3:cbase + 4]
        cc1 = colsf[:, cbase + 4:cbase + 5]
        cd1 = colsf[:, cbase + 5:cbase + 6]
        cmx = colsf[:, cbase + 6:cbase + 7]
        cmn = colsf[:, cbase + 7:cbase + 8]
        rc = rcols[cbase]
        x = af(xoff, 2048)
        rx = rAs(xoff, 2048)
        sq = af(xoff + 2048, 2048)
        rsq = rAs(xoff + 2048, 2048)
        P.add("scalar", lambda e: e.activation(sq, x, AF.Square, accum_out=cac), reads=rx, writes=rsq + [rc], small=True)
        P.add("vector", lambda e: e.tensor_reduce(cm, x, AX.X, ALU.add), reads=rx, writes=[rc], small=True)
        P.add("vector", lambda e: e.tensor_scalar(cm, cm, 1.0 / 2048.0, None, ALU.mult), reads=[rc], writes=[rc], small=True)
        P.add("vector", lambda e: e.tensor_scalar(cac, cac, 1.0 / 2048.0, None, ALU.mult), reads=[rc], writes=[rc], small=True)
        P.add("vector", lambda e: e.tensor_mul(ctm, cm, cm), reads=[rc], writes=[rc], small=True)
        P.add("vector", lambda e: e.tensor_sub(cac, cac, ctm), reads=[rc], writes=[rc], small=True)
        P.add("vector", lambda e: e.tensor_scalar(ctm, cac, 1.0, 1e-5, ALU.mult, ALU.add), reads=[rc], writes=[rc], small=True)
        P.add("scalar", lambda e: e.activation(ctm, ctm, AF.Sqrt), reads=[rc], writes=[rc], small=True)
        P.add("vector", lambda e: e.reciprocal(crs, ctm), reads=[rc], writes=[rc], small=True)
        if not quant:
            P.add("vector", lambda e: e.tensor_scalar(ab(boff, 2048), x, cm, crs, ALU.subtract, ALU.mult),
                  reads=rx + [rc], writes=rBs(boff, 2048), small=True)
            return
        P.add("vector", lambda e: e.tensor_mul(cc1, crs, crs), reads=[rc], writes=[rc], small=True)
        P.add("vector", lambda e: e.tensor_mul(cc1, cc1, cac), reads=[rc], writes=[rc], small=True)
        P.add("vector", lambda e: e.tensor_scalar(cc1, cc1, 2048.0, None, ALU.mult), reads=[rc], writes=[rc], small=True)
        P.add("scalar", lambda e: e.activation(cc1, cc1, AF.Sqrt), reads=[rc], writes=[rc], small=True)
        P.add("vector", lambda e: e.tensor_scalar(cc1, cc1, 1.0, 1e-12, ALU.mult, ALU.max), reads=[rc], writes=[rc], small=True)
        P.add("vector", lambda e: e.reciprocal(ctm, cc1), reads=[rc], writes=[rc], small=True)
        P.add("vector", lambda e: e.tensor_scalar(ctm, ctm, math.sqrt(2048.0), None, ALU.mult), reads=[rc], writes=[rc], small=True)
        P.add("vector", lambda e: e.tensor_reduce(cmx, x, AX.X, ALU.max), reads=rx, writes=[rc], small=True)
        P.add("vector", lambda e: e.tensor_reduce(cmn, x, AX.X, ALU.min), reads=rx, writes=[rc], small=True)
        P.add("vector", lambda e: e.tensor_sub(cmx, cmx, cm), reads=[rc], writes=[rc], small=True)
        P.add("vector", lambda e: e.tensor_sub(cmn, cm, cmn), reads=[rc], writes=[rc], small=True)
        P.add("vector", lambda e: e.tensor_max(cmx, cmx, cmn), reads=[rc], writes=[rc], small=True)
        P.add("vector", lambda e: e.tensor_mul(cmx, cmx, crs), reads=[rc], writes=[rc], small=True)
        P.add("vector", lambda e: e.tensor_mul(cmx, cmx, ctm), reads=[rc], writes=[rc], small=True)
        P.add("vector", lambda e: e.tensor_scalar(cmx, cmx, 1.0, 1e-5, ALU.mult, ALU.max), reads=[rc], writes=[rc], small=True)
        P.add("vector", lambda e: e.reciprocal(cmn, cmx), reads=[rc], writes=[rc], small=True)
        P.add("vector", lambda e: e.tensor_scalar(cmn, cmn, 127.0, None, ALU.mult), reads=[rc], writes=[rc], small=True)
        P.add("vector", lambda e: e.tensor_mul(cc1, ctm, cmn), reads=[rc], writes=[rc], small=True)
        P.add("vector", lambda e: e.tensor_mul(cc1, cc1, crs), reads=[rc], writes=[rc], small=True)
        P.add("vector", lambda e: e.tensor_scalar(cd1, cmx, 1.0 / 127.0, None, ALU.mult), reads=[rc], writes=[rc], small=True)
        P.add("vector", lambda e: e.tensor_mul(cd1, cd1, swc[:, 0:1]), reads=[rc, rswc], writes=[rc], small=True)
        t0 = sq
        P.add("vector", lambda e: e.tensor_scalar(t0, x, cm, cc1, ALU.subtract, ALU.mult),
              reads=rx + [rc], writes=rsq, small=True)
        QR, QT = 8192, 10240
        emit_round2048(xoff + 2048, QR, QT)
        P.add("vector", lambda e: e.tensor_copy(ab(boff, 2048), af(QR, 2048)),
              reads=rAs(QR, 2048), writes=rBs(boff, 2048))


    for nm, cst, dstt, rr in (("o128b", C_o128b, o128b, rconst), ("o128f", C_o128f, o128f, rconst),
                              ("or128b", C_or128b, or128b, rconst), ("or128f", C_or128f, or128f, rconst),
                              ("perm", C_perm, permM, rconst)):
        P.add("sync", lambda e, c=cst, d=dstt: e.dma_start(out=d[:, :], in_=c), writes=[rr], dma=True, g=G("once"))

    mats = [(Wq, "Wq"), (Wk, "Wk"), (Wv, "Wv"), (Wo, "Wo")]
    XT = [0, 4096]
    XTa = [4096, 8192]
    XNB = [54272, 56320]

    def a1a_tile(n):
        kc, j = n // 4, n % 4
        sl = n % 2
        P.add("sync", lambda e, kc=kc, j=j, sl=sl: e.dma_start(
            out=af(XTa[sl], 2048), in_=xk[512 * kc + 128 * j:512 * kc + 128 * j + 128, :]),
            writes=rAs(XTa[sl], 2048), dma=True, g=G("a2" if sl == 0 else "a3"))
        emit_ln(XTa[sl], XNB[sl], 8, False)
        P.add("gpsimd", lambda e, kc=kc, j=j, sl=sl: e.dma_start(
            out=xnS[512 * kc + 128 * j:512 * kc + 128 * j + 128, :], in_=ab(XNB[sl], 2048)),
            reads=rBs(XNB[sl], 2048), writes=[rxn[kc]], dma=True, g=G("xns%d" % kc))

    for i in range(64):
        m, nmw = mats[i // 16]
        et = i % 16
        fo = 2048 * (i % 2)
        bo_ = 2048 * (i % 2)
        P.add("sync", lambda e, m=m, et=et, fo=fo: e.dma_start(out=af(fo, 2048), in_=m[128 * et:128 * (et + 1), :]),
              writes=rAs(fo, 2048), dma=True, g=G("a%d" % (i % 2)))
        P.add("vector", lambda e, fo=fo, bo_=bo_: e.tensor_copy(ab(bo_, 2048), af(fo, 2048)),
              reads=rAs(fo, 2048), writes=rBs(bo_, 2048))
        P.add("gpsimd", lambda e, nmw=nmw, et=et, bo_=bo_: e.dma_start(
            out=WbS[nmw][128 * et:128 * (et + 1), :], in_=ab(bo_, 2048)),
            reads=rBs(bo_, 2048), writes=[rscr[nmw + "bS"]], dma=True, g=G("wbs" + nmw))
        if i % 2 == 1:
            a1a_tile(i // 2)
    P.add("sync", lambda e: e.dma_start(out=A[0:16, 4096:4224], in_=bq.rearrange("(a b) -> a b", b=128)),
          writes=rAs(4096, 128), dma=True, g=G("a2"))
    P.add("sync", lambda e: e.dma_start(out=A[16:32, 4096:4224], in_=bk.rearrange("(a b) -> a b", b=128)),
          writes=rAs(4096, 128), dma=True, g=G("a2"))
    P.add("vector", lambda e: e.tensor_copy(Bb[0:32, 4096:4224], A[0:32, 4096:4224]),
          reads=rAs(4096, 128), writes=rBs(4096, 128))
    P.add("gpsimd", lambda e: e.dma_start(out=bqkS, in_=Bb[0:32, 4096:4224]),
          reads=rBs(4096, 128), writes=[rscr["bqkS"]], dma=True, g=G("bqks"))
    P.add("sync", lambda e: e.dma_start(out=af(8192, 2048)[0:1, :], in_=bv.rearrange("(o a) -> o a", o=1)),
          writes=rAs(8192, 2048), dma=True, g=G("a3"))
    P.add("vector", lambda e: e.tensor_copy(bvrow[0:1, :], af(8192, 2048)[0:1, :]),
          reads=rAs(8192, 2048), writes=[rbv], small=True)
    P.add("sync", lambda e: e.dma_start(out=af(8192, 2048)[0:1, :], in_=bo.rearrange("(o a) -> o a", o=1)),
          writes=rAs(8192, 2048), dma=True, g=G("a3"))
    P.add("vector", lambda e: e.tensor_copy(borow[0:1, :], af(8192, 2048)[0:1, :]),
          reads=rAs(8192, 2048), writes=[rbo], small=True)
    P.add("sync", lambda e: e.dma_start(out=Bb[:, 4096:4128], in_=bqkS, transpose=True),
          reads=[rscr["bqkS"]], writes=rBs(4096, 64), dma=True, g=G("bqkt"))
    P.add("vector", lambda e: e.tensor_copy(bqkf[:, :], Bb[:, 4096:4128]),
          reads=rBs(4096, 64), writes=[rbqk], small=True)

    for dst, srcap, w, rr in ((cosb, C_cosk, S, rtab), (sinb, C_sink, S, rtab),
                              (cosbo, cosoI, OWN, rtab), (sinbo, sinoI, OWN, rtab)):
        for c0 in range(0, w, 2048):
            P.add("sync", lambda e, dst=dst, srcap=srcap, c0=c0: e.dma_start(
                out=af(0, 2048), in_=srcap[:, c0:c0 + 2048]),
                writes=rAs(0, 2048), dma=True, g=G("a0"))
            P.add("vector", lambda e, dst=dst, c0=c0: e.tensor_copy(dst[:, c0:c0 + 2048], af(0, 2048)),
                  reads=rAs(0, 2048), writes=[rr])

    def w_src(i):
        if i < 64:
            return W1[128 * i:128 * (i + 1), :]
        j = i - 64
        return W2[128 * (j // 4):128 * (j // 4 + 1), 2048 * (j % 4):2048 * (j % 4 + 1)]

    def w_dst(i):
        if i < 64:
            return W1qS[128 * i:128 * (i + 1), :]
        j = i - 64
        return W2qS[128 * (j // 4):128 * (j // 4 + 1), 2048 * (j % 4):2048 * (j % 4 + 1)]

    ACC = [colsf[:, 4:5], colsf[:, 5:6]]
    RCOL = colsf[:, 6:7]
    rACC = [rcols[4], rcols[5]]
    rRC = rcols[6]
    MSL = [0, 2048, 4096, 6144]
    MGRP = ["a0", "am1", "a2", "am3"]
    MENG = ["sync", "gpsimd", "sync", "gpsimd"]

    def mean_tile(i):
        s = i % 4
        P.add(MENG[s], lambda e, i=i, s=s: e.dma_start(out=af(MSL[s], 2048), in_=w_src(i)),
              writes=rAs(MSL[s], 2048), dma=True, g=G(MGRP[s]))
        P.add("vector", lambda e, s=s: e.tensor_reduce(RCOL, af(MSL[s], 2048), AX.X, ALU.add,
                                                       apply_absolute_value=True),
              reads=rAs(MSL[s], 2048), writes=[rRC], small=True)
        jj = 0 if i < 64 else 1
        if i in (0, 64):
            P.add("vector", lambda e, jj=jj: e.tensor_copy(ACC[jj], RCOL), reads=[rRC], writes=[rACC[jj]], small=True)
        else:
            P.add("vector", lambda e, jj=jj: e.tensor_add(ACC[jj], ACC[jj], RCOL), reads=[rRC], writes=[rACC[jj]], small=True)

    def swc_chain():
        for j in range(2):
            P.add("tensor", lambda e, j=j: e.matmul(PS[0][0:1, 0:1], o128f[:, :], ACC[j], start=True, stop=True),
                  reads=[rACC[j], rconst], writes=[rPS[0]])
            P.add("vector", lambda e, j=j: e.tensor_scalar(smallf[0:1, 2 * j:2 * j + 1], PS[0][0:1, 0:1],
                                                           1.0 / (INNER * E), 1e-5, ALU.mult, ALU.max),
                  reads=[rPS[0]], writes=[rsmall], small=True)
            P.add("vector", lambda e, j=j: e.reciprocal(smallf[0:1, 2 * j + 1:2 * j + 2], smallf[0:1, 2 * j:2 * j + 1]),
                  reads=[rsmall], writes=[rsmall], small=True)
            P.add("tensor", lambda e, j=j: e.matmul(PS[1][:, 0:2], or128f[:, :], smallf[0:1, 2 * j:2 * j + 2],
                                                    start=True, stop=True),
                  reads=[rsmall, rconst], writes=[rPS[1]])
            P.add("scalar", lambda e, j=j: e.activation(swc[:, 2 * j:2 * j + 2], PS[1][:, 0:2], AF.Copy),
                  reads=[rPS[1]], writes=[rswc])

    T0q, GQ, LQ = 8192, 10240, 12288
    QOB = [54272, 56320]

    def quant_tile(i):
        s = i % 4
        bo_ = QOB[i % 2]
        scol = swc[:, 1:2] if i < 64 else swc[:, 3:4]
        P.add(MENG[s], lambda e, i=i, s=s: e.dma_start(out=af(MSL[s], 2048), in_=w_src(i)),
              writes=rAs(MSL[s], 2048), dma=True, g=G(MGRP[s]))
        P.add("vector", lambda e, s=s, scol=scol: e.tensor_scalar(af(T0q, 2048), af(MSL[s], 2048), scol, None, ALU.mult),
              reads=rAs(MSL[s], 2048) + [rswc], writes=rAs(T0q, 2048))
        P.add("vector", lambda e: e.tensor_scalar(af(GQ, 2048), af(T0q, 2048), 0.5, None, ALU.is_gt),
              reads=rAs(T0q, 2048), writes=rAs(GQ, 2048))
        P.add("vector", lambda e: e.tensor_scalar(af(LQ, 2048), af(T0q, 2048), -0.5, None, ALU.is_lt),
              reads=rAs(T0q, 2048), writes=rAs(LQ, 2048))
        P.add("vector", lambda e, bo_=bo_: e.tensor_sub(ab(bo_, 2048), af(GQ, 2048), af(LQ, 2048)),
              reads=rAs(GQ, 2048) + rAs(LQ, 2048), writes=rBs(bo_, 2048))
        dn = "W1qS" if i < 64 else "W2qS"
        P.add("gpsimd", lambda e, i=i, bo_=bo_: e.dma_start(out=w_dst(i), in_=ab(bo_, 2048)),
              reads=rBs(bo_, 2048), writes=[rscr[dn]], dma=True, g=G(dn))

    def mk_feeder(fn, total):
        state = {"i": 0}

        def feed():
            if state["i"] < total:
                fn(state["i"])
                state["i"] += 1
        return feed

    WT0 = 0
    XF = [32768, 40960]
    KB0, PB0, RT0, RO0, VB0 = 49152, 50176, 51200, 52224, 53248

    def wTt(et):
        return Bb[:, WT0 + 2048 * et:WT0 + 2048 * et + 2048]

    rWTg = rBs(WT0, 32768)

    def proj(which, extra=None):
        srcS = xnS if which in ("K", "V") else xnoS
        nkc = 8 if which in ("K", "V") else 4
        wname = which if which != "K" else "Wk"
        wname = {"K": "Wk", "V": "Wv", "Q": "Wq"}[which]
        for et in range(ET):
            P.add("sync", lambda e, et=et, wname=wname: e.dma_start(
                out=wTt(et), in_=WbS[wname][:, 128 * et:128 * (et + 1)], transpose=True),
                reads=[rscr[wname + "bS"]], writes=rBs(WT0 + 2048 * et, 2048), dma=True, g=G("bwt"))
        for kc in range(nkc):
            sl = kc % 2
            rsrc = rxn[kc] if which in ("K", "V") else rscr["xnoS"]
            for et in range(ET):
                P.add("sync", lambda e, kc=kc, et=et, sl=sl, srcS=srcS: e.dma_start(
                    out=Bb[:, XF[sl] + 512 * et:XF[sl] + 512 * et + 512],
                    in_=srcS[512 * kc:512 * (kc + 1), 128 * et:128 * (et + 1)], transpose=True),
                    reads=[rsrc], writes=rBs(XF[sl] + 512 * et, 512), dma=True, g=G("xf%d" % sl))
            rxf = rBs(XF[sl], 8192)
            if which in ("K", "Q"):
                outS = KS if which == "K" else QS
                rout = rscr["KS" if which == "K" else "QS"]
                bcol = 16 if which == "K" else 0
                cosT = cosb if which == "K" else cosbo
                sinT = sinb if which == "K" else sinbo
                for h in range(H):
                    pk = rPS[h % 2]
                    def mmk(e, h=h, sl=sl):
                        r = None
                        for et in range(ET):
                            r = e.matmul(PS[h % 2][:, 0:512], wTt(et)[:, 128 * h:128 * (h + 1)],
                                         Bb[:, XF[sl] + 512 * et:XF[sl] + 512 * et + 512],
                                         start=(et == 0), stop=(et == ET - 1))
                        return r
                    P.add("tensor", mmk, reads=rxf + rWTg, writes=[pk])
                    kb = 512 * (h % 2)
                    P.add("scalar", lambda e, h=h, kb=kb, bcol=bcol: e.activation(
                        Bb[:, KB0 + kb:KB0 + kb + 512], PS[h % 2][:, 0:512], AF.Identity,
                        bias=bqkf[:, bcol + h:bcol + h + 1]),
                        reads=[pk, rbqk], writes=rBs(KB0 + kb, 512))
                    P.add("tensor", lambda e, h=h, kb=kb: e.matmul(
                        PS[2 + h % 2][:, 0:512], permM[:, :], Bb[:, KB0 + kb:KB0 + kb + 512], start=True, stop=True),
                        reads=rBs(KB0 + kb, 512) + [rconst], writes=[rPS[2 + h % 2]])
                    P.add("scalar", lambda e, h=h, kb=kb: e.activation(
                        Bb[:, PB0 + kb:PB0 + kb + 512], PS[2 + h % 2][:, 0:512], AF.Copy),
                        reads=[rPS[2 + h % 2]], writes=rBs(PB0 + kb, 512))
                    def rope(e, kc=kc, h=h, kb=kb, cosT=cosT, sinT=sinT, wq=(which == "Q")):
                        t1 = Bb[:, RT0:RT0 + 512]
                        t2 = Bb[:, RT0 + 512:RT0 + 1024]
                        e.tensor_mul(t1, Bb[:, KB0 + kb:KB0 + kb + 512], cosT[:, 512 * kc:512 * (kc + 1)])
                        e.tensor_mul(t2, Bb[:, PB0 + kb:PB0 + kb + 512], sinT[:, 512 * kc:512 * (kc + 1)])
                        if not wq:
                            return e.tensor_add(Bb[:, RO0 + kb:RO0 + kb + 512], t1, t2)
                        e.tensor_add(t1, t1, t2)
                        e.tensor_scalar(t2, t1, SC_LO, None, ALU.mult)
                        e.tensor_scalar(t1, t1, SC_HI, None, ALU.mult)
                        return e.tensor_add(Bb[:, RO0 + kb:RO0 + kb + 512], t1, t2)
                    P.add("vector", rope, reads=rBs(KB0 + kb, 512) + rBs(PB0 + kb, 512) + [rtab],
                          writes=rBs(RT0, 1024) + rBs(RO0 + kb, 512))
                    P.add("gpsimd", lambda e, kc=kc, h=h, kb=kb, outS=outS: e.dma_start(
                        out=outS[128 * h:128 * (h + 1), 512 * kc:512 * (kc + 1)], in_=Bb[:, RO0 + kb:RO0 + kb + 512]),
                        reads=rBs(RO0 + kb, 512), writes=[rout], dma=True, g=G("ks" if which == "K" else "qs"))
                    if extra:
                        extra()
            else:
                for g2 in range(16):
                    kt, fc = g2 // 4, g2 % 4
                    pv = rPS[4 + g2 % 2]
                    def mmv(e, kt=kt, fc=fc, g2=g2, sl=sl):
                        for et in range(ET):
                            e.matmul(PS[4 + g2 % 2][:, 0:512],
                                     Bb[:, XF[sl] + 512 * et + 128 * kt:XF[sl] + 512 * et + 128 * kt + 128],
                                     wTt(et)[:, 512 * fc:512 * (fc + 1)], start=(et == 0), stop=False)
                        return e.matmul(PS[4 + g2 % 2][:, 0:512], or128b[:, :], bvrow[0:1, 512 * fc:512 * (fc + 1)],
                                        start=False, stop=True)
                    P.add("tensor", mmv, reads=rxf + rWTg + [rconst, rbv], writes=[pv])
                    vb = 512 * (g2 % 2)
                    P.add("scalar", lambda e, g2=g2, vb=vb: e.activation(
                        Bb[:, VB0 + vb:VB0 + vb + 512], PS[4 + g2 % 2][:, 0:512], AF.Copy),
                        reads=[pv], writes=rBs(VB0 + vb, 512))
                    P.add("gpsimd", lambda e, kc=kc, kt=kt, fc=fc, vb=vb: e.dma_start(
                        out=VS[512 * kc + 128 * kt:512 * kc + 128 * kt + 128, 512 * fc:512 * (fc + 1)],
                        in_=Bb[:, VB0 + vb:VB0 + vb + 512]),
                        reads=rBs(VB0 + vb, 512), writes=[rscr["VS"]], dma=True, g=G("vs"))
                    if extra:
                        extra()

    proj("K", extra=mk_feeder(mean_tile, 128))
    swc_chain()
    proj("V", extra=mk_feeder(quant_tile, 128))
    for n in range(16):
        oc, j = n // 4, n % 4
        sl = n % 2
        P.add("sync", lambda e, oc=oc, j=j, sl=sl: e.dma_start(
            out=af(XT[sl], 2048), in_=xo[512 * oc + 128 * j:512 * oc + 128 * j + 128, :]),
            writes=rAs(XT[sl], 2048), dma=True, g=G("a0" if sl == 0 else "a2"))
        emit_ln(XT[sl], XNB[sl], 8, False)
        P.add("gpsimd", lambda e, oc=oc, j=j, sl=sl: e.dma_start(
            out=xnoS[512 * oc + 128 * j:512 * oc + 128 * j + 128, :], in_=ab(XNB[sl], 2048)),
            reads=rBs(XNB[sl], 2048), writes=[rscr["xnoS"]], dma=True, g=G("xnos"))
    proj("Q")

    KH0, VH0, QH0, MSK0, KSQ0, PB_, ONB, QSQ0 = 0, 8192, 16384, 20480, 28672, 32768, 49152, 49408
    SST = 4096
    P.add("sync", lambda e: e.dma_start(out=Bb[:, MSK0:MSK0 + 8192],
                                        in_=maskin.rearrange("(t p) q -> p t q", p=128)),
          writes=rBs(MSK0, 8192), dma=True, g=G("msk"))
    for h in range(H):
        hs = h % 2
        rKH = rBs(KH0 + 4096 * hs, 4096)
        rVH = rBs(VH0 + 4096 * hs, 4096)
        rQH = rBs(QH0 + 2048 * hs, 2048)
        P.add("sync", lambda e, h=h, hs=hs: e.dma_start(
            out=Bb[:, KH0 + 4096 * hs:KH0 + 4096 * hs + 4096], in_=KS[128 * h:128 * (h + 1), :]),
            reads=[rscr["KS"]], writes=rKH, dma=True, g=G("kh%d" % hs))
        for part in range(4):
            P.add("sync", lambda e, h=h, hs=hs, part=part: e.dma_start(
                out=Bb[:, VH0 + 4096 * hs + 1024 * part:VH0 + 4096 * hs + 1024 * part + 1024],
                in_=VS[1024 * part:1024 * (part + 1), 128 * h:128 * (h + 1)].rearrange("(kt p) d -> p kt d", p=128)),
                reads=[rscr["VS"]], writes=rBs(VH0 + 4096 * hs + 1024 * part, 1024), dma=True, g=G("vh%d" % hs))
        P.add("sync", lambda e, h=h, hs=hs: e.dma_start(
            out=Bb[:, QH0 + 2048 * hs:QH0 + 2048 * hs + 2048], in_=QS[128 * h:128 * (h + 1), :]),
            reads=[rscr["QS"]], writes=rQH, dma=True, g=G("qh%d" % hs))
        P.add("scalar", lambda e, hs=hs: e.activation(Bb[:, KSQ0:KSQ0 + 4096],
                                                      Bb[:, KH0 + 4096 * hs:KH0 + 4096 * hs + 4096], AF.Square),
              reads=rKH, writes=rBs(KSQ0, 4096))
        for sl8 in range(8):
            P.add("tensor", lambda e, sl8=sl8: e.matmul(PS[6][0:1, 0:512], o128b[:, :],
                                                        Bb[:, KSQ0 + 512 * sl8:KSQ0 + 512 * sl8 + 512],
                                                        start=True, stop=True),
                  reads=rBs(KSQ0 + 512 * sl8, 512) + [rconst], writes=[rPS[6]])
            P.add("scalar", lambda e, sl8=sl8: e.activation(af(1024 + 0, 512)[0:1, 512 * 0:512] if False else
                                                            A[0:1, 2048 + 512 * (sl8 % 4):2048 + 512 * (sl8 % 4) + 512],
                                                            PS[6][0:1, 0:512], AF.Copy),
                  reads=[rPS[6]], writes=rAs(2048 + 512 * (sl8 % 4), 512), small=True)
            if sl8 % 4 == 3:
                P.add("vector", lambda e, sl8=sl8: e.tensor_reduce(smallf[0:1, 8 + sl8 // 4:9 + sl8 // 4],
                                                                   A[0:1, 2048:2048 + 2048], AX.X, ALU.max),
                      reads=rAs(2048, 2048), writes=[rsmall], small=True)
        P.add("vector", lambda e: e.tensor_max(smallf[0:1, 10:11], smallf[0:1, 8:9], smallf[0:1, 9:10]),
              reads=[rsmall], writes=[rsmall], small=True)
        P.add("vector", lambda e: e.tensor_scalar(smallf[0:1, 10:11], smallf[0:1, 10:11], 1.0, 1e-35, ALU.mult, ALU.add),
              reads=[rsmall], writes=[rsmall], small=True)
        P.add("scalar", lambda e: e.activation(smallf[0:1, 11:12], smallf[0:1, 10:11], AF.Ln),
              reads=[rsmall], writes=[rsmall], small=True)
        P.add("scalar", lambda e, hs=hs: e.activation(Bb[:, KSQ0:KSQ0 + 2048],
                                                      Bb[:, QH0 + 2048 * hs:QH0 + 2048 * hs + 2048], AF.Square),
              reads=rQH, writes=rBs(KSQ0, 2048))
        for j4 in range(4):
            P.add("tensor", lambda e, j4=j4: e.matmul(PS[6][0:1, 0:512], o128b[:, :],
                                                      Bb[:, KSQ0 + 512 * j4:KSQ0 + 512 * j4 + 512],
                                                      start=True, stop=True),
                  reads=rBs(KSQ0 + 512 * j4, 512) + [rconst], writes=[rPS[6]])
            P.add("vector", lambda e, j4=j4: e.tensor_scalar(A[0:1, 12288 + 512 * j4:12800 + 512 * j4],
                                                             PS[6][0:1, 0:512], 1.0, 1e-35, ALU.mult, ALU.add),
                  reads=[rPS[6]], writes=rAs(12288 + 512 * j4, 512), small=True)
        P.add("scalar", lambda e: e.activation(A[0:1, 12288:14336], A[0:1, 12288:14336], AF.Ln),
              reads=rAs(12288, 2048), writes=rAs(12288, 2048), small=True)
        P.add("vector", lambda e: e.tensor_scalar(A[0:1, 12288:14336], A[0:1, 12288:14336],
                                                  smallf[0:1, 11:12], None, ALU.add),
              reads=rAs(12288, 2048) + [rsmall], writes=rAs(12288, 2048), small=True)
        P.add("vector", lambda e: e.tensor_scalar(A[0:1, 12288:14336], A[0:1, 12288:14336], 0.5, 0.02,
                                                  ALU.mult, ALU.add),
              reads=rAs(12288, 2048), writes=rAs(12288, 2048), small=True)
        P.add("scalar", lambda e: e.activation(A[0:1, 12288:14336], A[0:1, 12288:14336], AF.Exp),
              reads=rAs(12288, 2048), writes=rAs(12288, 2048), small=True)
        P.add("vector", lambda e: e.tensor_scalar(A[0:1, 12288:14336], A[0:1, 12288:14336], -1.0, EXPC,
                                                  ALU.mult, ALU.add),
              reads=rAs(12288, 2048), writes=rAs(12288, 2048), small=True)
        def a2_phase1(qg):
            nkt = 4 * (qg + 1)
            gsl = qg % 2
            P.add("tensor", lambda e, qg=qg: e.matmul(PS[7][:, 0:256], or128f[:, :],
                                                      A[0:1, 12288 + 256 * qg:12544 + 256 * qg], start=True, stop=True),
                  reads=rAs(12288 + 256 * qg, 256) + [rconst], writes=[rPS[7]])
            P.add("scalar", lambda e, gsl=gsl: e.activation(af(256 * gsl, 256), PS[7][:, 0:256], AF.Copy),
                  reads=[rPS[7]], writes=rAs(256 * gsl, 256))
            for kt in range(nkt):
                ks = kt % 2
                P.add("tensor", lambda e, hs=hs, kt=kt, ks=ks, qg=qg: e.matmul(
                    PS[ks][:, 0:256], Bb[:, KH0 + 4096 * hs + 128 * kt:KH0 + 4096 * hs + 128 * kt + 128],
                    Bb[:, QH0 + 2048 * hs + 256 * qg:QH0 + 2048 * hs + 256 * qg + 256], start=True, stop=True),
                    reads=rKH + rQH, writes=[rPS[ks]])
                P.add("vector", lambda e, ks=ks, gsl=gsl, kt=kt: e.tensor_add(
                    af(SST + 256 * kt, 256), PS[ks][:, 0:256], af(256 * gsl, 256)),
                    reads=[rPS[ks]] + rAs(256 * gsl, 256), writes=rAs(SST + 256 * kt, 256))
            pso = PB_ + 8192 * gsl
            w1 = 256 * (nkt - 4)
            if w1 > 0:
                P.add("scalar", lambda e, pso=pso, w1=w1: e.activation(
                    Bb[:, pso:pso + w1], af(SST, w1), AF.Exp),
                    reads=rAs(SST, w1), writes=rBs(pso, w1))
            P.add("scalar", lambda e, pso=pso, w1=w1: e.activation(
                Bb[:, pso + w1:pso + w1 + 1024], af(SST + w1, 1024), AF.Exp),
                reads=rAs(SST + w1, 1024), writes=rBs(pso + w1, 1024))
            kt0 = nkt - 4
            P.add("vector", lambda e, kt0=kt0, qg=qg, pso=pso: e.tensor_mul(
                Bb[:, pso + 256 * kt0:pso + 256 * kt0 + 1024], Bb[:, pso + 256 * kt0:pso + 256 * kt0 + 1024],
                Bb[:, MSK0 + 1024 * qg:MSK0 + 1024 * qg + 1024]),
                reads=rBs(pso + 256 * kt0, 1024) + rBs(MSK0 + 1024 * qg, 1024), writes=rBs(pso + 256 * kt0, 1024))

        def a2_phase2(qg):
            nkt = 4 * (qg + 1)
            gsl = qg % 2
            pso = PB_ + 8192 * gsl
            for kt in range(nkt):
                P.add("tensor", lambda e, hs=hs, kt=kt, nkt=nkt, pso=pso: e.matmul(
                    PS[2][:, 0:256], Bb[:, VH0 + 4096 * hs + 128 * kt:VH0 + 4096 * hs + 128 * kt + 128],
                    Bb[:, pso + 256 * kt:pso + 256 * kt + 256], start=(kt == 0), stop=(kt == nkt - 1)),
                    reads=rBs(pso + 256 * kt, 256) + rVH, writes=[rPS[2]])
                P.add("tensor", lambda e, kt=kt, nkt=nkt, pso=pso: e.matmul(
                    PS[3][0:1, 0:256], o128b[:, :], Bb[:, pso + 256 * kt:pso + 256 * kt + 256],
                    start=(kt == 0), stop=(kt == nkt - 1)),
                    reads=rBs(pso + 256 * kt, 256) + [rconst], writes=[rPS[3]])
            P.add("vector", lambda e: e.tensor_scalar(A[0:1, 1536:1792], PS[3][0:1, 0:256], 1.0, 1e-30,
                                                      ALU.mult, ALU.add),
                  reads=[rPS[3]], writes=rAs(1536, 256), small=True)
            P.add("vector", lambda e: e.reciprocal(A[0:1, 1536:1792], A[0:1, 1536:1792]),
                  reads=rAs(1536, 256), writes=rAs(1536, 256), small=True)
            P.add("tensor", lambda e: e.matmul(PS[7][:, 0:256], or128f[:, :], A[0:1, 1536:1792], start=True, stop=True),
                  reads=rAs(1536, 256) + [rconst], writes=[rPS[7]])
            P.add("scalar", lambda e: e.activation(af(1792, 256), PS[7][:, 0:256], AF.Copy),
                  reads=[rPS[7]], writes=rAs(1792, 256))
            P.add("vector", lambda e: e.tensor_mul(Bb[:, ONB:ONB + 256], PS[2][:, 0:256], af(1792, 256)),
                  reads=[rPS[2]] + rAs(1792, 256), writes=rBs(ONB, 256))
            P.add("gpsimd", lambda e, h=h, qg=qg: e.dma_start(
                out=OS[128 * h:128 * (h + 1), 256 * qg:256 * (qg + 1)], in_=Bb[:, ONB:ONB + 256]),
                reads=rBs(ONB, 256), writes=[rscr["OS"]], dma=True, g=G("os"))

        for qg in range(NQG):
            a2_phase1(qg)
            if qg > 0:
                a2_phase2(qg - 1)
        a2_phase2(NQG - 1)

    QA1F0, W1T0, GH0, OFM0, WOT0 = 0, 16384, 32768, 33792, 50176
    for ck in range(NCK):
        for j in range(8):
            sl = j % 2
            row0 = CK * ck + 128 * j
            P.add("sync", lambda e, row0=row0, sl=sl: e.dma_start(out=af(XT[sl], 2048), in_=xo[row0:row0 + 128, :]),
                  writes=rAs(XT[sl], 2048), dma=True, g=G("a0" if sl == 0 else "a2"))
            emit_ln(XT[sl], XNB[sl], 16 + 8 * sl, True)
            P.add("vector", lambda e, j=j, sl=sl: e.tensor_copy(colsf[:, 32 + j:33 + j],
                                                                colsf[:, 16 + 8 * sl + 5:16 + 8 * sl + 6]),
                  reads=[rcols[16 + 8 * sl]], writes=[rcols[32 + j]], small=True)
            P.add("gpsimd", lambda e, j=j, sl=sl: e.dma_start(out=QA1S[128 * j:128 * (j + 1), :], in_=ab(XNB[sl], 2048)),
                  reads=rBs(XNB[sl], 2048), writes=[rscr["QA1S"]], dma=True, g=G("qa1s"))
        for et in range(ET):
            P.add("sync", lambda e, et=et: e.dma_start(out=Bb[:, QA1F0 + 1024 * et:QA1F0 + 1024 * et + 1024],
                                                       in_=QA1S[:, 128 * et:128 * (et + 1)], transpose=True),
                  reads=[rscr["QA1S"]], writes=rBs(QA1F0 + 1024 * et, 1024), dma=True, g=G("qa1f"))
        rQA1F = rBs(QA1F0, 16384)
        for og in range(16):
            osl = og % 2
            rW1T = rBs(W1T0 + 8192 * osl, 8192)
            for et in range(ET):
                P.add("sync", lambda e, og=og, et=et, osl=osl: e.dma_start(
                    out=Bb[:, W1T0 + 8192 * osl + 512 * et:W1T0 + 8192 * osl + 512 * et + 512],
                    in_=W1qS[512 * og:512 * (og + 1), 128 * et:128 * (et + 1)], transpose=True),
                    reads=[rscr["W1qS"]], writes=rBs(W1T0 + 8192 * osl + 512 * et, 512), dma=True, g=G("w1t%d" % osl))
            for tt in range(8):
                pb = rPS[tt % 2]
                def mm1(e, tt=tt, osl=osl):
                    r = None
                    for et in range(ET):
                        r = e.matmul(PS[tt % 2][:, 0:512],
                                     Bb[:, QA1F0 + 1024 * et + 128 * tt:QA1F0 + 1024 * et + 128 * tt + 128],
                                     Bb[:, W1T0 + 8192 * osl + 512 * et:W1T0 + 8192 * osl + 512 * et + 512],
                                     start=(et == 0), stop=(et == ET - 1))
                    return r
                P.add("tensor", mm1, reads=rQA1F + rW1T, writes=[pb])
                gb = 13312 + 512 * ((og * 8 + tt) % 2)
                P.add("scalar", lambda e, tt=tt, gb=gb: e.activation(
                    af(gb, 512), PS[tt % 2][:, 0:512], AF.Gelu, scale=colsf[:, 32 + tt:33 + tt]),
                    reads=[pb, rcols[32 + tt]], writes=rAs(gb, 512), small=True)
                P.add("scalar", lambda e, gb=gb: e.activation(af(12288, 512), af(gb, 512),
                                                              AF.Square, accum_out=colsf[:, 56:57]),
                      reads=rAs(gb, 512), writes=rAs(12288, 512) + [rcols[56]], small=True)
                if og == 0:
                    P.add("vector", lambda e, tt=tt: e.tensor_copy(colsf[:, 40 + tt:41 + tt], colsf[:, 56:57]),
                          reads=[rcols[56]], writes=[rcols[40 + tt]], small=True)
                else:
                    P.add("vector", lambda e, tt=tt: e.tensor_add(colsf[:, 40 + tt:41 + tt],
                                                                  colsf[:, 40 + tt:41 + tt], colsf[:, 56:57]),
                          reads=[rcols[56]], writes=[rcols[40 + tt]], small=True)
                P.add("vector", lambda e, gb=gb: e.tensor_reduce(colsf[:, 57:58], af(gb, 512),
                                                                 AX.X, ALU.max, apply_absolute_value=True),
                      reads=rAs(gb, 512), writes=[rcols[57]], small=True)
                if og == 0:
                    P.add("vector", lambda e, tt=tt: e.tensor_copy(colsf[:, 48 + tt:49 + tt], colsf[:, 57:58]),
                          reads=[rcols[57]], writes=[rcols[48 + tt]], small=True)
                else:
                    P.add("vector", lambda e, tt=tt: e.tensor_max(colsf[:, 48 + tt:49 + tt],
                                                                  colsf[:, 48 + tt:49 + tt], colsf[:, 57:58]),
                          reads=[rcols[57]], writes=[rcols[48 + tt]], small=True)
                P.add("gpsimd", lambda e, og=og, tt=tt, gb=gb: e.dma_start(
                    out=ghS[128 * tt:128 * (tt + 1), 512 * og:512 * (og + 1)], in_=af(gb, 512)),
                    reads=rAs(gb, 512), writes=[rscr["ghS"]], dma=True, g=G("ghs"))
        for tt in range(8):
            gs = colsf[:, 40 + tt:41 + tt]
            gm = colsf[:, 48 + tt:49 + tt]
            rgs, rgm = rcols[40 + tt], rcols[48 + tt]
            P.add("scalar", lambda e, gs=gs: e.activation(gs, gs, AF.Sqrt), reads=[rgs], writes=[rgs], small=True)
            P.add("vector", lambda e, gs=gs: e.tensor_scalar(gs, gs, 1.0, 1e-12, ALU.mult, ALU.max),
                  reads=[rgs], writes=[rgs], small=True)
            P.add("vector", lambda e, gs=gs: e.reciprocal(gs, gs), reads=[rgs], writes=[rgs], small=True)
            P.add("vector", lambda e, gs=gs: e.tensor_scalar(colsf[:, 59:60], gs, SQB_LO, None, ALU.mult),
                  reads=[rgs], writes=[rcols[59]], small=True)
            P.add("vector", lambda e, gs=gs: e.tensor_scalar(gs, gs, SQB_HI, None, ALU.mult),
                  reads=[rgs], writes=[rgs], small=True)
            P.add("vector", lambda e, gs=gs: e.tensor_add(gs, gs, colsf[:, 59:60]),
                  reads=[rgs, rcols[59]], writes=[rgs], small=True)
            P.add("vector", lambda e, gs=gs, gm=gm: e.tensor_mul(gm, gm, gs), reads=[rgs], writes=[rgm], small=True)
            P.add("vector", lambda e, gm=gm: e.tensor_scalar(gm, gm, 1.0, 1e-5, ALU.mult, ALU.max),
                  reads=[rgm], writes=[rgm], small=True)
            P.add("vector", lambda e, gm=gm, tt=tt: e.tensor_scalar(colsf[:, 64 + tt:65 + tt], gm,
                                                                    I127_HI, None, ALU.mult),
                  reads=[rgm], writes=[rcols[64 + tt]], small=True)
            P.add("vector", lambda e, gm=gm: e.tensor_scalar(colsf[:, 59:60], gm, I127_LO, None, ALU.mult),
                  reads=[rgm], writes=[rcols[59]], small=True)
            P.add("vector", lambda e, tt=tt: e.tensor_add(colsf[:, 64 + tt:65 + tt], colsf[:, 64 + tt:65 + tt],
                                                          colsf[:, 59:60]),
                  reads=[rcols[59]], writes=[rcols[64 + tt]], small=True)
            P.add("vector", lambda e, tt=tt: e.tensor_mul(colsf[:, 64 + tt:65 + tt], colsf[:, 64 + tt:65 + tt],
                                                          swc[:, 2:3]),
                  reads=[rswc], writes=[rcols[64 + tt]], small=True)
            P.add("vector", lambda e, gm=gm: e.reciprocal(gm, gm), reads=[rgm], writes=[rgm], small=True)
            P.add("vector", lambda e, gm=gm: e.tensor_scalar(gm, gm, 127.0, None, ALU.mult), reads=[rgm], writes=[rgm], small=True)
            P.add("vector", lambda e, gs=gs, gm=gm: e.tensor_mul(gs, gs, gm), reads=[rgm], writes=[rgs], small=True)
        for og in range(16):
            for tt in range(8):
                gb = 512 * ((og * 8 + tt) % 8)
                gbo = 512 * ((og * 8 + tt) % 2)
                P.add("sync", lambda e, og=og, tt=tt, gb=gb: e.dma_start(
                    out=af(gb, 512), in_=ghS[128 * tt:128 * (tt + 1), 512 * og:512 * (og + 1)]),
                    reads=[rscr["ghS"]], writes=rAs(gb, 512), dma=True, g=G("ghb%d" % (gb // 512)))
                P.add("vector", lambda e, tt=tt, gb=gb: e.tensor_scalar(af(8192, 512), af(gb, 512),
                                                                        colsf[:, 40 + tt:41 + tt], None, ALU.mult),
                      reads=rAs(gb, 512) + [rcols[40 + tt]], writes=rAs(8192, 512), small=True)
                P.add("vector", lambda e: e.tensor_copy(Ii[:, 0:512], af(8192, 512)), reads=rAs(8192, 512), writes=[rI])
                P.add("vector", lambda e: e.tensor_copy(af(8704, 512), Ii[:, 0:512]), reads=[rI], writes=rAs(8704, 512))
                P.add("vector", lambda e: e.tensor_sub(af(9216, 512), af(8704, 512), af(8192, 512)),
                      reads=rAs(8704, 512) + rAs(8192, 512), writes=rAs(9216, 512))
                P.add("vector", lambda e: e.scalar_tensor_tensor(af(8704, 512), af(9216, 512), -0.5, af(8704, 512),
                                                                 ALU.is_lt, ALU.add),
                      reads=rAs(9216, 512) + rAs(8704, 512), writes=rAs(8704, 512))
                P.add("vector", lambda e: e.tensor_scalar(af(9216, 512), af(9216, 512), 0.5, None, ALU.is_gt),
                      reads=rAs(9216, 512), writes=rAs(9216, 512))
                P.add("vector", lambda e, gbo=gbo: e.tensor_sub(Bb[:, GH0 + gbo:GH0 + gbo + 512], af(8704, 512), af(9216, 512)),
                      reads=rAs(8704, 512) + rAs(9216, 512), writes=rBs(GH0 + gbo, 512))
                P.add("gpsimd", lambda e, og=og, tt=tt, gbo=gbo: e.dma_start(
                    out=QA2S[128 * tt:128 * (tt + 1), 512 * og:512 * (og + 1)], in_=Bb[:, GH0 + gbo:GH0 + gbo + 512]),
                    reads=rBs(GH0 + gbo, 512), writes=[rscr["QA2S"]], dma=True, g=G("qa2s"))
        for et in range(ET):
            P.add("sync", lambda e, ck=ck, et=et: e.dma_start(
                out=Bb[:, OFM0 + 1024 * et:OFM0 + 1024 * et + 1024],
                in_=OS[128 * et:128 * (et + 1), CK * ck:CK * (ck + 1)]),
                reads=[rscr["OS"]], writes=rBs(OFM0 + 1024 * et, 1024), dma=True, g=G("ofm"))
        rOFM = rBs(OFM0, 16384)
        for fc in range(4):
            for ot in range(OTN):
                osl = ot % 4
                q2o = W1T0 + 1024 * osl
                w2o = W1T0 + 8192 + 512 * osl
                rq2 = rBs(q2o, 1024)
                rw2 = rBs(w2o, 512)
                P.add("sync", lambda e, ot=ot, q2o=q2o: e.dma_start(
                    out=Bb[:, q2o:q2o + 1024],
                    in_=QA2S[:, 128 * ot:128 * (ot + 1)], transpose=True),
                    reads=[rscr["QA2S"]], writes=rq2, dma=True, g=G("q2f%d" % osl))
                P.add("sync", lambda e, fc=fc, ot=ot, w2o=w2o: e.dma_start(
                    out=Bb[:, w2o:w2o + 512],
                    in_=W2qS[512 * fc:512 * (fc + 1), 128 * ot:128 * (ot + 1)], transpose=True),
                    reads=[rscr["W2qS"]], writes=rw2, dma=True, g=G("w2b%d" % osl))
                def mm2(e, ot=ot, q2o=q2o, w2o=w2o):
                    r = None
                    for tt in range(8):
                        r = e.matmul(PS[tt][:, 0:512],
                                     Bb[:, q2o + 128 * tt:q2o + 128 * tt + 128],
                                     Bb[:, w2o:w2o + 512],
                                     start=(ot == 0), stop=(ot == OTN - 1))
                    return r
                P.add("tensor", mm2, reads=rq2 + rw2, writes=rPS)
            for tt in range(8):
                P.add("vector", lambda e, tt=tt: e.tensor_scalar(af(8192 + 512 * tt // 1, 512) if False else
                                                                 A[:, 8192 + 512 * (tt % 8):8192 + 512 * (tt % 8) + 512],
                                                                 PS[tt][:, 0:512], colsf[:, 64 + tt:65 + tt], None, ALU.mult),
                      reads=[rPS[tt], rcols[64 + tt]], writes=rAs(8192 + 512 * tt, 512), small=True)
            for et in range(ET):
                P.add("sync", lambda e, fc=fc, et=et: e.dma_start(
                    out=Bb[:, WOT0 + 512 * et:WOT0 + 512 * et + 512],
                    in_=WbS["Wo"][512 * fc:512 * (fc + 1), 128 * et:128 * (et + 1)], transpose=True),
                    reads=[rscr["WobS"]], writes=rBs(WOT0 + 512 * et, 512), dma=True, g=G("wot"))
            rWOT = rBs(WOT0, 8192)
            for tt in range(8):
                pa = rPS[tt]
                def mmA(e, tt=tt, fc=fc):
                    for et in range(ET):
                        e.matmul(PS[tt][:, 0:512],
                                 Bb[:, OFM0 + 1024 * et + 128 * tt:OFM0 + 1024 * et + 128 * tt + 128],
                                 Bb[:, WOT0 + 512 * et:WOT0 + 512 * et + 512], start=(et == 0), stop=False)
                    return e.matmul(PS[tt][:, 0:512], or128b[:, :], borow[0:1, 512 * fc:512 * (fc + 1)],
                                    start=False, stop=True)
                P.add("tensor", mmA, reads=rOFM + rWOT + [rconst, rbo] + rAs(8192 + 512 * tt, 512), writes=[pa])
                xb = 12288 + 512 * (tt % 2)
                P.add("sync", lambda e, ck=ck, tt=tt, fc=fc, xb=xb: e.dma_start(
                    out=af(xb, 512), in_=xo[CK * ck + 128 * tt:CK * ck + 128 * tt + 128, 512 * fc:512 * (fc + 1)]),
                    writes=rAs(xb, 512), dma=True, g=G("xs%d" % (tt % 2)))
                P.add("vector", lambda e, tt=tt, xb=xb: e.tensor_add(af(xb, 512), af(xb, 512),
                                                                     A[:, 8192 + 512 * tt:8192 + 512 * tt + 512]),
                      reads=rAs(8192 + 512 * tt, 512) + rAs(xb, 512), writes=rAs(xb, 512))
                P.add("vector", lambda e, tt=tt, xb=xb: e.tensor_add(af(xb, 512), af(xb, 512), PS[tt][:, 0:512]),
                      reads=[pa] + rAs(xb, 512), writes=rAs(xb, 512))
                o = P.add("gpsimd", lambda e, ck=ck, tt=tt, fc=fc, xb=xb: e.dma_start(
                    out=out[CK * ck + 128 * tt:CK * ck + 128 * tt + 128, 512 * fc:512 * (fc + 1)], in_=af(xb, 512)),
                    reads=rAs(xb, 512), dma=True, g=G("gout"))
                OUT_OPS.append(o)

    P.add("sync", lambda e: e.nop(), deps=OUT_OPS)

    if DEBUG_DUMP:
        dbg_list = [("xnS", xnS, [S, E]), ("KS", KS, [E, S]), ("VS", VS, [S, E]),
                    ("QS", QS, [E, OWN]), ("OS", OS, [E, OWN]),
                    ("W1qS", W1qS, [INNER, E]), ("ghS", ghS, [CK, INNER]), ("QA2S", QA2S, [CK, INNER]),
                    ("QA1S", QA1S, [CK, E])]
        for nm, ap, shp in dbg_list:
            dout = nc.dram_tensor("dbg_" + nm, shp, BF16, kind="ExternalOutput").ap()
            P.add("gpsimd", lambda e, ap=ap, dout=dout: e.dma_start(out=dout, in_=ap),
                  reads=[rscr[nm] if nm in rscr else rxn[7]], deps=OUT_OPS, dma=True, g=G("gdbg"))
        dsw = nc.dram_tensor("dbg_swc", [128, 4], F32, kind="ExternalOutput").ap()
        P.add("gpsimd", lambda e: e.dma_start(out=dsw, in_=swc[:, :]), reads=[rswc], deps=OUT_OPS, dma=True, g=G("gdbg"))
        fin = [op for op in P.ops["gpsimd"] if op.grp is not None and op.grp.name == "gdbg"]
        P.add("sync", lambda e: e.nop(), deps=fin)

    P.finalize()
    sems = {e: nc.alloc_semaphore("sem_" + e) for e in ROLES}
    for gg in GR.values():
        gg.sem = nc.alloc_semaphore("g_" + gg.name)
    with nc.Block() as block:
        @block.sync
        def _(eng):
            P.emit("sync", eng, sems)

        @block.tensor
        def _(eng):
            P.emit("tensor", eng, sems)

        @block.vector
        def _(eng):
            P.emit("vector", eng, sems)

        @block.scalar
        def _(eng):
            P.emit("scalar", eng, sems)

        @block.gpsimd
        def _(eng):
            P.emit("gpsimd", eng, sems)
    return nc


_NC = None


def _get_nc():
    global _NC
    if _NC is None:
        _NC = build()
    return _NC


def _pack_inputs(inputs):
    x = np.asarray(inputs["L0_attn_x"], dtype=np.float32)
    masks = {}
    posos = {}
    cosos = {}
    sinos = {}
    for g in range(2):
        m = np.zeros((KTN, 128, QG), dtype=np.float32)
        for qg in range(NQG):
            nkt = 4 * (qg + 1)
            for slot in range(4):
                kt = nkt - 4 + slot
                for lt in range(2):
                    gt = 4 * qg + 2 * lt + g
                    qglob = 128 * gt + np.arange(128)
                    kglob = 128 * kt + np.arange(128)
                    m[4 * qg + slot, :, 128 * lt:128 * (lt + 1)] = (kglob[:, None] <= qglob[None, :])
        masks[g] = np.ascontiguousarray(m.reshape(KTN * 128, QG).astype(BF))
        own_tiles = [2 * i + g for i in range(NT)]
        pos = np.concatenate([128 * t + np.arange(128) for t in own_tiles]).astype(np.float32)
        posos[g] = np.ascontiguousarray(np.tile(pos, (128, 1)))
        cos32, sinF32 = _rope_tables_f32()
        idx = pos.astype(np.int64)
        cosos[g] = np.ascontiguousarray(cos32[idx].T)
        sinos[g] = np.ascontiguousarray(sinF32[idx].T)
    base = {
        "Wq": np.ascontiguousarray(np.asarray(inputs["L0_attn_Wq"], dtype=np.float32)),
        "Wk": np.ascontiguousarray(np.asarray(inputs["L0_attn_Wk"], dtype=np.float32)),
        "Wv": np.ascontiguousarray(np.asarray(inputs["L0_attn_Wv"], dtype=np.float32)),
        "Wo": np.ascontiguousarray(np.asarray(inputs["L0_attn_Wo"], dtype=np.float32)),
        "bq": np.ascontiguousarray(np.asarray(inputs["L0_attn_bq"], dtype=np.float32)),
        "bk": np.ascontiguousarray(np.asarray(inputs["L0_attn_bk"], dtype=np.float32)),
        "bv": np.ascontiguousarray(np.asarray(inputs["L0_attn_bv"], dtype=np.float32)),
        "bo": np.ascontiguousarray(np.asarray(inputs["L0_attn_bo"], dtype=np.float32)),
        "W1": np.ascontiguousarray(np.asarray(inputs["L0_ffn_w1"], dtype=np.float32)),
        "W2": np.ascontiguousarray(np.asarray(inputs["L0_ffn_w2"], dtype=np.float32)),
    }
    in_maps = []
    for c in range(8):
        b, g = c // 2, c % 2
        own_tiles = [2 * i + g for i in range(NT)]
        xb = np.ascontiguousarray(x[b])
        xown = np.ascontiguousarray(xb.reshape(32, 128, E)[own_tiles].reshape(OWN, E))
        d = dict(base)
        d.update({"xk": xb, "xo": xown, "maskin": masks[g], "poso": posos[g],
                  "cosoI": cosos[g], "sinoI": sinos[g]})
        in_maps.append(d)
    return in_maps


def kernel(**inputs):
    nc = _get_nc()
    in_maps = _pack_inputs(inputs)
    res = run_bass_kernel_spmd(nc, in_maps, core_ids=list(range(8)))
    outp = np.zeros((B_, S, E), dtype=np.float32)
    for c in range(8):
        b, g = c // 2, c % 2
        own_tiles = [2 * i + g for i in range(NT)]
        o = np.asarray(res.results[c]["out"], dtype=np.float32).reshape(NT, 128, E)
        outp.reshape(B_, 32, 128, E)[b, own_tiles] = o
    return outp


# version 17
# speedup vs baseline: 1.0141x; 1.0079x over previous
import math
import numpy as np
import ml_dtypes
import concourse.bass as bass
import concourse.mybir as mybir
from concourse.bass_utils import run_bass_kernel_spmd

F32 = mybir.dt.float32
BF16 = mybir.dt.bfloat16
I32 = mybir.dt.int32
AF = mybir.ActivationFunctionType
ALU = mybir.AluOpType
AX = mybir.AxisListType

B_, S, E, H, D = 4, 4096, 2048, 16, 128
INNER = 8192
OWN = 2048
NT = 16
KTN = 32
NQG = 8
QG = 256
CK = 1024
NCK = OWN // CK
ET = E // 128
OTN = INNER // 128
SCALE = 1.0 / math.sqrt(D)
EXPC = 18.0
BF = ml_dtypes.bfloat16
ROLES = ("sync", "tensor", "vector", "scalar", "gpsimd")
DEBUG_DUMP = False


def _split(v):
    hi = np.float32(v)
    return float(hi), float(np.float64(v) - np.float64(hi))


LNC_HI, LNC_LO = _split(math.log(10000.0))
SC_HI, SC_LO = _split(1.0 / float(np.float32(np.sqrt(np.float32(128.0)))))
SQA_HI, SQA_LO = _split(math.sqrt(2048.0))
SQB_HI, SQB_LO = _split(math.sqrt(8192.0))
I127_HI, I127_LO = _split(1.0 / 127.0)
TPI_HI, TPI_LO = _split(2.0 * math.pi)
PIH_HI, PIH_LO = _split(0.5 * math.pi)
INV2PI = float(np.float32(1.0 / (2.0 * math.pi)))
PI_CLAMP = 3.1415925


_TABLES = None


def _rope_tables_f32():
    global _TABLES
    if _TABLES is not None:
        return _TABLES
    try:
        import jax
        import jax.numpy as jnp
        with jax.default_device(jax.devices("cpu")[0]):
            theta = jnp.repeat(-jnp.arange(0, D, 2, dtype=jnp.float32) / D, 2)
            theta = jnp.power(10000.0, theta)
            pos = jnp.arange(S, dtype=jnp.float32)[:, None]
            ang = pos * theta[None, :]
            cos = np.asarray(jax.device_get(jnp.cos(ang)), dtype=np.float32)
            sin = np.asarray(jax.device_get(jnp.sin(ang)), dtype=np.float32)
    except Exception:
        d = np.float32(128.0)
        theta = np.repeat(-(np.arange(0, 128, dtype=np.float32))[::2] / d, 2).astype(np.float32)
        theta = np.power(np.float32(10000.0), theta).astype(np.float32)
        ang = (np.arange(S, dtype=np.float32)[:, None] * theta[None, :]).astype(np.float32)
        cos = np.cos(ang).astype(np.float32)
        sin = np.sin(ang).astype(np.float32)
    sgn = np.where(np.arange(128) % 2 == 0, np.float32(-1.0), np.float32(1.0))
    _TABLES = (cos, (sin * sgn[None, :]).astype(np.float32))
    return _TABLES


class Op:
    __slots__ = ("eng", "fn", "deps", "inc", "cum", "needed", "grp", "gcum", "small")

    def __init__(self, eng, fn, deps, inc):
        self.eng, self.fn, self.deps, self.inc = eng, fn, deps, inc
        self.cum, self.needed = 0, False
        self.grp, self.gcum = None, 0
        self.small = False


class Grp:
    __slots__ = ("name", "count", "sem")

    def __init__(self, name):
        self.name, self.count, self.sem = name, 0, None


class Res:
    __slots__ = ("w", "readers")

    def __init__(self):
        self.w, self.readers = None, []


class Prog:
    def __init__(self):
        self.ops = {e: [] for e in ROLES}
        self.all = []

    def add(self, eng, fn, reads=(), writes=(), deps=(), dma=False, g=None, small=False):
        dd = []
        seen = set()
        def push(d):
            if d is not None and id(d) not in seen:
                seen.add(id(d))
                dd.append(d)
        for d in deps:
            push(d)
        for r in reads:
            push(r.w)
        for r in writes:
            for x in r.readers:
                push(x)
            push(r.w)
        op = Op(eng, fn, dd, 16 if dma else 1)
        op.small = small
        if dma:
            assert g is not None, "dma op needs a group"
            g.count += 16
            op.grp, op.gcum = g, g.count
        for r in reads:
            r.readers.append(op)
        for r in writes:
            r.w, r.readers = op, []
        self.ops[eng].append(op)
        self.all.append(op)
        return op

    def finalize(self):
        for op in self.all:
            if op.grp is not None:
                op.needed = True
            for d in op.deps:
                if d.eng != op.eng or d.grp is not None:
                    d.needed = True
        for e in ROLES:
            c = 0
            for op in self.ops[e]:
                if op.needed and op.grp is None:
                    c += op.inc
                    op.cum = c

    def emit(self, role, eng, sems):
        wm = {}
        for op in self.ops[role]:
            for d in op.deps:
                if not d.needed:
                    continue
                if d.grp is not None:
                    key, val, sem = d.grp.name, d.gcum, d.grp.sem
                elif d.eng != role:
                    key, val, sem = d.eng, d.cum, sems[d.eng]
                else:
                    continue
                if val > wm.get(key, 0):
                    eng.wait_ge(sem, val)
                    wm[key] = val
            r = op.fn(eng)
            if op.small and op.grp is None:
                r = eng.drain()
            if op.needed:
                tgt = op.grp.sem if op.grp is not None else sems[op.eng]
                (r if r is not None else eng.nop()).then_inc(tgt, op.inc)


def build():
    nc = bass.Bass("TRN2", target_bir_lowering=False, detect_race_conditions=False)

    xk = nc.dram_tensor("xk", [S, E], F32, kind="ExternalInput").ap()
    xo = nc.dram_tensor("xo", [OWN, E], F32, kind="ExternalInput").ap()
    Wq = nc.dram_tensor("Wq", [E, E], F32, kind="ExternalInput").ap()
    Wk = nc.dram_tensor("Wk", [E, E], F32, kind="ExternalInput").ap()
    Wv = nc.dram_tensor("Wv", [E, E], F32, kind="ExternalInput").ap()
    Wo = nc.dram_tensor("Wo", [E, E], F32, kind="ExternalInput").ap()
    bq = nc.dram_tensor("bq", [E], F32, kind="ExternalInput").ap()
    bk = nc.dram_tensor("bk", [E], F32, kind="ExternalInput").ap()
    bv = nc.dram_tensor("bv", [E], F32, kind="ExternalInput").ap()
    bo = nc.dram_tensor("bo", [E], F32, kind="ExternalInput").ap()
    W1 = nc.dram_tensor("W1", [INNER, E], F32, kind="ExternalInput").ap()
    W2 = nc.dram_tensor("W2", [E, INNER], F32, kind="ExternalInput").ap()
    maskin = nc.dram_tensor("maskin", [KTN * 128, QG], BF16, kind="ExternalInput").ap()
    poso = nc.dram_tensor("poso", [128, OWN], F32, kind="ExternalInput").ap()
    out = nc.dram_tensor("out", [OWN, E], F32, kind="ExternalOutput").ap()

    def scr(name, shape, dt=BF16):
        return nc.dram_tensor(name, shape, dt, kind="Internal").ap()

    WbS = {"Wq": scr("WqbS", [E, E]), "Wk": scr("WkbS", [E, E]),
           "Wv": scr("WvbS", [E, E]), "Wo": scr("WobS", [E, E])}
    bqkS = scr("bqkS", [32, 128])
    W1qS = scr("W1qS", [INNER, E])
    W2qS = scr("W2qS", [E, INNER])
    xnS = scr("xnS", [S, E])
    xnoS = scr("xnoS", [OWN, E])
    KS = scr("KS", [E, S])
    VS = scr("VS", [S, E])
    QS = scr("QS", [E, OWN])
    OS = scr("OS", [E, OWN])
    QA1S = scr("QA1S", [CK, E])
    ghS = scr("ghS", [CK, INNER], F32)
    QA2S = scr("QA2S", [CK, INNER])

    cos32, sinF32 = _rope_tables_f32()
    C_cosk = nc.inline_tensor(np.ascontiguousarray(cos32.T), "coskC").ap()
    C_sink = nc.inline_tensor(np.ascontiguousarray(sinF32.T), "sinkC").ap()
    cosoI = nc.dram_tensor("cosoI", [128, OWN], F32, kind="ExternalInput").ap()
    sinoI = nc.dram_tensor("sinoI", [128, OWN], F32, kind="ExternalInput").ap()
    perm_np = np.zeros((128, 128), dtype=np.float32)
    for i in range(64):
        perm_np[2 * i + 1, 2 * i] = 1.0
        perm_np[2 * i, 2 * i + 1] = 1.0
    C_perm = nc.inline_tensor(perm_np.astype(BF), "permC").ap()
    C_o128b = nc.inline_tensor(np.ones((128, 1), dtype=BF), "o128bC").ap()
    C_o128f = nc.inline_tensor(np.ones((128, 1), dtype=np.float32), "o128fC").ap()
    C_or128b = nc.inline_tensor(np.ones((1, 128), dtype=BF), "or128bC").ap()
    C_or128f = nc.inline_tensor(np.ones((1, 128), dtype=np.float32), "or128fC").ap()

    A = nc.alloc_sbuf_tensor("Af", [128, 14336], F32)
    Bb = nc.alloc_sbuf_tensor("Bbf", [128, 58368], BF16)
    Ii = nc.alloc_sbuf_tensor("Iint", [128, 1024], I32)
    cosb = nc.alloc_sbuf_tensor("cosb", [128, S], BF16)
    sinb = nc.alloc_sbuf_tensor("sinb", [128, S], BF16)
    cosbo = nc.alloc_sbuf_tensor("cosbo", [128, OWN], BF16)
    sinbo = nc.alloc_sbuf_tensor("sinbo", [128, OWN], BF16)
    o128b = nc.alloc_sbuf_tensor("o128b", [128, 1], BF16)
    o128f = nc.alloc_sbuf_tensor("o128f", [128, 1], F32)
    or128b = nc.alloc_sbuf_tensor("or128b", [1, 128], BF16)
    or128f = nc.alloc_sbuf_tensor("or128f", [1, 128], F32)
    permM = nc.alloc_sbuf_tensor("permM", [128, 128], BF16)
    bqkf = nc.alloc_sbuf_tensor("bqkf", [128, 32], F32)
    bvrow = nc.alloc_sbuf_tensor("bvrow", [1, E], BF16)
    borow = nc.alloc_sbuf_tensor("borow", [1, E], BF16)
    swc = nc.alloc_sbuf_tensor("swc", [128, 4], F32)
    smallf = nc.alloc_sbuf_tensor("smallf", [1, 64], F32)
    colsf = nc.alloc_sbuf_tensor("colsf", [128, 80], F32)
    PS = [nc.alloc_psum_tensor("ps%d" % i, [128, 512], F32) for i in range(8)]

    P = Prog()
    GR = {}

    def G(name):
        if name not in GR:
            GR[name] = Grp(name)
        return GR[name]

    NGA = 14336 // 256
    NGB = 58368 // 512
    rga = [Res() for _ in range(NGA)]
    rgb = [Res() for _ in range(NGB)]
    rI = Res()
    rPS = [Res() for _ in range(8)]
    rtab = Res()
    rconst = Res()
    rbqk, rbv, rbo, rswc, rsmall = Res(), Res(), Res(), Res(), Res()
    rcols = [Res() for _ in range(80)]
    rscr = {k: Res() for k in ("WqbS", "WkbS", "WvbS", "WobS", "bqkS", "W1qS", "W2qS",
                               "xnoS", "QS", "OS", "QA1S", "ghS", "QA2S", "KS", "VS")}
    rxn = [Res() for _ in range(8)]

    def af(off, w):
        assert off + w <= 14336
        return A[:, off:off + w]

    def ab(off, w):
        assert off + w <= 58368
        return Bb[:, off:off + w]

    def rAs(off, w):
        return [rga[i] for i in range(off // 256, (off + w + 255) // 256)]

    def rBs(off, w):
        return [rgb[i] for i in range(off // 512, (off + w + 511) // 512)]

    OUT_OPS = []

    def emit_round2048(src, dst, tmp):
        for c0 in range(0, 2048, 1024):
            P.add("vector", lambda e, c0=c0: e.tensor_copy(Ii[:, 0:1024], af(src + c0, 1024)),
                  reads=rAs(src + c0, 1024), writes=[rI])
            P.add("vector", lambda e, c0=c0: e.tensor_copy(af(dst + c0, 1024), Ii[:, 0:1024]),
                  reads=[rI], writes=rAs(dst + c0, 1024))
        P.add("vector", lambda e: e.tensor_sub(af(tmp, 2048), af(dst, 2048), af(src, 2048)),
              reads=rAs(dst, 2048) + rAs(src, 2048), writes=rAs(tmp, 2048))
        P.add("vector", lambda e: e.scalar_tensor_tensor(af(dst, 2048), af(tmp, 2048), -0.5, af(dst, 2048),
                                                         ALU.is_lt, ALU.add),
              reads=rAs(tmp, 2048) + rAs(dst, 2048), writes=rAs(dst, 2048))
        P.add("vector", lambda e: e.tensor_scalar(af(tmp, 2048), af(tmp, 2048), 0.5, None, ALU.is_gt),
              reads=rAs(tmp, 2048), writes=rAs(tmp, 2048))
        P.add("vector", lambda e: e.tensor_sub(af(dst, 2048), af(dst, 2048), af(tmp, 2048)),
              reads=rAs(dst, 2048) + rAs(tmp, 2048), writes=rAs(dst, 2048))


    def emit_ln(xoff, boff, cbase, quant):
        cm = colsf[:, cbase + 0:cbase + 1]
        crs = colsf[:, cbase + 1:cbase + 2]
        cac = colsf[:, cbase + 2:cbase + 3]
        ctm = colsf[:, cbase + 3:cbase + 4]
        cc1 = colsf[:, cbase + 4:cbase + 5]
        cd1 = colsf[:, cbase + 5:cbase + 6]
        cmx = colsf[:, cbase + 6:cbase + 7]
        cmn = colsf[:, cbase + 7:cbase + 8]
        rc = rcols[cbase]
        x = af(xoff, 2048)
        rx = rAs(xoff, 2048)
        sq = af(xoff + 2048, 2048)
        rsq = rAs(xoff + 2048, 2048)
        P.add("scalar", lambda e: e.activation(sq, x, AF.Square, accum_out=cac), reads=rx, writes=rsq + [rc], small=True)
        P.add("vector", lambda e: e.tensor_reduce(cm, x, AX.X, ALU.add), reads=rx, writes=[rc], small=True)
        P.add("vector", lambda e: e.tensor_scalar(cm, cm, 1.0 / 2048.0, None, ALU.mult), reads=[rc], writes=[rc], small=True)
        P.add("vector", lambda e: e.tensor_scalar(cac, cac, 1.0 / 2048.0, None, ALU.mult), reads=[rc], writes=[rc], small=True)
        P.add("vector", lambda e: e.tensor_mul(ctm, cm, cm), reads=[rc], writes=[rc], small=True)
        P.add("vector", lambda e: e.tensor_sub(cac, cac, ctm), reads=[rc], writes=[rc], small=True)
        P.add("vector", lambda e: e.tensor_scalar(ctm, cac, 1.0, 1e-5, ALU.mult, ALU.add), reads=[rc], writes=[rc], small=True)
        P.add("scalar", lambda e: e.activation(ctm, ctm, AF.Sqrt), reads=[rc], writes=[rc], small=True)
        P.add("vector", lambda e: e.reciprocal(crs, ctm), reads=[rc], writes=[rc], small=True)
        if not quant:
            P.add("vector", lambda e: e.tensor_scalar(ab(boff, 2048), x, cm, crs, ALU.subtract, ALU.mult),
                  reads=rx + [rc], writes=rBs(boff, 2048), small=True)
            return
        P.add("vector", lambda e: e.tensor_mul(cc1, crs, crs), reads=[rc], writes=[rc], small=True)
        P.add("vector", lambda e: e.tensor_mul(cc1, cc1, cac), reads=[rc], writes=[rc], small=True)
        P.add("vector", lambda e: e.tensor_scalar(cc1, cc1, 2048.0, None, ALU.mult), reads=[rc], writes=[rc], small=True)
        P.add("scalar", lambda e: e.activation(cc1, cc1, AF.Sqrt), reads=[rc], writes=[rc], small=True)
        P.add("vector", lambda e: e.tensor_scalar(cc1, cc1, 1.0, 1e-12, ALU.mult, ALU.max), reads=[rc], writes=[rc], small=True)
        P.add("vector", lambda e: e.reciprocal(ctm, cc1), reads=[rc], writes=[rc], small=True)
        P.add("vector", lambda e: e.tensor_scalar(ctm, ctm, math.sqrt(2048.0), None, ALU.mult), reads=[rc], writes=[rc], small=True)
        P.add("vector", lambda e: e.tensor_reduce(cmx, x, AX.X, ALU.max), reads=rx, writes=[rc], small=True)
        P.add("vector", lambda e: e.tensor_reduce(cmn, x, AX.X, ALU.min), reads=rx, writes=[rc], small=True)
        P.add("vector", lambda e: e.tensor_sub(cmx, cmx, cm), reads=[rc], writes=[rc], small=True)
        P.add("vector", lambda e: e.tensor_sub(cmn, cm, cmn), reads=[rc], writes=[rc], small=True)
        P.add("vector", lambda e: e.tensor_max(cmx, cmx, cmn), reads=[rc], writes=[rc], small=True)
        P.add("vector", lambda e: e.tensor_mul(cmx, cmx, crs), reads=[rc], writes=[rc], small=True)
        P.add("vector", lambda e: e.tensor_mul(cmx, cmx, ctm), reads=[rc], writes=[rc], small=True)
        P.add("vector", lambda e: e.tensor_scalar(cmx, cmx, 1.0, 1e-5, ALU.mult, ALU.max), reads=[rc], writes=[rc], small=True)
        P.add("vector", lambda e: e.reciprocal(cmn, cmx), reads=[rc], writes=[rc], small=True)
        P.add("vector", lambda e: e.tensor_scalar(cmn, cmn, 127.0, None, ALU.mult), reads=[rc], writes=[rc], small=True)
        P.add("vector", lambda e: e.tensor_mul(cc1, ctm, cmn), reads=[rc], writes=[rc], small=True)
        P.add("vector", lambda e: e.tensor_mul(cc1, cc1, crs), reads=[rc], writes=[rc], small=True)
        P.add("vector", lambda e: e.tensor_scalar(cd1, cmx, 1.0 / 127.0, None, ALU.mult), reads=[rc], writes=[rc], small=True)
        P.add("vector", lambda e: e.tensor_mul(cd1, cd1, swc[:, 0:1]), reads=[rc, rswc], writes=[rc], small=True)
        t0 = sq
        P.add("vector", lambda e: e.tensor_scalar(t0, x, cm, cc1, ALU.subtract, ALU.mult),
              reads=rx + [rc], writes=rsq, small=True)
        QR, QT = 8192, 10240
        emit_round2048(xoff + 2048, QR, QT)
        P.add("vector", lambda e: e.tensor_copy(ab(boff, 2048), af(QR, 2048)),
              reads=rAs(QR, 2048), writes=rBs(boff, 2048))


    for nm, cst, dstt, rr in (("o128b", C_o128b, o128b, rconst), ("o128f", C_o128f, o128f, rconst),
                              ("or128b", C_or128b, or128b, rconst), ("or128f", C_or128f, or128f, rconst),
                              ("perm", C_perm, permM, rconst)):
        P.add("sync", lambda e, c=cst, d=dstt: e.dma_start(out=d[:, :], in_=c), writes=[rr], dma=True, g=G("once"))

    mats = [(Wq, "Wq"), (Wk, "Wk"), (Wv, "Wv"), (Wo, "Wo")]
    XT = [0, 4096]
    XTa = [4096, 8192]
    XNB = [54272, 56320]

    def a1a_tile(n):
        kc, j = n // 4, n % 4
        sl = n % 2
        P.add("sync", lambda e, kc=kc, j=j, sl=sl: e.dma_start(
            out=af(XTa[sl], 2048), in_=xk[512 * kc + 128 * j:512 * kc + 128 * j + 128, :]),
            writes=rAs(XTa[sl], 2048), dma=True, g=G("a2" if sl == 0 else "a3"))
        emit_ln(XTa[sl], XNB[sl], 8, False)
        P.add("gpsimd", lambda e, kc=kc, j=j, sl=sl: e.dma_start(
            out=xnS[512 * kc + 128 * j:512 * kc + 128 * j + 128, :], in_=ab(XNB[sl], 2048)),
            reads=rBs(XNB[sl], 2048), writes=[rxn[kc]], dma=True, g=G("xns%d" % kc))

    for i in range(64):
        m, nmw = mats[i // 16]
        et = i % 16
        fo = 2048 * (i % 2)
        bo_ = 2048 * (i % 2)
        P.add("sync", lambda e, m=m, et=et, fo=fo: e.dma_start(out=af(fo, 2048), in_=m[128 * et:128 * (et + 1), :]),
              writes=rAs(fo, 2048), dma=True, g=G("a%d" % (i % 2)))
        P.add("vector", lambda e, fo=fo, bo_=bo_: e.tensor_copy(ab(bo_, 2048), af(fo, 2048)),
              reads=rAs(fo, 2048), writes=rBs(bo_, 2048))
        P.add("gpsimd", lambda e, nmw=nmw, et=et, bo_=bo_: e.dma_start(
            out=WbS[nmw][128 * et:128 * (et + 1), :], in_=ab(bo_, 2048)),
            reads=rBs(bo_, 2048), writes=[rscr[nmw + "bS"]], dma=True, g=G("wbs" + nmw))
        if i % 2 == 1:
            a1a_tile(i // 2)
    P.add("sync", lambda e: e.dma_start(out=A[0:16, 4096:4224], in_=bq.rearrange("(a b) -> a b", b=128)),
          writes=rAs(4096, 128), dma=True, g=G("a2"))
    P.add("sync", lambda e: e.dma_start(out=A[16:32, 4096:4224], in_=bk.rearrange("(a b) -> a b", b=128)),
          writes=rAs(4096, 128), dma=True, g=G("a2"))
    P.add("vector", lambda e: e.tensor_copy(Bb[0:32, 4096:4224], A[0:32, 4096:4224]),
          reads=rAs(4096, 128), writes=rBs(4096, 128))
    P.add("gpsimd", lambda e: e.dma_start(out=bqkS, in_=Bb[0:32, 4096:4224]),
          reads=rBs(4096, 128), writes=[rscr["bqkS"]], dma=True, g=G("bqks"))
    P.add("sync", lambda e: e.dma_start(out=af(8192, 2048)[0:1, :], in_=bv.rearrange("(o a) -> o a", o=1)),
          writes=rAs(8192, 2048), dma=True, g=G("a3"))
    P.add("vector", lambda e: e.tensor_copy(bvrow[0:1, :], af(8192, 2048)[0:1, :]),
          reads=rAs(8192, 2048), writes=[rbv], small=True)
    P.add("sync", lambda e: e.dma_start(out=af(8192, 2048)[0:1, :], in_=bo.rearrange("(o a) -> o a", o=1)),
          writes=rAs(8192, 2048), dma=True, g=G("a3"))
    P.add("vector", lambda e: e.tensor_copy(borow[0:1, :], af(8192, 2048)[0:1, :]),
          reads=rAs(8192, 2048), writes=[rbo], small=True)
    P.add("sync", lambda e: e.dma_start(out=Bb[:, 4096:4128], in_=bqkS, transpose=True),
          reads=[rscr["bqkS"]], writes=rBs(4096, 64), dma=True, g=G("bqkt"))
    P.add("vector", lambda e: e.tensor_copy(bqkf[:, :], Bb[:, 4096:4128]),
          reads=rBs(4096, 64), writes=[rbqk], small=True)

    for dst, srcap, w, rr in ((cosb, C_cosk, S, rtab), (sinb, C_sink, S, rtab),
                              (cosbo, cosoI, OWN, rtab), (sinbo, sinoI, OWN, rtab)):
        for c0 in range(0, w, 2048):
            P.add("sync", lambda e, dst=dst, srcap=srcap, c0=c0: e.dma_start(
                out=af(0, 2048), in_=srcap[:, c0:c0 + 2048]),
                writes=rAs(0, 2048), dma=True, g=G("a0"))
            P.add("vector", lambda e, dst=dst, c0=c0: e.tensor_copy(dst[:, c0:c0 + 2048], af(0, 2048)),
                  reads=rAs(0, 2048), writes=[rr])

    def w_src(i):
        if i < 64:
            return W1[128 * i:128 * (i + 1), :]
        j = i - 64
        return W2[128 * (j // 4):128 * (j // 4 + 1), 2048 * (j % 4):2048 * (j % 4 + 1)]

    def w_dst(i):
        if i < 64:
            return W1qS[128 * i:128 * (i + 1), :]
        j = i - 64
        return W2qS[128 * (j // 4):128 * (j // 4 + 1), 2048 * (j % 4):2048 * (j % 4 + 1)]

    ACC = [colsf[:, 4:5], colsf[:, 5:6]]
    RCOL = colsf[:, 6:7]
    rACC = [rcols[4], rcols[5]]
    rRC = rcols[6]
    MSL = [0, 2048, 4096, 6144]
    MGRP = ["a0", "am1", "a2", "am3"]
    MENG = ["sync", "gpsimd", "sync", "gpsimd"]

    def mean_tile(i):
        s = i % 4
        P.add(MENG[s], lambda e, i=i, s=s: e.dma_start(out=af(MSL[s], 2048), in_=w_src(i)),
              writes=rAs(MSL[s], 2048), dma=True, g=G(MGRP[s]))
        P.add("vector", lambda e, s=s: e.tensor_reduce(RCOL, af(MSL[s], 2048), AX.X, ALU.add,
                                                       apply_absolute_value=True),
              reads=rAs(MSL[s], 2048), writes=[rRC], small=True)
        jj = 0 if i < 64 else 1
        if i in (0, 64):
            P.add("vector", lambda e, jj=jj: e.tensor_copy(ACC[jj], RCOL), reads=[rRC], writes=[rACC[jj]], small=True)
        else:
            P.add("vector", lambda e, jj=jj: e.tensor_add(ACC[jj], ACC[jj], RCOL), reads=[rRC], writes=[rACC[jj]], small=True)

    def swc_chain():
        for j in range(2):
            P.add("tensor", lambda e, j=j: e.matmul(PS[0][0:1, 0:1], o128f[:, :], ACC[j], start=True, stop=True),
                  reads=[rACC[j], rconst], writes=[rPS[0]])
            P.add("vector", lambda e, j=j: e.tensor_scalar(smallf[0:1, 2 * j:2 * j + 1], PS[0][0:1, 0:1],
                                                           1.0 / (INNER * E), 1e-5, ALU.mult, ALU.max),
                  reads=[rPS[0]], writes=[rsmall], small=True)
            P.add("vector", lambda e, j=j: e.reciprocal(smallf[0:1, 2 * j + 1:2 * j + 2], smallf[0:1, 2 * j:2 * j + 1]),
                  reads=[rsmall], writes=[rsmall], small=True)
            P.add("tensor", lambda e, j=j: e.matmul(PS[1][:, 0:2], or128f[:, :], smallf[0:1, 2 * j:2 * j + 2],
                                                    start=True, stop=True),
                  reads=[rsmall, rconst], writes=[rPS[1]])
            P.add("scalar", lambda e, j=j: e.activation(swc[:, 2 * j:2 * j + 2], PS[1][:, 0:2], AF.Copy),
                  reads=[rPS[1]], writes=[rswc])

    T0q, GQ, LQ = 8192, 10240, 12288
    QOB = [54272, 56320]

    def quant_tile(i):
        s = i % 4
        bo_ = QOB[i % 2]
        scol = swc[:, 1:2] if i < 64 else swc[:, 3:4]
        P.add(MENG[s], lambda e, i=i, s=s: e.dma_start(out=af(MSL[s], 2048), in_=w_src(i)),
              writes=rAs(MSL[s], 2048), dma=True, g=G(MGRP[s]))
        P.add("vector", lambda e, s=s, scol=scol: e.tensor_scalar(af(T0q, 2048), af(MSL[s], 2048), scol, None, ALU.mult),
              reads=rAs(MSL[s], 2048) + [rswc], writes=rAs(T0q, 2048))
        P.add("vector", lambda e: e.tensor_scalar(af(GQ, 2048), af(T0q, 2048), 0.5, None, ALU.is_gt),
              reads=rAs(T0q, 2048), writes=rAs(GQ, 2048))
        P.add("vector", lambda e: e.tensor_scalar(af(LQ, 2048), af(T0q, 2048), -0.5, None, ALU.is_lt),
              reads=rAs(T0q, 2048), writes=rAs(LQ, 2048))
        P.add("vector", lambda e, bo_=bo_: e.tensor_sub(ab(bo_, 2048), af(GQ, 2048), af(LQ, 2048)),
              reads=rAs(GQ, 2048) + rAs(LQ, 2048), writes=rBs(bo_, 2048))
        dn = "W1qS" if i < 64 else "W2qS"
        P.add("gpsimd", lambda e, i=i, bo_=bo_: e.dma_start(out=w_dst(i), in_=ab(bo_, 2048)),
              reads=rBs(bo_, 2048), writes=[rscr[dn]], dma=True, g=G(dn))

    def mk_feeder(fn, total):
        state = {"i": 0}

        def feed():
            if state["i"] < total:
                fn(state["i"])
                state["i"] += 1
        return feed

    WT0 = 0
    XF = [32768, 40960]
    KB0, PB0, RT0, RO0, VB0 = 49152, 50176, 51200, 52224, 53248

    def wTt(et):
        return Bb[:, WT0 + 2048 * et:WT0 + 2048 * et + 2048]

    rWTg = rBs(WT0, 32768)

    def proj(which, extra=None):
        srcS = xnS if which in ("K", "V") else xnoS
        nkc = 8 if which in ("K", "V") else 4
        wname = which if which != "K" else "Wk"
        wname = {"K": "Wk", "V": "Wv", "Q": "Wq"}[which]
        for et in range(ET):
            P.add("sync", lambda e, et=et, wname=wname: e.dma_start(
                out=wTt(et), in_=WbS[wname][:, 128 * et:128 * (et + 1)], transpose=True),
                reads=[rscr[wname + "bS"]], writes=rBs(WT0 + 2048 * et, 2048), dma=True, g=G("bwt"))
        for kc in range(nkc):
            sl = kc % 2
            rsrc = rxn[kc] if which in ("K", "V") else rscr["xnoS"]
            for et in range(ET):
                P.add("sync", lambda e, kc=kc, et=et, sl=sl, srcS=srcS: e.dma_start(
                    out=Bb[:, XF[sl] + 512 * et:XF[sl] + 512 * et + 512],
                    in_=srcS[512 * kc:512 * (kc + 1), 128 * et:128 * (et + 1)], transpose=True),
                    reads=[rsrc], writes=rBs(XF[sl] + 512 * et, 512), dma=True, g=G("xf%d" % sl))
            rxf = rBs(XF[sl], 8192)
            if which in ("K", "Q"):
                outS = KS if which == "K" else QS
                rout = rscr["KS" if which == "K" else "QS"]
                bcol = 16 if which == "K" else 0
                cosT = cosb if which == "K" else cosbo
                sinT = sinb if which == "K" else sinbo
                for h in range(H):
                    pk = rPS[h % 2]
                    def mmk(e, h=h, sl=sl):
                        r = None
                        for et in range(ET):
                            r = e.matmul(PS[h % 2][:, 0:512], wTt(et)[:, 128 * h:128 * (h + 1)],
                                         Bb[:, XF[sl] + 512 * et:XF[sl] + 512 * et + 512],
                                         start=(et == 0), stop=(et == ET - 1))
                        return r
                    P.add("tensor", mmk, reads=rxf + rWTg, writes=[pk])
                    kb = 512 * (h % 2)
                    P.add("scalar", lambda e, h=h, kb=kb, bcol=bcol: e.activation(
                        Bb[:, KB0 + kb:KB0 + kb + 512], PS[h % 2][:, 0:512], AF.Identity,
                        bias=bqkf[:, bcol + h:bcol + h + 1]),
                        reads=[pk, rbqk], writes=rBs(KB0 + kb, 512))
                    P.add("tensor", lambda e, h=h, kb=kb: e.matmul(
                        PS[2 + h % 2][:, 0:512], permM[:, :], Bb[:, KB0 + kb:KB0 + kb + 512], start=True, stop=True),
                        reads=rBs(KB0 + kb, 512) + [rconst], writes=[rPS[2 + h % 2]])
                    P.add("scalar", lambda e, h=h, kb=kb: e.activation(
                        Bb[:, PB0 + kb:PB0 + kb + 512], PS[2 + h % 2][:, 0:512], AF.Copy),
                        reads=[rPS[2 + h % 2]], writes=rBs(PB0 + kb, 512))
                    def rope(e, kc=kc, h=h, kb=kb, cosT=cosT, sinT=sinT, wq=(which == "Q")):
                        t1 = Bb[:, RT0:RT0 + 512]
                        t2 = Bb[:, RT0 + 512:RT0 + 1024]
                        e.tensor_mul(t1, Bb[:, KB0 + kb:KB0 + kb + 512], cosT[:, 512 * kc:512 * (kc + 1)])
                        e.tensor_mul(t2, Bb[:, PB0 + kb:PB0 + kb + 512], sinT[:, 512 * kc:512 * (kc + 1)])
                        if not wq:
                            return e.tensor_add(Bb[:, RO0 + kb:RO0 + kb + 512], t1, t2)
                        e.tensor_add(t1, t1, t2)
                        e.tensor_scalar(t2, t1, SC_LO, None, ALU.mult)
                        e.tensor_scalar(t1, t1, SC_HI, None, ALU.mult)
                        return e.tensor_add(Bb[:, RO0 + kb:RO0 + kb + 512], t1, t2)
                    P.add("vector", rope, reads=rBs(KB0 + kb, 512) + rBs(PB0 + kb, 512) + [rtab],
                          writes=rBs(RT0, 1024) + rBs(RO0 + kb, 512))
                    P.add("gpsimd", lambda e, kc=kc, h=h, kb=kb, outS=outS: e.dma_start(
                        out=outS[128 * h:128 * (h + 1), 512 * kc:512 * (kc + 1)], in_=Bb[:, RO0 + kb:RO0 + kb + 512]),
                        reads=rBs(RO0 + kb, 512), writes=[rout], dma=True, g=G("ks" if which == "K" else "qs"))
                    if extra:
                        extra()
            else:
                for g2 in range(16):
                    kt, fc = g2 // 4, g2 % 4
                    pv = rPS[4 + g2 % 2]
                    def mmv(e, kt=kt, fc=fc, g2=g2, sl=sl):
                        for et in range(ET):
                            e.matmul(PS[4 + g2 % 2][:, 0:512],
                                     Bb[:, XF[sl] + 512 * et + 128 * kt:XF[sl] + 512 * et + 128 * kt + 128],
                                     wTt(et)[:, 512 * fc:512 * (fc + 1)], start=(et == 0), stop=False)
                        return e.matmul(PS[4 + g2 % 2][:, 0:512], or128b[:, :], bvrow[0:1, 512 * fc:512 * (fc + 1)],
                                        start=False, stop=True)
                    P.add("tensor", mmv, reads=rxf + rWTg + [rconst, rbv], writes=[pv])
                    vb = 512 * (g2 % 2)
                    P.add("scalar", lambda e, g2=g2, vb=vb: e.activation(
                        Bb[:, VB0 + vb:VB0 + vb + 512], PS[4 + g2 % 2][:, 0:512], AF.Copy),
                        reads=[pv], writes=rBs(VB0 + vb, 512))
                    P.add("gpsimd", lambda e, kc=kc, kt=kt, fc=fc, vb=vb: e.dma_start(
                        out=VS[512 * kc + 128 * kt:512 * kc + 128 * kt + 128, 512 * fc:512 * (fc + 1)],
                        in_=Bb[:, VB0 + vb:VB0 + vb + 512]),
                        reads=rBs(VB0 + vb, 512), writes=[rscr["VS"]], dma=True, g=G("vs"))
                    if extra:
                        extra()

    proj("K", extra=mk_feeder(mean_tile, 128))
    swc_chain()
    proj("V", extra=mk_feeder(quant_tile, 128))
    for n in range(16):
        oc, j = n // 4, n % 4
        sl = n % 2
        P.add("sync", lambda e, oc=oc, j=j, sl=sl: e.dma_start(
            out=af(XT[sl], 2048), in_=xo[512 * oc + 128 * j:512 * oc + 128 * j + 128, :]),
            writes=rAs(XT[sl], 2048), dma=True, g=G("a0" if sl == 0 else "a2"))
        emit_ln(XT[sl], XNB[sl], 8, False)
        P.add("gpsimd", lambda e, oc=oc, j=j, sl=sl: e.dma_start(
            out=xnoS[512 * oc + 128 * j:512 * oc + 128 * j + 128, :], in_=ab(XNB[sl], 2048)),
            reads=rBs(XNB[sl], 2048), writes=[rscr["xnoS"]], dma=True, g=G("xnos"))
    proj("Q")

    KH0, VH0, QH0, MSK0, KSQ0, PB_, ONB, QSQ0 = 0, 8192, 16384, 20480, 28672, 32768, 49152, 49408
    SST = 4096
    P.add("sync", lambda e: e.dma_start(out=Bb[:, MSK0:MSK0 + 8192],
                                        in_=maskin.rearrange("(t p) q -> p t q", p=128)),
          writes=rBs(MSK0, 8192), dma=True, g=G("msk"))
    for h in range(H):
        hs = h % 2
        rKH = rBs(KH0 + 4096 * hs, 4096)
        rVH = rBs(VH0 + 4096 * hs, 4096)
        rQH = rBs(QH0 + 2048 * hs, 2048)
        P.add("sync", lambda e, h=h, hs=hs: e.dma_start(
            out=Bb[:, KH0 + 4096 * hs:KH0 + 4096 * hs + 4096], in_=KS[128 * h:128 * (h + 1), :]),
            reads=[rscr["KS"]], writes=rKH, dma=True, g=G("kh%d" % hs))
        for part in range(4):
            P.add("sync", lambda e, h=h, hs=hs, part=part: e.dma_start(
                out=Bb[:, VH0 + 4096 * hs + 1024 * part:VH0 + 4096 * hs + 1024 * part + 1024],
                in_=VS[1024 * part:1024 * (part + 1), 128 * h:128 * (h + 1)].rearrange("(kt p) d -> p kt d", p=128)),
                reads=[rscr["VS"]], writes=rBs(VH0 + 4096 * hs + 1024 * part, 1024), dma=True, g=G("vh%d" % hs))
        P.add("sync", lambda e, h=h, hs=hs: e.dma_start(
            out=Bb[:, QH0 + 2048 * hs:QH0 + 2048 * hs + 2048], in_=QS[128 * h:128 * (h + 1), :]),
            reads=[rscr["QS"]], writes=rQH, dma=True, g=G("qh%d" % hs))
        P.add("scalar", lambda e, hs=hs: e.activation(Bb[:, KSQ0:KSQ0 + 4096],
                                                      Bb[:, KH0 + 4096 * hs:KH0 + 4096 * hs + 4096], AF.Square),
              reads=rKH, writes=rBs(KSQ0, 4096))
        for sl8 in range(8):
            P.add("tensor", lambda e, sl8=sl8: e.matmul(PS[6][0:1, 0:512], o128b[:, :],
                                                        Bb[:, KSQ0 + 512 * sl8:KSQ0 + 512 * sl8 + 512],
                                                        start=True, stop=True),
                  reads=rBs(KSQ0 + 512 * sl8, 512) + [rconst], writes=[rPS[6]])
            P.add("scalar", lambda e, sl8=sl8: e.activation(af(1024 + 0, 512)[0:1, 512 * 0:512] if False else
                                                            A[0:1, 2048 + 512 * (sl8 % 4):2048 + 512 * (sl8 % 4) + 512],
                                                            PS[6][0:1, 0:512], AF.Copy),
                  reads=[rPS[6]], writes=rAs(2048 + 512 * (sl8 % 4), 512), small=True)
            if sl8 % 4 == 3:
                P.add("vector", lambda e, sl8=sl8: e.tensor_reduce(smallf[0:1, 8 + sl8 // 4:9 + sl8 // 4],
                                                                   A[0:1, 2048:2048 + 2048], AX.X, ALU.max),
                      reads=rAs(2048, 2048), writes=[rsmall], small=True)
        P.add("vector", lambda e: e.tensor_max(smallf[0:1, 10:11], smallf[0:1, 8:9], smallf[0:1, 9:10]),
              reads=[rsmall], writes=[rsmall], small=True)
        P.add("vector", lambda e: e.tensor_scalar(smallf[0:1, 10:11], smallf[0:1, 10:11], 1.0, 1e-35, ALU.mult, ALU.add),
              reads=[rsmall], writes=[rsmall], small=True)
        P.add("scalar", lambda e: e.activation(smallf[0:1, 11:12], smallf[0:1, 10:11], AF.Ln),
              reads=[rsmall], writes=[rsmall], small=True)
        P.add("scalar", lambda e, hs=hs: e.activation(Bb[:, KSQ0:KSQ0 + 2048],
                                                      Bb[:, QH0 + 2048 * hs:QH0 + 2048 * hs + 2048], AF.Square),
              reads=rQH, writes=rBs(KSQ0, 2048))
        for j4 in range(4):
            P.add("tensor", lambda e, j4=j4: e.matmul(PS[6][0:1, 0:512], o128b[:, :],
                                                      Bb[:, KSQ0 + 512 * j4:KSQ0 + 512 * j4 + 512],
                                                      start=True, stop=True),
                  reads=rBs(KSQ0 + 512 * j4, 512) + [rconst], writes=[rPS[6]])
            P.add("vector", lambda e, j4=j4: e.tensor_scalar(A[0:1, 12288 + 512 * j4:12800 + 512 * j4],
                                                             PS[6][0:1, 0:512], 1.0, 1e-35, ALU.mult, ALU.add),
                  reads=[rPS[6]], writes=rAs(12288 + 512 * j4, 512), small=True)
        P.add("scalar", lambda e: e.activation(A[0:1, 12288:14336], A[0:1, 12288:14336], AF.Ln),
              reads=rAs(12288, 2048), writes=rAs(12288, 2048), small=True)
        P.add("vector", lambda e: e.tensor_scalar(A[0:1, 12288:14336], A[0:1, 12288:14336],
                                                  smallf[0:1, 11:12], None, ALU.add),
              reads=rAs(12288, 2048) + [rsmall], writes=rAs(12288, 2048), small=True)
        P.add("vector", lambda e: e.tensor_scalar(A[0:1, 12288:14336], A[0:1, 12288:14336], 0.5, 0.02,
                                                  ALU.mult, ALU.add),
              reads=rAs(12288, 2048), writes=rAs(12288, 2048), small=True)
        P.add("scalar", lambda e: e.activation(A[0:1, 12288:14336], A[0:1, 12288:14336], AF.Exp),
              reads=rAs(12288, 2048), writes=rAs(12288, 2048), small=True)
        P.add("vector", lambda e: e.tensor_scalar(A[0:1, 12288:14336], A[0:1, 12288:14336], -1.0, EXPC,
                                                  ALU.mult, ALU.add),
              reads=rAs(12288, 2048), writes=rAs(12288, 2048), small=True)
        def a2_phase1(qg):
            nkt = 4 * (qg + 1)
            gsl = qg % 2
            P.add("tensor", lambda e, qg=qg: e.matmul(PS[7][:, 0:256], or128f[:, :],
                                                      A[0:1, 12288 + 256 * qg:12544 + 256 * qg], start=True, stop=True),
                  reads=rAs(12288 + 256 * qg, 256) + [rconst], writes=[rPS[7]])
            P.add("scalar", lambda e, gsl=gsl: e.activation(af(256 * gsl, 256), PS[7][:, 0:256], AF.Copy),
                  reads=[rPS[7]], writes=rAs(256 * gsl, 256))
            for kt in range(nkt):
                ks = kt % 2
                P.add("tensor", lambda e, hs=hs, kt=kt, ks=ks, qg=qg: e.matmul(
                    PS[ks][:, 0:256], Bb[:, KH0 + 4096 * hs + 128 * kt:KH0 + 4096 * hs + 128 * kt + 128],
                    Bb[:, QH0 + 2048 * hs + 256 * qg:QH0 + 2048 * hs + 256 * qg + 256], start=True, stop=True),
                    reads=rKH + rQH, writes=[rPS[ks]])
                P.add("vector", lambda e, ks=ks, gsl=gsl, kt=kt: e.tensor_add(
                    af(SST + 256 * kt, 256), PS[ks][:, 0:256], af(256 * gsl, 256)),
                    reads=[rPS[ks]] + rAs(256 * gsl, 256), writes=rAs(SST + 256 * kt, 256))
            pso = PB_ + 8192 * gsl
            P.add("scalar", lambda e, nkt=nkt, pso=pso: e.activation(
                Bb[:, pso:pso + 256 * nkt], af(SST, 256 * nkt), AF.Exp),
                reads=rAs(SST, 256 * nkt), writes=rBs(pso, 256 * nkt))
            kt0 = nkt - 4
            P.add("vector", lambda e, kt0=kt0, qg=qg, pso=pso: e.tensor_mul(
                Bb[:, pso + 256 * kt0:pso + 256 * kt0 + 1024], Bb[:, pso + 256 * kt0:pso + 256 * kt0 + 1024],
                Bb[:, MSK0 + 1024 * qg:MSK0 + 1024 * qg + 1024]),
                reads=rBs(pso + 256 * kt0, 1024) + rBs(MSK0 + 1024 * qg, 1024), writes=rBs(pso + 256 * kt0, 1024))

        def a2_phase2(qg):
            nkt = 4 * (qg + 1)
            gsl = qg % 2
            pso = PB_ + 8192 * gsl
            for kt in range(nkt):
                P.add("tensor", lambda e, hs=hs, kt=kt, nkt=nkt, pso=pso: e.matmul(
                    PS[2][:, 0:256], Bb[:, VH0 + 4096 * hs + 128 * kt:VH0 + 4096 * hs + 128 * kt + 128],
                    Bb[:, pso + 256 * kt:pso + 256 * kt + 256], start=(kt == 0), stop=(kt == nkt - 1)),
                    reads=rBs(pso + 256 * kt, 256) + rVH, writes=[rPS[2]])
                P.add("tensor", lambda e, kt=kt, nkt=nkt, pso=pso: e.matmul(
                    PS[3][0:1, 0:256], o128b[:, :], Bb[:, pso + 256 * kt:pso + 256 * kt + 256],
                    start=(kt == 0), stop=(kt == nkt - 1)),
                    reads=rBs(pso + 256 * kt, 256) + [rconst], writes=[rPS[3]])
            P.add("vector", lambda e: e.tensor_scalar(A[0:1, 1536:1792], PS[3][0:1, 0:256], 1.0, 1e-30,
                                                      ALU.mult, ALU.add),
                  reads=[rPS[3]], writes=rAs(1536, 256), small=True)
            P.add("vector", lambda e: e.reciprocal(A[0:1, 1536:1792], A[0:1, 1536:1792]),
                  reads=rAs(1536, 256), writes=rAs(1536, 256), small=True)
            P.add("tensor", lambda e: e.matmul(PS[7][:, 0:256], or128f[:, :], A[0:1, 1536:1792], start=True, stop=True),
                  reads=rAs(1536, 256) + [rconst], writes=[rPS[7]])
            P.add("scalar", lambda e: e.activation(af(1792, 256), PS[7][:, 0:256], AF.Copy),
                  reads=[rPS[7]], writes=rAs(1792, 256))
            P.add("vector", lambda e: e.tensor_mul(Bb[:, ONB:ONB + 256], PS[2][:, 0:256], af(1792, 256)),
                  reads=[rPS[2]] + rAs(1792, 256), writes=rBs(ONB, 256))
            P.add("gpsimd", lambda e, h=h, qg=qg: e.dma_start(
                out=OS[128 * h:128 * (h + 1), 256 * qg:256 * (qg + 1)], in_=Bb[:, ONB:ONB + 256]),
                reads=rBs(ONB, 256), writes=[rscr["OS"]], dma=True, g=G("os"))

        for qg in range(NQG):
            a2_phase1(qg)
            if qg > 0:
                a2_phase2(qg - 1)
        a2_phase2(NQG - 1)

    QA1F0, W1T0, GH0, OFM0, WOT0 = 0, 16384, 32768, 33792, 50176
    for ck in range(NCK):
        for j in range(8):
            sl = j % 2
            row0 = CK * ck + 128 * j
            P.add("sync", lambda e, row0=row0, sl=sl: e.dma_start(out=af(XT[sl], 2048), in_=xo[row0:row0 + 128, :]),
                  writes=rAs(XT[sl], 2048), dma=True, g=G("a0" if sl == 0 else "a2"))
            emit_ln(XT[sl], XNB[sl], 16 + 8 * sl, True)
            P.add("vector", lambda e, j=j, sl=sl: e.tensor_copy(colsf[:, 32 + j:33 + j],
                                                                colsf[:, 16 + 8 * sl + 5:16 + 8 * sl + 6]),
                  reads=[rcols[16 + 8 * sl]], writes=[rcols[32 + j]], small=True)
            P.add("gpsimd", lambda e, j=j, sl=sl: e.dma_start(out=QA1S[128 * j:128 * (j + 1), :], in_=ab(XNB[sl], 2048)),
                  reads=rBs(XNB[sl], 2048), writes=[rscr["QA1S"]], dma=True, g=G("qa1s"))
        for et in range(ET):
            P.add("sync", lambda e, et=et: e.dma_start(out=Bb[:, QA1F0 + 1024 * et:QA1F0 + 1024 * et + 1024],
                                                       in_=QA1S[:, 128 * et:128 * (et + 1)], transpose=True),
                  reads=[rscr["QA1S"]], writes=rBs(QA1F0 + 1024 * et, 1024), dma=True, g=G("qa1f"))
        rQA1F = rBs(QA1F0, 16384)
        for og in range(16):
            osl = og % 2
            rW1T = rBs(W1T0 + 8192 * osl, 8192)
            for et in range(ET):
                P.add("sync", lambda e, og=og, et=et, osl=osl: e.dma_start(
                    out=Bb[:, W1T0 + 8192 * osl + 512 * et:W1T0 + 8192 * osl + 512 * et + 512],
                    in_=W1qS[512 * og:512 * (og + 1), 128 * et:128 * (et + 1)], transpose=True),
                    reads=[rscr["W1qS"]], writes=rBs(W1T0 + 8192 * osl + 512 * et, 512), dma=True, g=G("w1t%d" % osl))
            for tt in range(8):
                pb = rPS[tt % 2]
                def mm1(e, tt=tt, osl=osl):
                    r = None
                    for et in range(ET):
                        r = e.matmul(PS[tt % 2][:, 0:512],
                                     Bb[:, QA1F0 + 1024 * et + 128 * tt:QA1F0 + 1024 * et + 128 * tt + 128],
                                     Bb[:, W1T0 + 8192 * osl + 512 * et:W1T0 + 8192 * osl + 512 * et + 512],
                                     start=(et == 0), stop=(et == ET - 1))
                    return r
                P.add("tensor", mm1, reads=rQA1F + rW1T, writes=[pb])
                gb = 13312 + 512 * ((og * 8 + tt) % 2)
                P.add("scalar", lambda e, tt=tt, gb=gb: e.activation(
                    af(gb, 512), PS[tt % 2][:, 0:512], AF.Gelu, scale=colsf[:, 32 + tt:33 + tt]),
                    reads=[pb, rcols[32 + tt]], writes=rAs(gb, 512))
                P.add("scalar", lambda e, gb=gb: e.activation(af(12288, 512), af(gb, 512),
                                                              AF.Square, accum_out=colsf[:, 56:57]),
                      reads=rAs(gb, 512), writes=rAs(12288, 512) + [rcols[56]], small=True)
                if og == 0:
                    P.add("vector", lambda e, tt=tt: e.tensor_copy(colsf[:, 40 + tt:41 + tt], colsf[:, 56:57]),
                          reads=[rcols[56]], writes=[rcols[40 + tt]], small=True)
                else:
                    P.add("vector", lambda e, tt=tt: e.tensor_add(colsf[:, 40 + tt:41 + tt],
                                                                  colsf[:, 40 + tt:41 + tt], colsf[:, 56:57]),
                          reads=[rcols[56]], writes=[rcols[40 + tt]], small=True)
                P.add("vector", lambda e, gb=gb: e.tensor_reduce(colsf[:, 57:58], af(gb, 512),
                                                                 AX.X, ALU.max, apply_absolute_value=True),
                      reads=rAs(gb, 512), writes=[rcols[57]], small=True)
                if og == 0:
                    P.add("vector", lambda e, tt=tt: e.tensor_copy(colsf[:, 48 + tt:49 + tt], colsf[:, 57:58]),
                          reads=[rcols[57]], writes=[rcols[48 + tt]], small=True)
                else:
                    P.add("vector", lambda e, tt=tt: e.tensor_max(colsf[:, 48 + tt:49 + tt],
                                                                  colsf[:, 48 + tt:49 + tt], colsf[:, 57:58]),
                          reads=[rcols[57]], writes=[rcols[48 + tt]], small=True)
                P.add("gpsimd", lambda e, og=og, tt=tt, gb=gb: e.dma_start(
                    out=ghS[128 * tt:128 * (tt + 1), 512 * og:512 * (og + 1)], in_=af(gb, 512)),
                    reads=rAs(gb, 512), writes=[rscr["ghS"]], dma=True, g=G("ghs"))
        for tt in range(8):
            gs = colsf[:, 40 + tt:41 + tt]
            gm = colsf[:, 48 + tt:49 + tt]
            rgs, rgm = rcols[40 + tt], rcols[48 + tt]
            P.add("scalar", lambda e, gs=gs: e.activation(gs, gs, AF.Sqrt), reads=[rgs], writes=[rgs], small=True)
            P.add("vector", lambda e, gs=gs: e.tensor_scalar(gs, gs, 1.0, 1e-12, ALU.mult, ALU.max),
                  reads=[rgs], writes=[rgs], small=True)
            P.add("vector", lambda e, gs=gs: e.reciprocal(gs, gs), reads=[rgs], writes=[rgs], small=True)
            P.add("vector", lambda e, gs=gs: e.tensor_scalar(colsf[:, 59:60], gs, SQB_LO, None, ALU.mult),
                  reads=[rgs], writes=[rcols[59]], small=True)
            P.add("vector", lambda e, gs=gs: e.tensor_scalar(gs, gs, SQB_HI, None, ALU.mult),
                  reads=[rgs], writes=[rgs], small=True)
            P.add("vector", lambda e, gs=gs: e.tensor_add(gs, gs, colsf[:, 59:60]),
                  reads=[rgs, rcols[59]], writes=[rgs], small=True)
            P.add("vector", lambda e, gs=gs, gm=gm: e.tensor_mul(gm, gm, gs), reads=[rgs], writes=[rgm], small=True)
            P.add("vector", lambda e, gm=gm: e.tensor_scalar(gm, gm, 1.0, 1e-5, ALU.mult, ALU.max),
                  reads=[rgm], writes=[rgm], small=True)
            P.add("vector", lambda e, gm=gm, tt=tt: e.tensor_scalar(colsf[:, 64 + tt:65 + tt], gm,
                                                                    I127_HI, None, ALU.mult),
                  reads=[rgm], writes=[rcols[64 + tt]], small=True)
            P.add("vector", lambda e, gm=gm: e.tensor_scalar(colsf[:, 59:60], gm, I127_LO, None, ALU.mult),
                  reads=[rgm], writes=[rcols[59]], small=True)
            P.add("vector", lambda e, tt=tt: e.tensor_add(colsf[:, 64 + tt:65 + tt], colsf[:, 64 + tt:65 + tt],
                                                          colsf[:, 59:60]),
                  reads=[rcols[59]], writes=[rcols[64 + tt]], small=True)
            P.add("vector", lambda e, tt=tt: e.tensor_mul(colsf[:, 64 + tt:65 + tt], colsf[:, 64 + tt:65 + tt],
                                                          swc[:, 2:3]),
                  reads=[rswc], writes=[rcols[64 + tt]], small=True)
            P.add("vector", lambda e, gm=gm: e.reciprocal(gm, gm), reads=[rgm], writes=[rgm], small=True)
            P.add("vector", lambda e, gm=gm: e.tensor_scalar(gm, gm, 127.0, None, ALU.mult), reads=[rgm], writes=[rgm], small=True)
            P.add("vector", lambda e, gs=gs, gm=gm: e.tensor_mul(gs, gs, gm), reads=[rgm], writes=[rgs], small=True)
        for og in range(16):
            for tt in range(8):
                gb = 512 * ((og * 8 + tt) % 8)
                gbo = 512 * ((og * 8 + tt) % 2)
                P.add("sync", lambda e, og=og, tt=tt, gb=gb: e.dma_start(
                    out=af(gb, 512), in_=ghS[128 * tt:128 * (tt + 1), 512 * og:512 * (og + 1)]),
                    reads=[rscr["ghS"]], writes=rAs(gb, 512), dma=True, g=G("ghb%d" % (gb // 512)))
                P.add("vector", lambda e, tt=tt, gb=gb: e.tensor_scalar(af(8192, 512), af(gb, 512),
                                                                        colsf[:, 40 + tt:41 + tt], None, ALU.mult),
                      reads=rAs(gb, 512) + [rcols[40 + tt]], writes=rAs(8192, 512))
                P.add("vector", lambda e: e.tensor_copy(Ii[:, 0:512], af(8192, 512)), reads=rAs(8192, 512), writes=[rI])
                P.add("vector", lambda e: e.tensor_copy(af(8704, 512), Ii[:, 0:512]), reads=[rI], writes=rAs(8704, 512))
                P.add("vector", lambda e: e.tensor_sub(af(9216, 512), af(8704, 512), af(8192, 512)),
                      reads=rAs(8704, 512) + rAs(8192, 512), writes=rAs(9216, 512))
                P.add("vector", lambda e: e.scalar_tensor_tensor(af(8704, 512), af(9216, 512), -0.5, af(8704, 512),
                                                                 ALU.is_lt, ALU.add),
                      reads=rAs(9216, 512) + rAs(8704, 512), writes=rAs(8704, 512))
                P.add("vector", lambda e: e.tensor_scalar(af(9216, 512), af(9216, 512), 0.5, None, ALU.is_gt),
                      reads=rAs(9216, 512), writes=rAs(9216, 512))
                P.add("vector", lambda e, gbo=gbo: e.tensor_sub(Bb[:, GH0 + gbo:GH0 + gbo + 512], af(8704, 512), af(9216, 512)),
                      reads=rAs(8704, 512) + rAs(9216, 512), writes=rBs(GH0 + gbo, 512))
                P.add("gpsimd", lambda e, og=og, tt=tt, gbo=gbo: e.dma_start(
                    out=QA2S[128 * tt:128 * (tt + 1), 512 * og:512 * (og + 1)], in_=Bb[:, GH0 + gbo:GH0 + gbo + 512]),
                    reads=rBs(GH0 + gbo, 512), writes=[rscr["QA2S"]], dma=True, g=G("qa2s"))
        for et in range(ET):
            P.add("sync", lambda e, ck=ck, et=et: e.dma_start(
                out=Bb[:, OFM0 + 1024 * et:OFM0 + 1024 * et + 1024],
                in_=OS[128 * et:128 * (et + 1), CK * ck:CK * (ck + 1)]),
                reads=[rscr["OS"]], writes=rBs(OFM0 + 1024 * et, 1024), dma=True, g=G("ofm"))
        rOFM = rBs(OFM0, 16384)
        for fc in range(4):
            for ot in range(OTN):
                osl = ot % 4
                q2o = W1T0 + 1024 * osl
                w2o = W1T0 + 8192 + 512 * osl
                rq2 = rBs(q2o, 1024)
                rw2 = rBs(w2o, 512)
                P.add("sync", lambda e, ot=ot, q2o=q2o: e.dma_start(
                    out=Bb[:, q2o:q2o + 1024],
                    in_=QA2S[:, 128 * ot:128 * (ot + 1)], transpose=True),
                    reads=[rscr["QA2S"]], writes=rq2, dma=True, g=G("q2f%d" % osl))
                P.add("sync", lambda e, fc=fc, ot=ot, w2o=w2o: e.dma_start(
                    out=Bb[:, w2o:w2o + 512],
                    in_=W2qS[512 * fc:512 * (fc + 1), 128 * ot:128 * (ot + 1)], transpose=True),
                    reads=[rscr["W2qS"]], writes=rw2, dma=True, g=G("w2b%d" % osl))
                def mm2(e, ot=ot, q2o=q2o, w2o=w2o):
                    r = None
                    for tt in range(8):
                        r = e.matmul(PS[tt][:, 0:512],
                                     Bb[:, q2o + 128 * tt:q2o + 128 * tt + 128],
                                     Bb[:, w2o:w2o + 512],
                                     start=(ot == 0), stop=(ot == OTN - 1))
                    return r
                P.add("tensor", mm2, reads=rq2 + rw2, writes=rPS)
            for tt in range(8):
                P.add("vector", lambda e, tt=tt: e.tensor_scalar(af(8192 + 512 * tt // 1, 512) if False else
                                                                 A[:, 8192 + 512 * (tt % 8):8192 + 512 * (tt % 8) + 512],
                                                                 PS[tt][:, 0:512], colsf[:, 64 + tt:65 + tt], None, ALU.mult),
                      reads=[rPS[tt], rcols[64 + tt]], writes=rAs(8192 + 512 * tt, 512))
            for et in range(ET):
                P.add("sync", lambda e, fc=fc, et=et: e.dma_start(
                    out=Bb[:, WOT0 + 512 * et:WOT0 + 512 * et + 512],
                    in_=WbS["Wo"][512 * fc:512 * (fc + 1), 128 * et:128 * (et + 1)], transpose=True),
                    reads=[rscr["WobS"]], writes=rBs(WOT0 + 512 * et, 512), dma=True, g=G("wot"))
            rWOT = rBs(WOT0, 8192)
            for tt in range(8):
                pa = rPS[tt]
                def mmA(e, tt=tt, fc=fc):
                    for et in range(ET):
                        e.matmul(PS[tt][:, 0:512],
                                 Bb[:, OFM0 + 1024 * et + 128 * tt:OFM0 + 1024 * et + 128 * tt + 128],
                                 Bb[:, WOT0 + 512 * et:WOT0 + 512 * et + 512], start=(et == 0), stop=False)
                    return e.matmul(PS[tt][:, 0:512], or128b[:, :], borow[0:1, 512 * fc:512 * (fc + 1)],
                                    start=False, stop=True)
                P.add("tensor", mmA, reads=rOFM + rWOT + [rconst, rbo] + rAs(8192 + 512 * tt, 512), writes=[pa])
                xb = 12288 + 512 * (tt % 2)
                P.add("sync", lambda e, ck=ck, tt=tt, fc=fc, xb=xb: e.dma_start(
                    out=af(xb, 512), in_=xo[CK * ck + 128 * tt:CK * ck + 128 * tt + 128, 512 * fc:512 * (fc + 1)]),
                    writes=rAs(xb, 512), dma=True, g=G("xs%d" % (tt % 2)))
                P.add("vector", lambda e, tt=tt, xb=xb: e.tensor_add(af(xb, 512), af(xb, 512),
                                                                     A[:, 8192 + 512 * tt:8192 + 512 * tt + 512]),
                      reads=rAs(8192 + 512 * tt, 512) + rAs(xb, 512), writes=rAs(xb, 512))
                P.add("vector", lambda e, tt=tt, xb=xb: e.tensor_add(af(xb, 512), af(xb, 512), PS[tt][:, 0:512]),
                      reads=[pa] + rAs(xb, 512), writes=rAs(xb, 512))
                o = P.add("gpsimd", lambda e, ck=ck, tt=tt, fc=fc, xb=xb: e.dma_start(
                    out=out[CK * ck + 128 * tt:CK * ck + 128 * tt + 128, 512 * fc:512 * (fc + 1)], in_=af(xb, 512)),
                    reads=rAs(xb, 512), dma=True, g=G("gout"))
                OUT_OPS.append(o)

    P.add("sync", lambda e: e.nop(), deps=OUT_OPS)

    if DEBUG_DUMP:
        dbg_list = [("xnS", xnS, [S, E]), ("KS", KS, [E, S]), ("VS", VS, [S, E]),
                    ("QS", QS, [E, OWN]), ("OS", OS, [E, OWN]),
                    ("W1qS", W1qS, [INNER, E]), ("ghS", ghS, [CK, INNER]), ("QA2S", QA2S, [CK, INNER]),
                    ("QA1S", QA1S, [CK, E])]
        for nm, ap, shp in dbg_list:
            dout = nc.dram_tensor("dbg_" + nm, shp, BF16, kind="ExternalOutput").ap()
            P.add("gpsimd", lambda e, ap=ap, dout=dout: e.dma_start(out=dout, in_=ap),
                  reads=[rscr[nm] if nm in rscr else rxn[7]], deps=OUT_OPS, dma=True, g=G("gdbg"))
        dsw = nc.dram_tensor("dbg_swc", [128, 4], F32, kind="ExternalOutput").ap()
        P.add("gpsimd", lambda e: e.dma_start(out=dsw, in_=swc[:, :]), reads=[rswc], deps=OUT_OPS, dma=True, g=G("gdbg"))
        fin = [op for op in P.ops["gpsimd"] if op.grp is not None and op.grp.name == "gdbg"]
        P.add("sync", lambda e: e.nop(), deps=fin)

    P.finalize()
    sems = {e: nc.alloc_semaphore("sem_" + e) for e in ROLES}
    for gg in GR.values():
        gg.sem = nc.alloc_semaphore("g_" + gg.name)
    with nc.Block() as block:
        @block.sync
        def _(eng):
            P.emit("sync", eng, sems)

        @block.tensor
        def _(eng):
            P.emit("tensor", eng, sems)

        @block.vector
        def _(eng):
            P.emit("vector", eng, sems)

        @block.scalar
        def _(eng):
            P.emit("scalar", eng, sems)

        @block.gpsimd
        def _(eng):
            P.emit("gpsimd", eng, sems)
    return nc


_NC = None


def _get_nc():
    global _NC
    if _NC is None:
        _NC = build()
    return _NC


def _pack_inputs(inputs):
    x = np.asarray(inputs["L0_attn_x"], dtype=np.float32)
    masks = {}
    posos = {}
    cosos = {}
    sinos = {}
    for g in range(2):
        m = np.zeros((KTN, 128, QG), dtype=np.float32)
        for qg in range(NQG):
            nkt = 4 * (qg + 1)
            for slot in range(4):
                kt = nkt - 4 + slot
                for lt in range(2):
                    gt = 4 * qg + 2 * lt + g
                    qglob = 128 * gt + np.arange(128)
                    kglob = 128 * kt + np.arange(128)
                    m[4 * qg + slot, :, 128 * lt:128 * (lt + 1)] = (kglob[:, None] <= qglob[None, :])
        masks[g] = np.ascontiguousarray(m.reshape(KTN * 128, QG).astype(BF))
        own_tiles = [2 * i + g for i in range(NT)]
        pos = np.concatenate([128 * t + np.arange(128) for t in own_tiles]).astype(np.float32)
        posos[g] = np.ascontiguousarray(np.tile(pos, (128, 1)))
        cos32, sinF32 = _rope_tables_f32()
        idx = pos.astype(np.int64)
        cosos[g] = np.ascontiguousarray(cos32[idx].T)
        sinos[g] = np.ascontiguousarray(sinF32[idx].T)
    base = {
        "Wq": np.ascontiguousarray(np.asarray(inputs["L0_attn_Wq"], dtype=np.float32)),
        "Wk": np.ascontiguousarray(np.asarray(inputs["L0_attn_Wk"], dtype=np.float32)),
        "Wv": np.ascontiguousarray(np.asarray(inputs["L0_attn_Wv"], dtype=np.float32)),
        "Wo": np.ascontiguousarray(np.asarray(inputs["L0_attn_Wo"], dtype=np.float32)),
        "bq": np.ascontiguousarray(np.asarray(inputs["L0_attn_bq"], dtype=np.float32)),
        "bk": np.ascontiguousarray(np.asarray(inputs["L0_attn_bk"], dtype=np.float32)),
        "bv": np.ascontiguousarray(np.asarray(inputs["L0_attn_bv"], dtype=np.float32)),
        "bo": np.ascontiguousarray(np.asarray(inputs["L0_attn_bo"], dtype=np.float32)),
        "W1": np.ascontiguousarray(np.asarray(inputs["L0_ffn_w1"], dtype=np.float32)),
        "W2": np.ascontiguousarray(np.asarray(inputs["L0_ffn_w2"], dtype=np.float32)),
    }
    in_maps = []
    for c in range(8):
        b, g = c // 2, c % 2
        own_tiles = [2 * i + g for i in range(NT)]
        xb = np.ascontiguousarray(x[b])
        xown = np.ascontiguousarray(xb.reshape(32, 128, E)[own_tiles].reshape(OWN, E))
        d = dict(base)
        d.update({"xk": xb, "xo": xown, "maskin": masks[g], "poso": posos[g],
                  "cosoI": cosos[g], "sinoI": sinos[g]})
        in_maps.append(d)
    return in_maps


def kernel(**inputs):
    nc = _get_nc()
    in_maps = _pack_inputs(inputs)
    res = run_bass_kernel_spmd(nc, in_maps, core_ids=list(range(8)))
    outp = np.zeros((B_, S, E), dtype=np.float32)
    for c in range(8):
        b, g = c // 2, c % 2
        own_tiles = [2 * i + g for i in range(NT)]
        o = np.asarray(res.results[c]["out"], dtype=np.float32).reshape(NT, 128, E)
        outp.reshape(B_, 32, 128, E)[b, own_tiles] = o
    return outp
